# version 21
# baseline (speedup 1.0000x reference)
import numpy as np
from contextlib import ExitStack

import concourse.bass as bass
import concourse.mybir as mybir
from concourse.bass_utils import run_bass_kernel_spmd

F32 = mybir.dt.float32
BF16 = mybir.dt.bfloat16
AF = mybir.ActivationFunctionType
ALU = mybir.AluOpType
AX = mybir.AxisListType

D = 1024
T = 4096
NCH = 8
TC = 512
NTC = T // TC
EPS_LN = 1e-5
NEG = -1.0e9
HID = 2730
HPAD = 2816



def _f(x):
    return np.ascontiguousarray(np.asarray(x), dtype=np.float32)


def _bf(x):
    import ml_dtypes
    return np.ascontiguousarray(
        np.asarray(x, dtype=np.float32).astype(ml_dtypes.bfloat16))


def _host_shared(inputs):
    d = {}
    d["aWq"] = _bf(inputs["L0_attn_Wq"]); d["aWk"] = _bf(inputs["L0_attn_Wk"])
    d["aWv"] = _bf(inputs["L0_attn_Wv"]); d["aWo"] = _bf(inputs["L0_attn_Wo"])
    d["abq"] = _f(inputs["L0_attn_bq"]); d["abo"] = _f(inputs["L0_attn_bo"])
    w1 = _f(inputs["L0_ffn_w1"]); b1 = _f(inputs["L0_ffn_b1"])
    w1p = np.zeros((2, D, 2 * HPAD), np.float32)
    b1p = np.zeros((2, 2 * HPAD), np.float32)
    w1p[:, :, :HID] = w1[:, :, :HID]; w1p[:, :, HPAD:HPAD + HID] = w1[:, :, HID:]
    b1p[:, :HID] = b1[:, :HID]; b1p[:, HPAD:HPAD + HID] = b1[:, HID:]
    w2p = np.zeros((2, HPAD, D), np.float32)
    w2p[:, :HID, :] = _f(inputs["L0_ffn_w2"])
    d["fwqkv"] = _bf(inputs["L0_ffn_wqkv"])
    d["fw1"] = _bf(w1p); d["fb1"] = b1p; d["fw2"] = _bf(w2p)
    d["fwo"] = _bf(inputs["L0_ffn_wo"]); d["fb2"] = _f(inputs["L0_ffn_b2"])
    d["flng"] = _f(inputs["L0_ffn_ln_g"]); d["flnb"] = _f(inputs["L0_ffn_ln_b"])
    d["fqs"] = _f(inputs["L0_ffn_q_scale"]); d["fks"] = _f(inputs["L0_ffn_k_scale"])
    d["cW1T"] = _bf(np.asarray(inputs["L1_attn_W1"]).T)
    d["cb1"] = _f(inputs["L1_attn_b1_"])
    d["cW2c"] = _f(np.asarray(inputs["L1_attn_W2"]).reshape(1, D).T)
    d["cb2"] = _f(np.asarray(inputs["L1_attn_b2_"]).reshape(1, 1))
    d["rWrT"] = _bf(np.asarray(inputs["L1_ffn_W_r"]).T)
    d["rWkT"] = _bf(np.asarray(inputs["L1_ffn_W_k"]).T)
    d["rWvT"] = _bf(np.asarray(inputs["L1_ffn_W_v"]).T)
    d["rmur"] = _f(inputs["L1_ffn_mu_r"]); d["rmuk"] = _f(inputs["L1_ffn_mu_k"])
    d.update(_host_consts())
    return d


def _alibi_plan():
    slopes = (2.0 ** (-8.0 / 16.0)) ** np.arange(1, 17)
    heads = []
    for h in range(16):
        sl = float(slopes[h])
        skip = 104.0 / sl
        jlos = []
        dists = set()
        for ig in range(T // 512):
            igbase = 512 * ig
            jlo = max(0, int(np.ceil((igbase - skip - 127) / 128.0)))
            jlos.append(jlo)
            for jt in range(jlo, (igbase + 511) // 128 + 1):
                dists.add(igbase - 128 * jt)
        dists = sorted(dists)
        heads.append({"slope": sl, "dists": dists, "jlo": jlos,
                      "dmap": {dd: k for k, dd in enumerate(dists)}})
    return heads


ALIBI = _alibi_plan()
ALIBI_ND = max(len(hh["dists"]) for hh in ALIBI)


def _host_consts():
    d = {}
    jj, ii = np.meshgrid(np.arange(128), np.arange(128), indexing="ij")
    mc = np.zeros((128, 640), np.float32)
    mc[:, :128] = np.where(jj > ii, NEG, 0.0)
    d["mask_causal"] = mc
    d["mask_window"] = np.where(jj < ii, NEG, 0.0).astype(np.float32)
    hb = np.zeros((16, 128, ALIBI_ND), np.float32)
    for h in range(16):
        hh = ALIBI[h]
        for k, dd in enumerate(hh["dists"]):
            hb[h, :, k] = hh["slope"] * (np.arange(128) - dd)
    d["alibi_hb"] = hb.reshape(16 * 128, ALIBI_ND)
    import ml_dtypes as _mld

    def _split3(v):
        out = np.zeros((3, T), np.float32)
        r = v.astype(np.float64)
        for t in range(3):
            hi = r.astype(np.float32).astype(_mld.bfloat16).astype(np.float64)
            out[t] = hi.astype(np.float32)
            r = r - hi
        return out

    albq = np.zeros((16, 3, T), np.float32)
    alck = np.zeros((16, 3, T), np.float32)
    for h in range(16):
        albq[h] = _split3(-ALIBI[h]["slope"] * np.arange(T, dtype=np.float64))
        alck[h] = _split3(+ALIBI[h]["slope"] * np.arange(T, dtype=np.float64))
    d["alibi_bq"] = albq.reshape(48, T)
    d["alibi_ck"] = alck.reshape(48, T)
    DIM_HEAD, WINDOW = 64, 512
    half_idx = np.arange(0, DIM_HEAD, 2, dtype=np.float64)
    inv_freq = 1.0 / 10000.0 ** (half_idx / DIM_HEAD)
    t = np.arange(T, dtype=np.float64)
    freqs = np.concatenate([t[:, None] * inv_freq[None, :]] * 2, axis=-1)
    base = (half_idx + 0.4 * DIM_HEAD) / (1.4 * DIM_HEAD)
    power = (t - T // 2) / float(WINDOW // 2)
    scale = np.concatenate([base[None, :] ** power[:, None]] * 2, axis=-1)
    cos, sin = np.cos(freqs), np.sin(freqs)
    sgn = np.where(np.arange(DIM_HEAD) < DIM_HEAD // 2, -1.0, 1.0)
    for nm, v in (("csq", (cos * scale).T), ("ssq", (sin * scale).T * sgn[:, None]),
                  ("csk", (cos / scale).T), ("ssk", (sin / scale).T * sgn[:, None])):
        d["rot_" + nm] = np.ascontiguousarray(
            np.concatenate([v, v], axis=0), dtype=np.float32)
    return d



GP_DMA_SEMS = ("wldg", "faldg")


class Ctx:
    def __init__(self, nc, es, fuse=False):
        self.nc = nc
        self.es = es
        self.es_root = es
        self.fuse = fuse
        self._sems = {}

    def _canon(self, name):
        if not self.fuse:
            return name
        if name in GP_DMA_SEMS:
            return "g_dmg"
        if name.endswith(("ld", "st", "cl")) or name == "c0":
            return "g_dma"
        return name

    def sb(self, name, shape, dtype=F32):
        return self.es.enter_context(self.nc.sbuf_tensor(name, list(shape), dtype))

    def psum(self, name, shape, dtype=F32):
        return self.es.enter_context(self.nc.psum_tensor(name, list(shape), dtype))

    def scope(self):
        outer = self

        class _S:
            def __enter__(self_s):
                self_s.es2 = ExitStack()
                self_s.saved = outer.es
                outer.es = self_s.es2
                return outer

            def __exit__(self_s, *a):
                outer.es = self_s.saved
                return self_s.es2.__exit__(*a)

        return _S()

    def sem(self, name):
        name = self._canon(name)
        if name not in self._sems:
            self._sems[name] = [self.es_root.enter_context(self.nc.semaphore(name)), 0]
        return self._sems[name]

    def inc(self, name, instr, k=1):
        s = self.sem(name)
        instr.then_inc(s[0], k)
        s[1] += k

    def val(self, name):
        return self.sem(name)[1]

    def wait(self, eng, name, v=None):
        s = self.sem(name)
        v = s[1] if v is None else v
        if v > 0:
            eng.wait_ge(s[0], v)


def emit_consts(cx):
    cx.ones_col = cx.sb("cm_ones_col", [128, 1], F32)
    cx.ones_col_bf = cx.sb("cm_ones_col_bf", [128, 1], BF16)
    cx.ones_row = cx.sb("cm_ones_row", [65, 128], F32)
    cx.eps_t = cx.sb("cm_eps", [1, 1], F32)
    cx.tiny_t = cx.sb("cm_tiny", [128, 1], F32)
    cx.bsel = cx.sb("cm_bsel", [65, 128], F32)
    with cx.nc.Block() as blk:
        @blk.gpsimd
        def _(g):
            g.memset(cx.ones_col[:, :], 1.0)
            g.memset(cx.ones_col_bf[:, :], 1.0)
            g.memset(cx.ones_row[:, :], 1.0)
            g.memset(cx.eps_t[:, :], EPS_LN)
            g.memset(cx.tiny_t[:, :], 1e-38)
            g.memset(cx.bsel[:, :], 0.0)
            g.memset(cx.bsel[0:1, 0:64], 1.0)
            g.memset(cx.bsel[64:65, 64:128], 1.0)


def load_weight_bf(cx, tag, hbm, d_in, d_out, defer=False):
    nch = d_in // 128
    w = cx.sb(tag, [128, nch * d_out], BF16)
    with cx.nc.Block() as blk:
        @blk.gpsimd
        def _(g):
            for c in range(nch):
                cx.inc("wldg", g.dma_start(w[:, c * d_out:(c + 1) * d_out],
                                           hbm.ap()[c * 128:(c + 1) * 128, :]), 16)
            if not defer:
                cx.wait(g, "wldg")
    if defer:
        return w, cx.val("wldg")
    return w


def dma_drain(cx, sem, val):
    with cx.nc.Block() as blk:
        @blk.sync
        def _(s):
            s.wait_ge(cx.sem(sem)[0], val)


def load_pp(cx, tag, hbm_vec, n=D, row=None):
    nch = n // 128
    tile = cx.sb(tag, [128, nch], F32)
    src = hbm_vec.ap() if row is None else hbm_vec.ap()[row]
    with cx.nc.Block() as blk:
        @blk.sync
        def _(s):
            for c in range(nch):
                cx.inc("ppld", s.dma_start(
                    tile[:, c:c + 1],
                    src[c * 128:(c + 1) * 128].rearrange("(p o) -> p o", o=1)), 16)
            cx.wait(s, "ppld")
    return tile


def ln_phase(cx, u, src, out=None, out_bf=None, res=None, shift=False,
             gamma=None, beta=None, final_wait=True, names=None, out_sb=None,
             src_sb_full=None):
    with cx.scope():
        _ln_phase_body(cx, u, src, out, out_bf, res, shift, gamma, beta,
                       names or u, out_sb, src_sb_full)


def _ln_phase_body(cx, u, src, out, out_bf, res, shift, gamma, beta, nm,
                   out_sb=None, src_sb_full=None):
    nc = cx.nc
    inplace = src_sb_full is not None
    direct = (out_sb is not None and out is None and out_bf is None
              and res is None and gamma is None)
    direct_bf = (out_bf is not None and out is None and res is None
                 and not inplace)
    W = TC + 1 if shift else TC
    xin = None if inplace else [
        [cx.sb(f"{nm}_x{p}{c}", [128, W]) for c in range(NCH)] for p in range(2)]
    rin = [[cx.sb(f"{nm}_r{p}{c}", [128, TC]) for c in range(NCH)]
           for p in range(2)] if res is not None else None
    sq = [cx.sb(f"{nm}_sq{p}", [128, NCH * TC], BF16) for p in range(2)]
    xb = None if inplace else [
        [cx.sb(f"{nm}_xb{p}{c}", [128, TC], BF16) for c in range(NCH)]
        for p in range(2)]
    rows = [cx.sb(f"{nm}_rows{p}", [1, 5 * TC]) for p in range(2)]
    t1 = [cx.sb(f"{nm}_t1", [128, TC])] * 2
    yv = None if (inplace or direct or direct_bf) else [
        [cx.sb(f"{nm}_y{p}{c}", [128, TC]) for c in range(NCH)] for p in range(2)]
    yb = [[cx.sb(f"{nm}_yb{p}{c}", [128, TC], BF16) for c in range(NCH)]
          for p in range(2)] if out_bf is not None else None
    ps_s1 = [cx.psum(f"{nm}_ps1{p}", [128, TC]) for p in range(2)]
    ps_s2 = [cx.psum(f"{nm}_ps2{p}", [128, TC]) for p in range(2)]
    ps_a = [cx.psum(f"{nm}_psa{p}", [128, TC]) for p in range(2)]
    ps_b = [cx.psum(f"{nm}_psb{p}", [128, TC]) for p in range(2)]

    nl = NCH * (2 if res is not None else 1)
    ns = (1 if out is not None else 0) + (1 if out_bf is not None else 0)
    marks_ld = {}
    marks_st = {}

    def view(tc, c):
        if inplace:
            return src_sb_full[c][:, tc * TC:(tc + 1) * TC]
        p = tc % 2
        if not shift:
            return xin[p][c][:, :]
        return xin[p][c][:, 1:TC + 1] if c < 4 else xin[p][c][:, 0:TC]

    def ytile(tc, c):
        if inplace:
            return src_sb_full[c][:, tc * TC:(tc + 1) * TC]
        if direct:
            return out_sb[c][:, tc * TC:(tc + 1) * TC]
        return yv[tc % 2][c][:, :]

    if shift:
        with nc.Block() as blk:
            @blk.vector
            def _(v):
                for p in range(2):
                    for c in range(NCH):
                        v.memset(xin[p][c][:, 0:1], 0.0)

    sq0 = cx.val(u + "sqd"); mm0 = cx.val(u + "mmd"); va0 = cx.val(u + "vad")
    rs0 = cx.val(u + "rsd"); rw0 = cx.val(u + "rwd"); bc0 = cx.val(u + "bcd")
    ap0 = cx.val(u + "apd"); cb0 = cx.val(u + "cbd"); yb0 = cx.val(u + "ybd")

    with nc.Block() as blk:
        @blk.sync
        def _(s):
            apm = cx.sem(u + "apd")
            cbm = cx.sem(u + "cbd")
            ybm = cx.sem(u + "ybd")
            for tc in range(NTC):
                p = tc % 2
                t0 = tc * TC
                if tc >= 2:
                    if not inplace or res is not None:
                        s.wait_ge(apm[0], ap0 + NCH * (tc - 1))
                    if not inplace:
                        s.wait_ge(cbm[0], cb0 + tc - 1)
                for c in ([] if inplace else range(NCH)):
                    if shift and t0 > 0:
                        sl = src.ap()[c * 128:(c + 1) * 128, t0 - 1:t0 + TC]
                        dst = xin[p][c][:, 0:TC + 1]
                    elif shift:
                        sl = src.ap()[c * 128:(c + 1) * 128, 0:TC]
                        dst = xin[p][c][:, 1:TC + 1]
                    else:
                        sl = src.ap()[c * 128:(c + 1) * 128, t0:t0 + TC]
                        dst = xin[p][c][:, :]
                    cx.inc(u + "ld", s.dma_start(dst, sl), 16)
                if res is not None:
                    for c in range(NCH):
                        cx.inc(u + "ld", s.dma_start(
                            rin[p][c][:, :], res.ap()[c * 128:(c + 1) * 128, t0:t0 + TC]), 16)
                marks_ld[tc] = cx.val(u + "ld")
                if tc >= 1 and ns:
                    tp = tc - 1
                    for c in range(NCH):
                        if out is not None:
                            s.wait_ge(apm[0], ap0 + NCH * tp + c + 1)
                            cx.inc(u + "st", s.dma_start(
                                out.ap()[c * 128:(c + 1) * 128, tp * TC:(tp + 1) * TC],
                                ytile(tp, c)), 16)
                        if out_bf is not None:
                            if direct_bf:
                                s.wait_ge(apm[0], ap0 + NCH * tp + c + 1)
                            else:
                                s.wait_ge(ybm[0], yb0 + NCH * tp + c + 1)
                            cx.inc(u + "st", s.dma_start(
                                out_bf.ap()[c * 128:(c + 1) * 128, tp * TC:(tp + 1) * TC],
                                yb[tp % 2][c][:, :]), 16)
                marks_st[tc] = cx.val(u + "st")
            tp = NTC - 1
            if ns:
                for c in range(NCH):
                    if out is not None:
                        s.wait_ge(apm[0], ap0 + NCH * tp + c + 1)
                        cx.inc(u + "st", s.dma_start(
                            out.ap()[c * 128:(c + 1) * 128, tp * TC:(tp + 1) * TC],
                            ytile(tp, c)), 16)
                    if out_bf is not None:
                        if direct_bf:
                            s.wait_ge(apm[0], ap0 + NCH * tp + c + 1)
                        else:
                            s.wait_ge(ybm[0], yb0 + NCH * tp + c + 1)
                        cx.inc(u + "st", s.dma_start(
                            out_bf.ap()[c * 128:(c + 1) * 128, tp * TC:(tp + 1) * TC],
                            yb[tp % 2][c][:, :]), 16)
            cx.wait(s, u + "st")

        @blk.vector
        def _(v):
            ldm = cx.sem(u + "ld")
            for tc in range(NTC):
                p = tc % 2
                v.wait_ge(ldm[0], marks_ld[tc])
                if tc >= 2:
                    cx.wait(v, u + "mmd", mm0 + tc - 1)
                for c in range(NCH):
                    ins = v.tensor_mul(sq[p][:, c * TC:(c + 1) * TC], view(tc, c),
                                       view(tc, c))
                cx.inc(u + "sqd", ins)
                s1 = rows[p][:, 0:TC]; s2 = rows[p][:, TC:2 * TC]
                mu = rows[p][:, 2 * TC:3 * TC]; va = rows[p][:, 3 * TC:4 * TC]
                cx.wait(v, u + "mmd", mm0 + tc + 1)
                v.tensor_copy(s1[:, :], ps_s1[p][0:1, :])
                v.tensor_copy(s2[:, :], ps_s2[p][0:1, :])
                v.tensor_scalar_mul(mu[:, :], s1[:, :], 1.0 / D)
                v.tensor_mul(t1[p][0:1, 0:TC], mu[:, :], mu[:, :])
                ins = v.scalar_tensor_tensor(va[:, :], s2[:, :], 1.0 / D,
                                             t1[p][0:1, 0:TC], ALU.mult, ALU.subtract)
                cx.inc(u + "vad", ins)
                ar = rows[p][:, 4 * TC:5 * TC]; br = rows[p][:, TC:2 * TC]
                cx.wait(v, u + "rsd", rs0 + tc + 1)
                ins = v.scalar_tensor_tensor(br[:, :], mu[:, :], -1.0, ar[:, :],
                                             ALU.mult, ALU.mult)
                cx.inc(u + "rwd", ins)
                cx.wait(v, u + "bcd", bc0 + tc + 1)
                stm = cx.sem(u + "st")
                if tc >= 2 and not inplace and not direct and not direct_bf \
                        and (out_bf is not None or out_sb is not None):
                    cx.wait(v, u + "ybd", yb0 + NCH * (tc - 1))
                for c in range(NCH):
                    if tc >= 2 and not inplace:
                        v.wait_ge(stm[0], marks_st[tc - 1])
                    if direct_bf:
                        yt = yb[p][c][:, :]
                        v.tensor_mul(t1[p][:, :], view(tc, c), ps_a[p][:, :])
                        if gamma is not None:
                            v.tensor_add(t1[p][:, :], t1[p][:, :], ps_b[p][:, :])
                            ins = v.tensor_scalar(yt, t1[p][:, :],
                                                  gamma[:, c:c + 1],
                                                  beta[:, c:c + 1],
                                                  ALU.mult, ALU.add)
                        else:
                            ins = v.tensor_add(yt, t1[p][:, :], ps_b[p][:, :])
                        cx.inc(u + "apd", ins)
                        continue
                    yt = ytile(tc, c)
                    v.tensor_mul(t1[p][:, :], view(tc, c), ps_a[p][:, :])
                    ins = v.tensor_add(yt, t1[p][:, :], ps_b[p][:, :])
                    if gamma is not None:
                        ins = v.tensor_scalar(yt, yt,
                                              gamma[:, c:c + 1], beta[:, c:c + 1],
                                              ALU.mult, ALU.add)
                    if res is not None:
                        ins = v.tensor_add(yt, yt, rin[p][c][:, :])
                    cx.inc(u + "apd", ins)

        @blk.gpsimd
        def _(g):
            ldm = cx.sem(u + "ld")
            mmm = cx.sem(u + "mmd")
            for tc in range(0 if inplace else NTC):
                p = tc % 2
                g.wait_ge(ldm[0], marks_ld[tc])
                if tc >= 2:
                    g.wait_ge(mmm[0], mm0 + tc - 1)
                for c in range(NCH):
                    ins = g.tensor_copy(xb[p][c][:, :], view(tc, c))
                cx.inc(u + "cbd", ins)

        @blk.tensor
        def _(tn):
            for tc in range(NTC):
                p = tc % 2
                cx.wait(tn, u + "sqd", sq0 + tc + 1)
                if not inplace:
                    cx.wait(tn, u + "cbd", cb0 + tc + 1)
                if tc >= 2:
                    cx.wait(tn, u + "vad", va0 + tc - 1)
                for c in range(NCH):
                    if inplace:
                        tn.matmul(ps_s1[p][0:1, :], cx.ones_col[:, 0:1],
                                  view(tc, c),
                                  start=(c == 0), stop=(c == NCH - 1))
                    else:
                        tn.matmul(ps_s1[p][0:1, :], cx.ones_col_bf[:, 0:1],
                                  xb[p][c][:, :],
                                  start=(c == 0), stop=(c == NCH - 1))
                for c in range(NCH):
                    ins = tn.matmul(ps_s2[p][0:1, :], cx.ones_col_bf[:, 0:1],
                                    sq[p][:, c * TC:(c + 1) * TC],
                                    start=(c == 0), stop=(c == NCH - 1))
                cx.inc(u + "mmd", ins)
                cx.wait(tn, u + "rwd", rw0 + tc + 1)
                if tc >= 2:
                    cx.wait(tn, u + "apd", ap0 + NCH * (tc - 1))
                ar = rows[p][:, 4 * TC:5 * TC]; br = rows[p][:, TC:2 * TC]
                tn.matmul(ps_a[p][:, :], cx.ones_row[0:1, :], ar[:, :],
                          start=True, stop=True, skip_group_check=True)
                ins = tn.matmul(ps_b[p][:, :], cx.ones_row[0:1, :], br[:, :],
                                start=True, stop=True, skip_group_check=True)
                cx.inc(u + "bcd", ins)

        @blk.scalar
        def _(s):
            apm = cx.sem(u + "apd")
            stm = cx.sem(u + "st")
            for tc in range(NTC):
                p = tc % 2
                va = rows[p][:, 3 * TC:4 * TC]; ar = rows[p][:, 4 * TC:5 * TC]
                cx.wait(s, u + "vad", va0 + tc + 1)
                s.activation(ar[:, :], va[:, :], AF.Ln, bias=cx.eps_t[0:1, 0:1])
                ins = s.activation(ar[:, :], ar[:, :], AF.Exp, scale=-0.5)
                cx.inc(u + "rsd", ins)
                if (out_bf is not None or out_sb is not None) and not direct \
                        and not direct_bf:
                    if out_bf is not None and tc >= 2:
                        s.wait_ge(stm[0], marks_st[tc - 1])
                    for c in range(NCH):
                        s.wait_ge(apm[0], ap0 + NCH * tc + c + 1)
                        dst = (yb[p][c][:, :] if out_bf is not None else
                               out_sb[c][:, tc * TC:(tc + 1) * TC])
                        ins = s.activation(dst, ytile(tc, c), AF.Copy)
                        cx.inc(u + "ybd", ins)



def emit_l1ffn(cx, xin, wr, wk, wv, mur, muk, xout, lnx, ysc):
    nc = cx.nc
    with cx.scope():
        wpre = _l1ffn_loads(cx, wr, wk, wv, mur, muk)
        ln_phase(cx, "ln", xin, out_bf=lnx, names="s1")
        with cx.scope():
            _l1ffn_core(cx, wpre, lnx, ysc)
    ln_phase(cx, "ln", ysc, out=xout, res=xin, names="s3")


def _l1ffn_loads(cx, wr, wk, wv, mur, muk):
    w_r = load_weight_bf(cx, "w_r", wr, D, D, defer=True)[0]
    w_k = load_weight_bf(cx, "w_k", wk, D, D, defer=True)[0]
    w_v, wm = load_weight_bf(cx, "w_v", wv, D, D, defer=True)
    mur_t = load_pp(cx, "mur_t", mur)
    muk_t = load_pp(cx, "muk_t", muk)
    return w_r, w_k, w_v, wm, mur_t, muk_t


def _l1ffn_core(cx, wpre, lnx, ysc):
    nc = cx.nc
    w_r, w_k, w_v, wm, mur_t, muk_t = wpre
    dma_drain(cx, "wldg", wm)

    xt = [[cx.sb(f"s2_x{p}{c}", [128, TC + 1], BF16) for c in range(NCH)]
          for p in range(2)]
    xr = [cx.sb(f"s2_xr{c}", [128, TC], BF16) for c in range(NCH)]
    xk = [cx.sb(f"s2_xk{c}", [128, TC], BF16) for c in range(NCH)]
    rt = [cx.sb(f"s2_r{c}", [128, TC]) for c in range(NCH)]
    ka = [cx.sb(f"s2_ka{c}", [128, TC], BF16) for c in range(NCH)]
    tmp = cx.sb("s2_tmp", [128, TC])
    tmp2 = cx.sb("s2_tmp2", [128, TC])
    yo = [cx.sb(f"s2_y{c}", [128, TC]) for c in range(NCH)]
    pr = [cx.psum(f"s2_p{i}", [128, TC]) for i in range(4)]
    marks_ld = {}

    def emit_loads(s, tc):
        p = tc % 2
        t0 = tc * TC
        for c in range(NCH):
            if t0 == 0:
                cx.inc("s2ld", s.dma_start(
                    xt[p][c][:, 1:TC + 1], lnx.ap()[c * 128:(c + 1) * 128, 0:TC]), 16)
            else:
                cx.inc("s2ld", s.dma_start(
                    xt[p][c][:, 0:TC + 1], lnx.ap()[c * 128:(c + 1) * 128, t0 - 1:t0 + TC]), 16)
        marks_ld[tc] = cx.val("s2ld")

    with nc.Block() as blk:
        @blk.vector
        def _(v):
            for p in range(2):
                for c in range(NCH):
                    v.memset(xt[p][c][:, 0:1], 0.0)

    for tc in range(NTC):
        p = tc % 2
        t0 = tc * TC
        with nc.Block() as blk:
            @blk.sync
            def _(s):
                if tc == 0:
                    emit_loads(s, 0)
                    emit_loads(s, 1)
                elif tc + 1 < NTC:
                    emit_loads(s, tc + 1)
                cx.wait(s, "s2st")

            @blk.vector
            def _(v):
                v.wait_ge(cx.sem("s2ld")[0], marks_ld[tc])
                for c in range(NCH):
                    cur = xt[p][c][:, 1:TC + 1]
                    prev = xt[p][c][:, 0:TC]
                    v.tensor_sub(tmp[:, :], cur, prev)
                    v.scalar_tensor_tensor(xr[c][:, :], tmp[:, :], mur_t[:, c:c + 1],
                                           prev, ALU.mult, ALU.add)
                    v.tensor_sub(tmp2[:, :], cur, prev)
                    ins = v.scalar_tensor_tensor(xk[c][:, :], tmp2[:, :], muk_t[:, c:c + 1],
                                                 prev, ALU.mult, ALU.add)
                cx.inc("s2mix", ins)

            sig0 = cx.val("s2sig")
            ka0 = cx.val("s2ka")

            @blk.tensor
            def _(tn):
                cx.wait(tn, "s2mix")
                for m in range(NCH):
                    if m >= 2:
                        cx.wait(tn, "s2sig", sig0 + m - 1)
                        cx.wait(tn, "s2ka", ka0 + m - 1)
                    for c in range(NCH):
                        tn.matmul(pr[m % 2][:, :], w_r[:, c * D + m * 128: c * D + m * 128 + 128],
                                  xr[c][:, :], start=(c == 0), stop=(c == NCH - 1))
                    for c in range(NCH):
                        ins = tn.matmul(pr[2 + m % 2][:, :], w_k[:, c * D + m * 128: c * D + m * 128 + 128],
                                        xk[c][:, :], start=(c == 0), stop=(c == NCH - 1))
                    cx.inc("s2mm", ins)

            @blk.scalar
            def _(s):
                sm = cx.sem("s2mm")
                for m in range(NCH):
                    s.wait_ge(sm[0], sm[1] - (NCH - 1 - m))
                    ins = s.activation(rt[m][:, :], pr[m % 2][:, :], AF.Sigmoid)
                    cx.inc("s2sig", ins)

            @blk.vector
            def _(v):
                sm = cx.sem("s2mm")
                for m in range(NCH):
                    v.wait_ge(sm[0], sm[1] - (NCH - 1 - m))
                    v.tensor_scalar_max(tmp[:, :], pr[2 + m % 2][:, :], 0.0)
                    ins = v.tensor_mul(ka[m][:, :], tmp[:, :], tmp[:, :])
                    cx.inc("s2ka", ins)

            out0 = cx.val("s2out")

            @blk.tensor
            def _(tn):
                cx.wait(tn, "s2ka")
                cx.wait(tn, "s2sig")
                for m in range(NCH):
                    if m >= 2:
                        cx.wait(tn, "s2out", out0 + m - 1)
                    for c in range(NCH):
                        ins = tn.matmul(pr[m % 2][:, :], w_v[:, c * D + m * 128: c * D + m * 128 + 128],
                                        ka[c][:, :], start=(c == 0), stop=(c == NCH - 1))
                    cx.inc("s2mv", ins)

            @blk.vector
            def _(v):
                sm = cx.sem("s2mv")
                for m in range(NCH):
                    v.wait_ge(sm[0], sm[1] - (NCH - 1 - m))
                    ins = v.tensor_mul(yo[m][:, :], pr[m % 2][:, :], rt[m][:, :])
                    cx.inc("s2out", ins)

            @blk.sync
            def _(s):
                sm = cx.sem("s2out")
                for m in range(NCH):
                    s.wait_ge(sm[0], sm[1] - (NCH - 1 - m))
                    cx.inc("s2st", s.dma_start(
                        ysc.ap()[m * 128:(m + 1) * 128, t0:t0 + TC], yo[m][:, :]), 16)

    with nc.Block() as blk:
        @blk.sync
        def _(s):
            cx.wait(s, "s2st")

def build_l1ffn(fuse=False):
    nc = bass.Bass("TRN2", target_bir_lowering=False)
    xin = nc.dram_tensor("xin", [D, T], F32, kind="ExternalInput")
    wr = nc.dram_tensor("wr", [D, D], BF16, kind="ExternalInput")
    wk = nc.dram_tensor("wk", [D, D], BF16, kind="ExternalInput")
    wv = nc.dram_tensor("wv", [D, D], BF16, kind="ExternalInput")
    mur = nc.dram_tensor("mur", [D], F32, kind="ExternalInput")
    muk = nc.dram_tensor("muk", [D], F32, kind="ExternalInput")
    xout = nc.dram_tensor("xout", [D, T], F32, kind="ExternalOutput")
    lnx = nc.dram_tensor("lnx_scr", [D, T], BF16, kind="Internal")
    ysc = nc.dram_tensor("y_scr", [D, T], F32, kind="Internal")
    with ExitStack() as es:
        cx = Ctx(nc, es, fuse=fuse)
        emit_consts(cx)
        emit_l1ffn(cx, xin, wr, wk, wv, mur, muk, xout, lnx, ysc)
    return nc



def emit_l1attn(cx, xin, w1t, b1, w2c, b2, xout, lnx, csc):
    nc = cx.nc
    with cx.scope():
        lnx_sb = [cx.sb(f"r1sb{c}", [128, T], BF16) for c in range(NCH)]
        w_1 = load_weight_bf(cx, "w_1", w1t, D, D, defer=True)[0]
        w_2, wm = load_weight_bf(cx, "w_2", w2c, D, 1, defer=True)
        b1t = load_pp(cx, "b1t", b1)
        ln_phase(cx, "ln", xin, out_sb=lnx_sb, names="r1")
        _l1attn_core(cx, (w_1, w_2, wm, b1t), b2, lnx_sb, csc)
    ln_phase(cx, "ln", csc, out=xout, res=xin, names="r4")


def _l1attn_core(cx, wpre, b2, lnx_sb, csc):
    nc = cx.nc
    w_1, w_2, wm, b1t = wpre
    dma_drain(cx, "wldg", wm)
    b2t = cx.sb("b2t", [1, 1], F32)
    erow = cx.sb("erow", [1, T])
    mrow = cx.sb("mrow", [1, 1])
    drow = cx.sb("drow", [1, T])
    zrow = cx.sb("zrow", [1, T])
    with nc.Block() as blk:
        @blk.sync
        def _(s):
            cx.inc("c0", s.dma_start(b2t[:, :], b2.ap()), 16)
            cx.wait(s, "c0")
        @blk.vector
        def _(v):
            v.memset(zrow[:, :], 0.0)

    hb = [cx.sb(f"r2_h{c}", [128, TC], BF16) for c in range(NCH)]
    ph = [cx.psum(f"r2_p{i}", [128, TC]) for i in range(2)]
    psr = cx.psum("r2_ps", [1, TC])
    for tc in range(NTC):
        t0 = tc * TC
        with nc.Block() as blk:
            h0 = cx.val("r2h")
            e0 = cx.val("r2e")

            @blk.tensor
            def _(tn, t0=t0):
                cx.wait(tn, "r2e", e0)
                for m in range(NCH):
                    if m >= 2:
                        cx.wait(tn, "r2h", h0 + m - 1)
                    for c in range(NCH):
                        ins = tn.matmul(ph[m % 2][:, :], w_1[:, c * D + m * 128: c * D + m * 128 + 128],
                                        lnx_sb[c][:, t0:t0 + TC],
                                        start=(c == 0), stop=(c == NCH - 1))
                    cx.inc("r2mm", ins)

            @blk.scalar
            def _(s):
                sm = cx.sem("r2mm")
                for m in range(NCH):
                    s.wait_ge(sm[0], sm[1] - (NCH - 1 - m))
                    ins = s.activation(hb[m][:, :], ph[m % 2][:, :], AF.Tanh,
                                       bias=b1t[:, m:m + 1])
                    cx.inc("r2h", ins)

            @blk.tensor
            def _(tn):
                sm = cx.sem("r2h")
                for m in range(NCH):
                    tn.wait_ge(sm[0], sm[1] - (NCH - 1 - m))
                    ins = tn.matmul(psr[0:1, :], w_2[:, m:m + 1], hb[m][:, :],
                                    start=(m == 0), stop=(m == NCH - 1))
                cx.inc("r2sd", ins)

            @blk.vector
            def _(v):
                cx.wait(v, "r2sd")
                ins = v.tensor_scalar_add(erow[:, t0:t0 + TC], psr[0:1, :], b2t[0:1, 0:1])
                cx.inc("r2e", ins)

    with nc.Block() as blk:
        @blk.vector
        def _(v):
            cx.wait(v, "r2e")
            v.reduce_max(mrow[:, 0:1], erow[:, :], axis=AX.X)
            ins = v.tensor_scalar_mul(mrow[:, 0:1], mrow[:, 0:1], -1.0)
            cx.inc("r2m", ins)

    with nc.Block() as blk:
        @blk.scalar
        def _(s):
            cx.wait(s, "r2m")
            ins = s.activation(erow[:, :], erow[:, :], AF.Exp, bias=mrow[0:1, 0:1])
            cx.inc("r2ed", ins)

    with nc.Block() as blk:
        @blk.vector
        def _(v):
            cx.wait(v, "r2ed")
            ins = v.tensor_tensor_scan(drow[:, :], erow[:, :], zrow[:, :], 0.0,
                                       ALU.add, ALU.add)
            cx.inc("r2sc", ins)

        @blk.vector
        def _(v):
            cx.wait(v, "r2sc")
            v.tensor_scalar_max(drow[:, :], drow[:, :], 1e-35)
            v.reciprocal(drow[:, :], drow[:, :])

    wrow = cx.sb("r3_w", [128, T])
    nrow = cx.sb("r3_n", [128, T])
    zbig = cx.sb("r3_z", [128, T])
    crow = cx.sb("r3_c", [128, T])
    pse = [cx.psum(f"r3_pe{i}", [128, TC]) for i in range(2)]
    psd = [cx.psum(f"r3_pd{i}", [128, TC]) for i in range(2)]
    with nc.Block() as blk:
        @blk.vector
        def _(v):
            v.memset(zbig[:, :], 0.0)
    for c in range(NCH):
        with nc.Block() as blk:
            w0 = cx.val("r3w")
            c0 = cx.val("r3c")

            @blk.tensor
            def _(tn):
                for tc in range(NTC):
                    if tc >= 2:
                        cx.wait(tn, "r3w", w0 + tc - 1)
                    ins = tn.matmul(pse[tc % 2][:, :], cx.ones_row[0:1, :],
                                    erow[0:1, tc * TC:(tc + 1) * TC], start=True, stop=True)
                    cx.inc("r3bc", ins)

            @blk.vector
            def _(v, c=c):
                sm = cx.sem("r3bc")
                for tc in range(NTC):
                    v.wait_ge(sm[0], sm[1] - (NTC - 1 - tc))
                    ins = v.tensor_mul(wrow[:, tc * TC:(tc + 1) * TC],
                                       lnx_sb[c][:, tc * TC:(tc + 1) * TC], pse[tc % 2][:, :])
                    cx.inc("r3w", ins)
                v.tensor_tensor_scan(nrow[:, :], wrow[:, :], zbig[:, :], 0.0, ALU.add, ALU.add)

            @blk.tensor
            def _(tn):
                for tc in range(NTC):
                    if tc >= 2:
                        cx.wait(tn, "r3c", c0 + tc - 1)
                    ins = tn.matmul(psd[tc % 2][:, :], cx.ones_row[0:1, :],
                                    drow[0:1, tc * TC:(tc + 1) * TC], start=True, stop=True)
                    cx.inc("r3bd", ins)

            @blk.vector
            def _(v):
                sm = cx.sem("r3bd")
                for tc in range(NTC):
                    v.wait_ge(sm[0], sm[1] - (NTC - 1 - tc))
                    ins = v.tensor_mul(crow[:, tc * TC:(tc + 1) * TC],
                                       nrow[:, tc * TC:(tc + 1) * TC], psd[tc % 2][:, :])
                    cx.inc("r3c", ins)

            @blk.sync
            def _(s):
                cx.wait(s, "r3c")
                cx.inc("r3st", s.dma_start(csc.ap()[c * 128:(c + 1) * 128, :], crow[:, :]), 16)
                cx.wait(s, "r3st")


def build_l1attn(fuse=False):
    nc = bass.Bass("TRN2", target_bir_lowering=False)
    xin = nc.dram_tensor("xin", [D, T], F32, kind="ExternalInput")
    w1t = nc.dram_tensor("w1t", [D, D], BF16, kind="ExternalInput")
    b1 = nc.dram_tensor("b1", [D], F32, kind="ExternalInput")
    w2c = nc.dram_tensor("w2c", [D, 1], F32, kind="ExternalInput")
    b2 = nc.dram_tensor("b2", [1, 1], F32, kind="ExternalInput")
    xout = nc.dram_tensor("xout", [D, T], F32, kind="ExternalOutput")
    lnx = nc.dram_tensor("lnx_scr", [D, T], BF16, kind="Internal")
    csc = nc.dram_tensor("c_scr", [D, T], F32, kind="Internal")
    with ExitStack() as es:
        cx = Ctx(nc, es, fuse=fuse)
        emit_consts(cx)
        emit_l1attn(cx, xin, w1t, b1, w2c, b2, xout, lnx, csc)
    return nc



def _proj_block(cx, tn_eng, w_sb, rhs_tiles, ps, m, d_out=D, sem=None):
    for c in range(NCH):
        ins = tn_eng.matmul(ps, w_sb[:, c * d_out + m * 128: c * d_out + m * 128 + 128],
                            rhs_tiles[c], start=(c == 0), stop=(c == NCH - 1))
    if sem is not None:
        cx.inc(sem, ins)


def a2_loads(cx, wq, wk, bq):
    w_q = load_weight_bf(cx, "a2wq", wq, D, D, defer=True)[0]
    w_k, wm = load_weight_bf(cx, "a2wk", wk, D, D, defer=True)
    bq_t = load_pp(cx, "a2bq", bq)
    return w_q, w_k, wm, bq_t


def emit_l0attn_proj(cx, lnx_sb, a2w, wv, qsc, ksc, vsc, qn, kn):
    nc = cx.nc
    w_q, w_k, wm, bq_t = a2w
    dma_drain(cx, "wldg", wm)
    w_v = load_weight_bf(cx, "a2wv", wv, D, D)
    hmask = cx.sb("a2hm", [128, 128], BF16)
    qb = [cx.sb(f"a2_q{m}", [128, TC], BF16) for m in range(NCH)]
    kb = [cx.sb(f"a2_k{m}", [128, TC], BF16) for m in range(NCH)]
    sqt = cx.sb("a2_sq", [128, TC], BF16)
    nmx = cx.sb("a2_nm", [128, 1])
    v_sb = [cx.sb(f"a2_v{i}", [128, 1040], BF16) for i in range(4)]
    ps_q = [cx.psum(f"a2_pq{i}", [128, TC]) for i in range(2)]
    ps_n = [cx.psum(f"a2_pn{i}", [128, TC]) for i in range(2)]
    ps_v = [cx.psum(f"a2_pv{i}", [128, 512]) for i in range(4)]
    vst_marks = []
    qk_marks = []

    with nc.Block() as blk:
        @blk.vector
        def _(v):
            v.memset(hmask[:, :], 0.0)
            v.memset(hmask[0:64, 0:1], 1.0)
            v.memset(hmask[64:128, 64:65], 1.0)
            for i in range(4):
                v.memset(v_sb[i][:, :].rearrange("p (h w) -> p h w", w=65)[:, :, 64:65], 1.0)
            v.memset(qn[:, :], 0.0)
            v.memset(kn[:, :], 0.0)

    for tc in range(NTC):
        t0 = tc * TC
        with nc.Block() as blk:
            pre = [lnx_sb[c][:, t0:t0 + TC] for c in range(NCH)]
            for which in ("q", "k"):
                w_sb = w_q if which == "q" else w_k
                dst = qb if which == "q" else kb
                sc_bias = bq_t if which == "q" else None
                nrun = qn if which == "q" else kn
                ev = f"a2{which}ev"
                mm = f"a2{which}mm"
                sq = f"a2{which}sq"
                nmm = f"a2{which}nmm"
                nd = f"a2{which}nd"
                ev0 = cx.val(ev)
                nmm0 = cx.val(nmm)

                @blk.tensor
                def _(tn, w_sb=w_sb, ev=ev, ev0=ev0, which=which, mm=mm):
                    if which == "k":
                        cx.wait(tn, "a2qev")
                    else:
                        cx.wait(tn, "a2kev")
                    for m in range(NCH):
                        if m >= 2:
                            cx.wait(tn, ev, ev0 + m - 1)
                        _proj_block(cx, tn, w_sb, pre, ps_q[m % 2][:, :], m, sem=mm)

                @blk.vector
                def _(v, dst=dst, sc_bias=sc_bias, ev=ev, mm=mm, tc=tc):
                    sm = cx.sem(mm)
                    if tc >= 1:
                        v.wait_ge(cx.sem("a2st")[0], qk_marks[tc - 1])
                    for m in range(NCH):
                        v.wait_ge(sm[0], sm[1] - (NCH - 1 - m))
                        if sc_bias is not None:
                            ins = v.tensor_scalar(dst[m][:, :], ps_q[m % 2][:, :],
                                                  sc_bias[:, m:m + 1], 0.125,
                                                  ALU.add, ALU.mult)
                        else:
                            ins = v.tensor_copy(dst[m][:, :], ps_q[m % 2][:, :])
                        cx.inc(ev, ins)

                @blk.scalar
                def _(s, dst=dst, ev=ev, sq=sq, nmm=nmm, nmm0=nmm0, which=which):
                    sm = cx.sem(ev)
                    base = sm[1] - NCH
                    if which == "k":
                        cx.wait(s, "a2qnmm")
                    for m in range(NCH):
                        s.wait_ge(sm[0], base + m + 1)
                        if m >= 1:
                            cx.wait(s, nmm, nmm0 + m)
                        ins = s.activation(sqt[:, :], dst[m][:, :], AF.Square)
                        cx.inc(sq, ins)

                nd0 = cx.val(nd)

                @blk.tensor
                def _(tn, sq=sq, nmm=nmm, nd=nd, nd0=nd0):
                    sm = cx.sem(sq)
                    base = sm[1] - NCH
                    for m in range(NCH):
                        tn.wait_ge(sm[0], base + m + 1)
                        if m >= 2:
                            cx.wait(tn, nd, nd0 + m - 1)
                        ins = tn.matmul(ps_n[m % 2][:, :], hmask[:, :], sqt[:, :],
                                        start=True, stop=True)
                        cx.inc(nmm, ins)

                @blk.vector
                def _(v, nrun=nrun, nmm=nmm, nd=nd):
                    sm = cx.sem(nmm)
                    base = sm[1] - NCH
                    for m in range(NCH):
                        v.wait_ge(sm[0], base + m + 1)
                        v.reduce_max(nmx[:, 0:1], ps_n[m % 2][:, :], axis=AX.X)
                        ins = v.tensor_max(nrun[:, m:m + 1], nrun[:, m:m + 1], nmx[:, 0:1])
                        cx.inc(nd, ins)


            vev0 = cx.val("a2vev")

            @blk.tensor
            def _(tn, vev0=vev0, t0=t0):
                for tt in range(4):
                    if tt >= 2:
                        cx.wait(tn, "a2vev", vev0 + tt - 1)
                    for half in range(2):
                        for c in range(NCH):
                            ins = tn.matmul(ps_v[2 * (tt % 2) + half][:, :],
                                            lnx_sb[c][:, t0 + tt * 128: t0 + tt * 128 + 128],
                                            w_v[:, c * D + half * 512: c * D + half * 512 + 512],
                                            start=(c == 0), stop=(c == NCH - 1))
                    cx.inc("a2vmm", ins)

            @blk.vector
            def _(v, tc=tc):
                sm = cx.sem("a2vmm")
                for tt in range(4):
                    v.wait_ge(sm[0], sm[1] - (4 - 1 - tt))
                    if tc >= 1:
                        v.wait_ge(cx.sem("a2vst")[0], vst_marks[(tc - 1) * 4 + tt])
                    for half in range(2):
                        ins = v.tensor_copy(
                            v_sb[tt][:, half * 520:(half + 1) * 520]
                            .rearrange("p (h w) -> p h w", w=65)[:, :, 0:64],
                            ps_v[2 * (tt % 2) + half][:, :].rearrange("p (h w) -> p h w", w=64))
                    cx.inc("a2vev", ins)

            @blk.sync
            def _(s):
                sm = cx.sem("a2qev")
                for m in range(NCH):
                    s.wait_ge(sm[0], sm[1] - NCH + m + 1)
                    cx.inc("a2st", s.dma_start(
                        qsc.ap()[m * 128:(m + 1) * 128, t0:t0 + TC], qb[m][:, :]), 16)
                sm2 = cx.sem("a2kev")
                for m in range(NCH):
                    s.wait_ge(sm2[0], sm2[1] - NCH + m + 1)
                    cx.inc("a2st", s.dma_start(
                        ksc.ap()[m * 128:(m + 1) * 128, t0:t0 + TC], kb[m][:, :]), 16)
                qk_marks.append(cx.val("a2st"))
                sm3 = cx.sem("a2vev")
                for tt in range(4):
                    s.wait_ge(sm3[0], sm3[1] - (4 - 1 - tt))
                    cx.inc("a2vst", s.dma_start(vsc.ap()[tc * 4 + tt], v_sb[tt][:, :]), 16)
                    vst_marks.append(cx.val("a2vst"))

    with nc.Block() as blk:
        @blk.sync
        def _(s):
            cx.wait(s, "a2st")
            cx.wait(s, "a2vst")




def emit_l0attn_flash(cx, qsc, ksc, vsc, osc, qn, kn, albq, alck, mskc):
    nc = cx.nc
    NET = 8
    qh = [[cx.sb(f"a3_qh{b}{i}", [70, T], BF16) for i in range(2)]
          for b in range(2)]
    kh = [[cx.sb(f"a3_kh{b}{i}", [70, T], BF16) for i in range(2)]
          for b in range(2)]
    vp = [cx.sb(f"a3_v{b}", [128, 32 * 130], BF16) for b in range(2)]
    oT = cx.sb("a3_o", [128, T], BF16)
    et = [cx.sb(f"a3_e{i}", [128, TC], BF16) for i in range(NET)]
    mrow = cx.sb("a3_mr", [128, 2])
    mneg = cx.sb("a3_mn", [128, 2])
    drow = [cx.sb(f"a3_dr{i}", [65, TC]) for i in range(2)]
    one_tc = cx.sb("a3_one", [1, TC])
    bc_sb = cx.sb("a3_bc", [128, TC])
    msk = cx.sb("a3_msk", [128, 640])
    ps_s = [cx.psum(f"a3_ps{i}", [128, TC]) for i in range(3)]
    po = [[cx.psum(f"a3_po{hl}{p}", [128, TC]) for p in range(2)] for hl in range(2)]
    ps_x = [cx.psum("a3_px0", [128, TC])]

    with nc.Block() as blk:
        @blk.sync
        def _(s):
            cx.inc("a3mld", s.dma_start(msk[:, :], mskc.ap()), 16)
            cx.wait(s, "a3mld")

        @blk.vector
        def _(v):
            v.memset(one_tc[:, :], 1.0)
            v.memset(drow[0][:, :], 1.0)
            v.memset(drow[1][:, :], 1.0)
            for b in range(2):
                for i in range(2):
                    v.memset(kh[b][i][64:70, :], 1.0)
                    v.memset(qh[b][i][64:70, :], 1.0)

    ld_marks = {}
    g_marks = {}

    def a3_loads(s, pn):
        b = pn % 2
        for hl in range(2):
            r0 = pn * 128 + hl * 64
            for m in range(NCH):
                cx.inc("a3ld", s.dma_start(
                    qh[b][hl][0:64, m * TC:(m + 1) * TC],
                    qsc.ap()[r0:r0 + 64, m * TC:(m + 1) * TC]), 16)
                cx.inc("a3ld", s.dma_start(
                    kh[b][hl][0:64, m * TC:(m + 1) * TC],
                    ksc.ap()[r0:r0 + 64, m * TC:(m + 1) * TC]), 16)
        for jt in range(32):
            cx.inc("a3ld", s.dma_start(
                vp[b][:, jt * 130:(jt + 1) * 130],
                vsc.ap()[jt, :, pn * 130:(pn + 1) * 130]), 16)
        ld_marks[pn] = cx.val("a3ld")

    def a3_gloads(g, pn):
        b = pn % 2
        for hl in range(2):
            h = 2 * pn + hl
            cx.inc("wldg", g.dma_start(qh[b][hl][67:70, :],
                                       albq.ap()[3 * h:3 * h + 3, :]), 16)
            cx.inc("wldg", g.dma_start(kh[b][hl][64:67, :],
                                       alck.ap()[3 * h:3 * h + 3, :]), 16)
        g_marks[pn] = cx.val("wldg")

    for p in range(8):
        with nc.Block() as blk:
            @blk.sync
            def _(s):
                cx.wait(s, "a3ost")
                if p == 0:
                    a3_loads(s, 0)
                s.wait_ge(cx.sem("a3ld")[0], ld_marks[p])

            @blk.gpsimd
            def _(g):
                if p == 0:
                    a3_gloads(g, 0)
                g.wait_ge(cx.sem("wldg")[0], g_marks[p])

            @blk.vector
            def _(v):
                cx.wait(v, "a3ld")
                ins = v.tensor_mul(mrow[:, 0:1], qn[:, p:p + 1], kn[:, p:p + 1])
                cx.inc("a3m1", ins)

            @blk.scalar
            def _(s):
                cx.wait(s, "a3m1")
                s.activation(mrow[:, 1:2], mrow[:, 0:1], AF.Ln, bias=cx.tiny_t[:, 0:1])
                ins = s.activation(mrow[:, 0:1], mrow[:, 1:2], AF.Exp, scale=0.5)
                cx.inc("a3m2", ins)

            @blk.vector
            def _(v):
                cx.wait(v, "a3m2")
                ins = v.tensor_scalar_mul(mrow[:, 0:1], mrow[:, 0:1], -1.03)
                cx.inc("a3m3", ins)

            @blk.tensor
            def _(tn):
                cx.wait(tn, "a3m3")
                cx.wait(tn, "a3ev")
                tn.matmul(ps_x[0][:, 0:1], cx.ones_row[0:1, :], mrow[0:1, 0:1],
                          start=True, stop=True, skip_group_check=True)
                ins = tn.matmul(ps_x[0][:, 1:2], cx.ones_row[64:65, :], mrow[64:65, 0:1],
                                start=True, stop=True, skip_group_check=True)
                cx.inc("a3m4", ins)

            @blk.vector
            def _(v):
                cx.wait(v, "a3m4")
                ins = v.tensor_copy(mneg[:, 0:2], ps_x[0][:, 0:2])
                cx.inc("a3m5", ins)

        NIG = T // TC
        items = []
        evseq = {}
        pvlast = {}
        nev = 0
        for ig in range(NIG):
            jhi = (ig * TC + TC - 1) // 128
            jlos = [ALIBI[2 * p + hl]["jlo"][ig] for hl in range(2)]
            for jt in range(min(jlos), jhi + 1):
                for hl in range(2):
                    if jt >= jlos[hl]:
                        items.append((ig, hl, jt))
            for hl in range(2):
                evseq[(ig, hl)] = nev + 1
                nev += 1
        firsts = set()
        seen = set()
        for k, (ig, hl, jt) in enumerate(items):
            if (ig, hl) not in seen:
                firsts.add(k)
                seen.add((ig, hl))
            pvlast[(ig, hl)] = k
        n = len(items)

        def mask_ops(k):
            ig, hl, jt = items[k]
            d0 = jt * 128 - ig * TC
            ops = []
            if d0 > 0:
                ops.append(("neg", 0, min(d0, TC)))
            if -127 <= d0 < TC:
                a = max(0, d0)
                b = min(TC, d0 + 128)
                ops.append(("mc", a, b, a - d0))
            return ops

        nmask = [0]
        for k in range(n):
            nmask.append(nmask[-1] + len(mask_ops(k)))

        gfirst = {}
        glast = {}
        for k, (ig, hl, jt) in enumerate(items):
            if ig not in gfirst:
                gfirst[ig] = k
            glast[ig] = k
        BDELAY = 6
        bc_due = {}
        for ig in range(NIG):
            cap = n - 1
            if ig + 2 < NIG:
                cap = min(cap, gfirst[ig + 2] - 1)
            dk = max(min(glast[ig] + BDELAY, cap), glast[ig])
            bc_due.setdefault(dk, []).append(ig)

        s0 = cx.val("a3s"); e0 = cx.val("a3e"); v0 = cx.val("a3mk")
        p0 = cx.val("a3p"); rc0 = cx.val("a3rc"); bc0 = cx.val("a3bcd")
        ev0 = cx.val("a3ev")

        with nc.Block() as blk:
            @blk.sync
            def _(s):
                if p + 1 < 8:
                    a3_loads(s, p + 1)

            @blk.gpsimd
            def _(g):
                if p + 1 < 8:
                    a3_gloads(g, p + 1)

            @blk.tensor
            def _(tn):
                cx.wait(tn, "a3m5")

                def emit_pv(k):
                    ig, hl, jt = items[k]
                    cx.wait(tn, "a3e", e0 + k + 1)
                    if k in firsts and ig >= 2:
                        cx.wait(tn, "a3ev", ev0 + evseq[(ig - 2, hl)])
                    ins = tn.matmul(po[hl][ig % 2][0:65, :],
                                    vp[p % 2][:, jt * 130 + hl * 65: jt * 130 + (hl + 1) * 65],
                                    et[k % NET][:, :], start=(k in firsts),
                                    stop=(pvlast[(ig, hl)] == k), skip_group_check=True)
                    cx.inc("a3p", ins)
                    for ig2 in bc_due.get(k, []):
                        cx.wait(tn, "a3rc", rc0 + ig2 + 1)
                        if ig2 >= 1:
                            cx.wait(tn, "a3ev", ev0 + 2 * ig2)
                        ins = tn.matmul(ps_x[0][:, :], cx.bsel[0:65, :],
                                        drow[ig2 % 2][0:65, :], start=True,
                                        stop=True, skip_group_check=True)
                        cx.inc("a3bcd", ins)

                for k, (ig, hl, jt) in enumerate(items):
                    if k >= 3:
                        cx.wait(tn, "a3e", e0 + k - 2)
                    ins = tn.matmul(ps_s[k % 3][:, :],
                                    kh[p % 2][hl][0:70, jt * 128:(jt + 1) * 128],
                                    qh[p % 2][hl][0:70, ig * TC:(ig + 1) * TC],
                                    start=True, stop=True, skip_group_check=True)
                    cx.inc("a3s", ins)
                    if k >= 1:
                        emit_pv(k - 1)
                emit_pv(n - 1)

            @blk.vector
            def _(v):
                for k, (ig, hl, jt) in enumerate(items):
                    ops = mask_ops(k)
                    if ops:
                        v.wait_ge(cx.sem("a3s")[0], s0 + k + 1)
                        for op in ops:
                            if op[0] == "neg":
                                ins = v.tensor_scalar_add(ps_s[k % 3][:, op[1]:op[2]],
                                                          ps_s[k % 3][:, op[1]:op[2]], NEG)
                            else:
                                a, b, off = op[1], op[2], op[3]
                                ins = v.tensor_add(ps_s[k % 3][:, a:b],
                                                   ps_s[k % 3][:, a:b],
                                                   msk[:, off:off + (b - a)])
                            cx.inc("a3mk", ins)
                    if pvlast[(ig, hl)] == k and all(
                            pvlast[(ig, h2)] <= k for h2 in range(2)):
                        dt = drow[ig % 2]
                        v.wait_ge(cx.sem("a3p")[0],
                                  p0 + max(pvlast[(ig, 0)], pvlast[(ig, 1)]) + 1)
                        if ig >= 2:
                            cx.wait(v, "a3bcd", bc0 + ig - 1)
                        v.tensor_scalar_max(dt[0:1, :],
                                            po[0][ig % 2][64:65, :], 1e-35)
                        v.tensor_scalar_max(dt[64:65, :],
                                            po[1][ig % 2][64:65, :], 1e-35)
                        ins = v.reciprocal(dt[0:65, :], dt[0:65, :])
                        cx.inc("a3rc", ins)
                    for ig2 in bc_due.get(k, []):
                        cx.wait(v, "a3bcd", bc0 + ig2 + 1)
                        v.tensor_copy(bc_sb[:, :], ps_x[0][:, :])
                        for hl2 in range(2):
                            ins = v.tensor_mul(
                                oT[hl2 * 64:(hl2 + 1) * 64, ig2 * TC:(ig2 + 1) * TC],
                                po[hl2][ig2 % 2][0:64, :], bc_sb[hl2 * 64:(hl2 + 1) * 64, :])
                            cx.inc("a3ev", ins)

            @blk.scalar
            def _(s):
                for k, (ig, hl, jt) in enumerate(items):
                    hh = ALIBI[2 * p + hl]
                    s.wait_ge(cx.sem("a3s")[0], s0 + k + 1)
                    if nmask[k + 1] > 0:
                        s.wait_ge(cx.sem("a3mk")[0], v0 + nmask[k + 1])
                    if k >= NET:
                        cx.wait(s, "a3p", p0 + k - NET + 1)
                    ins = s.activation(et[k % NET][:, :], ps_s[k % 3][:, :],
                                       AF.Exp, bias=mneg[:, hl:hl + 1])
                    cx.inc("a3e", ins)

        with nc.Block() as blk:
            @blk.sync
            def _(s):
                cx.wait(s, "a3ev")
                for m in range(NCH):
                    cx.inc("a3ost", s.dma_start(
                        osc.ap()[p * 128:(p + 1) * 128, m * TC:(m + 1) * TC],
                        oT[:, m * TC:(m + 1) * TC]), 16)

    with nc.Block() as blk:
        @blk.sync
        def _(s):
            cx.wait(s, "a3ost")

def emit_l0attn_out(cx, osc, wo, bo, xin, xout, ysc):
    nc = cx.nc
    with cx.scope():
        ysb = [cx.sb(f"a4sb{m}", [128, T]) for m in range(NCH)]
        with cx.scope():
            w_o = load_weight_bf(cx, "a4wo", wo, D, D)
            bo_t = load_pp(cx, "a4bo", bo)
            ob = [cx.sb(f"a4_o{c}", [128, TC], BF16) for c in range(NCH)]
            ps = [cx.psum(f"a4_p{i}", [128, TC]) for i in range(2)]
            for tc in range(NTC):
                t0 = tc * TC
                with nc.Block() as blk:
                    @blk.sync
                    def _(s):
                        for c in range(NCH):
                            cx.inc("a4ld", s.dma_start(
                                ob[c][:, :], osc.ap()[c * 128:(c + 1) * 128, t0:t0 + TC]), 16)

                    ev0 = cx.val("a4ev")

                    @blk.tensor
                    def _(tn):
                        cx.wait(tn, "a4ld")
                        for m in range(NCH):
                            if m >= 2:
                                cx.wait(tn, "a4ev", ev0 + m - 1)
                            _proj_block(cx, tn, w_o, [t[:, :] for t in ob],
                                        ps[m % 2][:, :], m, sem="a4mm")

                    @blk.vector
                    def _(v, t0=t0):
                        sm = cx.sem("a4mm")
                        for m in range(NCH):
                            v.wait_ge(sm[0], sm[1] - (NCH - 1 - m))
                            ins = v.tensor_scalar_add(ysb[m][:, t0:t0 + TC],
                                                      ps[m % 2][:, :],
                                                      bo_t[:, m:m + 1])
                            cx.inc("a4ev", ins)
        ln_phase(cx, "ln", None, out=xout, res=xin, names="a5", src_sb_full=ysb)


def build_l0attn(debug=False, fuse=False):
    nc = bass.Bass("TRN2", target_bir_lowering=False)
    xin = nc.dram_tensor("xin", [D, T], F32, kind="ExternalInput")
    wq = nc.dram_tensor("wq", [D, D], BF16, kind="ExternalInput")
    wk = nc.dram_tensor("wk", [D, D], BF16, kind="ExternalInput")
    wv = nc.dram_tensor("wv", [D, D], BF16, kind="ExternalInput")
    wo = nc.dram_tensor("wo", [D, D], BF16, kind="ExternalInput")
    bq = nc.dram_tensor("bq", [D], F32, kind="ExternalInput")
    bo = nc.dram_tensor("bo", [D], F32, kind="ExternalInput")
    albq = nc.dram_tensor("albq", [48, T], F32, kind="ExternalInput")
    alck = nc.dram_tensor("alck", [48, T], F32, kind="ExternalInput")
    mskc = nc.dram_tensor("mskc", [128, 640], F32, kind="ExternalInput")
    xout = nc.dram_tensor("xout", [D, T], F32, kind="ExternalOutput")
    lnx = nc.dram_tensor("lnx_scr", [D, T], BF16, kind="Internal")
    qsc = nc.dram_tensor("q_scr", [D, T], BF16, kind="Internal")
    ksc = nc.dram_tensor("k_scr", [D, T], BF16, kind="Internal")
    vsc = nc.dram_tensor("v_scr", [32, 128, 1040], BF16, kind="Internal")
    osc = nc.dram_tensor("o_scr", [D, T], BF16,
                         kind="ExternalOutput" if debug else "Internal")
    ysc = nc.dram_tensor("y_scr", [D, T], F32, kind="Internal")
    with ExitStack() as es:
        cx = Ctx(nc, es, fuse=fuse)
        emit_consts(cx)
        qn = cx.sb("a_qn", [128, NCH])
        kn = cx.sb("a_kn", [128, NCH])
        with cx.scope():
            lnx_sb = [cx.sb(f"a1sb{c}", [128, T], BF16) for c in range(NCH)]
            a2w = a2_loads(cx, wq, wk, bq)
            ln_phase(cx, "ln", xin, out_sb=lnx_sb, names="a1")
            emit_l0attn_proj(cx, lnx_sb, a2w, wv, qsc, ksc, vsc, qn, kn)
        with cx.scope():
            emit_l0attn_flash(cx, qsc, ksc, vsc, osc, qn, kn, albq, alck, mskc)
        emit_l0attn_out(cx, osc, wo, bo, xin, xout, ysc)
    return nc



FQ_CH = 4
W1C = 22
TC2 = 256


def load_weight_bf2(cx, tag, src_ap, d_in, d_out, defer=False):
    nch = d_in // 128
    w = cx.sb(tag, [128, nch * d_out], BF16)
    with cx.nc.Block() as blk:
        @blk.gpsimd
        def _(g):
            for c in range(nch):
                cx.inc("wldg", g.dma_start(w[:, c * d_out:(c + 1) * d_out],
                                           src_ap[c * 128:(c + 1) * 128, :]), 16)
            if not defer:
                cx.wait(g, "wldg")
    if defer:
        return w, cx.val("wldg")
    return w


def emit_l0ffn_layer(cx, l, usc, lhsc, wqkv, w1, b1, w2, b2, lng, lnb, qs, ks,
                     rcsq, rssq, rcsk, rssk, mskc, mskw, half=False):
    nc = cx.nc
    with cx.scope():
        q_rot = cx.sb(f"f{l}qr", [128, FQ_CH * T], BF16)
        k_rot = cx.sb(f"f{l}kr", [128, FQ_CH * T], BF16)
        v_sb = cx.sb(f"f{l}vv", [128, 32 * 520], BF16)
        oT = cx.sb(f"f{l}oT", [128, FQ_CH * T], BF16)
        with cx.scope():
            _l0ffn_qkv(cx, l, usc, wqkv, qs, ks, rcsq, rssq, rcsk, rssk,
                       q_rot, k_rot, v_sb)
        wopre = load_weight_bf2(cx, f"f{l}cw", cx.fwo_ap[l], 512, D,
                                defer=True)
        with cx.scope():
            _l0ffn_attn(cx, l, q_rot, k_rot, v_sb, oT, qs, ks, mskc, mskw)
        with cx.scope():
            _l0ffn_wo(cx, l, oT, usc, wopre)
    if half:
        return
    with cx.scope():
        gam = load_pp(cx, f"f{l}lg", lng, row=l)
        bet = load_pp(cx, f"f{l}lb", lnb, row=l)
        w1pre = load_weight_bf2(cx, f"f{l}dw1", w1.ap()[l], D, 2 * HPAD,
                                defer=True)
        ln_phase(cx, "ln", usc, out_bf=lhsc, shift=True, gamma=gam, beta=bet,
                 names=f"f{l}n")
        with cx.scope():
            _l0ffn_ffn(cx, l, lhsc, w1pre, b1, w2, b2, usc)


def _l0ffn_qkv(cx, l, usc, wqkv, qs, ks, rcsq, rssq, rcsk, rssk, q_rot, k_rot, v_sb):
    nc = cx.nc
    u = "fa"
    nm = f"f{l}a"
    w_sb = load_weight_bf2(cx, nm + "w", wqkv.ap()[l], D, 1536)
    qs_t = cx.sb(nm + "qs", [128, 2])
    hmask = cx.sb(nm + "hm", [128, 128])
    sel2 = cx.sb(nm + "s2", [128, 128])
    ub = [cx.sb(f"{nm}_u{c}", [128, TC + 1], BF16) for c in range(NCH)]
    qraw = cx.sb(nm + "qraw", [128, TC])
    sq = cx.sb(nm + "sq", [128, TC])
    nnc = cx.sb(nm + "nnc", [128, TC])
    rn = cx.sb(nm + "rn", [128, TC])
    qn = cx.sb(nm + "qn", [128, TC])
    t1 = cx.sb(nm + "t1", [128, TC])
    t2 = cx.sb(nm + "t2", [128, TC])
    rot = [cx.sb(f"{nm}_rc{i}", [128, TC]) for i in range(2)]
    ps_m = [cx.psum(f"{nm}_pm{i}", [128, TC]) for i in range(2)]
    ps_n = [cx.psum(f"{nm}_pn{i}", [128, TC]) for i in range(2)]
    ps_r = [cx.psum(f"{nm}_pr{i}", [128, TC]) for i in range(2)]
    ps_v = [cx.psum(f"{nm}_pv{i}", [128, TC]) for i in range(2)]

    with nc.Block() as blk:
        @blk.vector
        def _(v):
            v.memset(hmask[:, :], 0.0)
            v.memset(hmask[0:64, 0:1], 1.0)
            v.memset(hmask[64:128, 64:65], 1.0)
            v.memset(sel2[:, :], 0.0)
            v.memset(sel2[0:1, 0:64], 1.0)
            v.memset(sel2[64:65, 64:128], 1.0)
            v.memset(v_sb[:, :].rearrange("p (h w) -> p h w", w=65)[:, :, 64:65], 1.0)
            for c in range(NCH):
                v.memset(ub[c][:, 0:1], 0.0)

        @blk.sync
        def _(s):
            for half in range(2):
                cx.inc(u + "qsld", s.dma_start(
                    qs_t[half * 64:(half + 1) * 64, 0:1],
                    qs.ap()[l].rearrange("(p o) -> p o", o=1)), 16)
                cx.inc(u + "qsld", s.dma_start(
                    qs_t[half * 64:(half + 1) * 64, 1:2],
                    ks.ap()[l].rearrange("(p o) -> p o", o=1)), 16)
            cx.wait(s, u + "qsld")

    for tc in range(NTC):
        t0 = tc * TC
        with nc.Block() as blk:
            @blk.gpsimd
            def _(g):
                for c in range(NCH):
                    if t0 == 0:
                        cx.inc(u + "ldg", g.dma_start(
                            ub[c][:, 1:TC + 1], usc.ap()[c * 128:(c + 1) * 128, 0:TC]), 16)
                    else:
                        cx.inc(u + "ldg", g.dma_start(
                            ub[c][:, 0:TC + 1],
                            usc.ap()[c * 128:(c + 1) * 128, t0 - 1:t0 + TC]), 16)

            def hv(c):
                return ub[c][:, 1:TC + 1] if c < 4 else ub[c][:, 0:TC]

            for which in ("q", "k"):
                mof = 0 if which == "q" else 4
                dst = q_rot if which == "q" else k_rot
                csrc = rcsq if which == "q" else rcsk
                ssrc = rssq if which == "q" else rssk
                qscol = 0 if which == "q" else 1
                w = u + which
                ev0 = cx.val(w + "ev"); sqd0 = cx.val(w + "sqd")
                nnm0 = cx.val(w + "nnm"); ncd0 = cx.val(w + "ncd")
                rnd0 = cx.val(w + "rnd"); rd0 = cx.val(w + "rd")
                ap0 = cx.val(w + "ap")

                @blk.sync
                def _(s, w=w, csrc=csrc, ssrc=ssrc):
                    other = (u + "k") if w.endswith("q") else (u + "q")
                    cx.wait(s, other + "ap")
                    cx.inc(w + "rld", s.dma_start(rot[0][:, :], csrc.ap()[:, t0:t0 + TC]), 16)
                    cx.inc(w + "rld", s.dma_start(rot[1][:, :], ssrc.ap()[:, t0:t0 + TC]), 16)

                @blk.tensor
                def _(tn, w=w, mof=mof, ev0=ev0, sqd0=sqd0, rnd0=rnd0,
                      ncd0=ncd0, ap0=ap0):
                    cx.wait(tn, u + "ldg")
                    order = [("qkv", 0), ("qkv", 1), ("nn", 0), ("r", 0),
                             ("qkv", 2), ("nn", 1), ("r", 1),
                             ("qkv", 3), ("nn", 2), ("r", 2), ("nn", 3), ("r", 3)]
                    for kind, mi in order:
                        if kind == "qkv":
                            if mi >= 2:
                                cx.wait(tn, w + "ev", ev0 + mi - 1)
                            for c in range(NCH):
                                ins = tn.matmul(
                                    ps_m[mi % 2][:, :],
                                    w_sb[:, c * 1536 + (mof + mi) * 128:
                                         c * 1536 + (mof + mi) * 128 + 128],
                                    hv(c), start=(c == 0), stop=(c == NCH - 1))
                            cx.inc(w + "mm", ins)
                        elif kind == "nn":
                            cx.wait(tn, w + "sqd", sqd0 + mi + 1)
                            if mi >= 2:
                                cx.wait(tn, w + "ncd", ncd0 + mi - 1)
                            ins = tn.matmul(ps_n[mi % 2][:, :], hmask[:, :], sq[:, :],
                                            start=True, stop=True, skip_group_check=True)
                            cx.inc(w + "nnm", ins)
                        else:
                            cx.wait(tn, w + "rnd", rnd0 + mi + 1)
                            if mi >= 2:
                                cx.wait(tn, w + "ap", ap0 + mi - 1)
                            ins = tn.matmul(ps_r[mi % 2][:, :], sel2[:, :], rn[:, :],
                                            start=True, stop=True, skip_group_check=True)
                            cx.inc(w + "rd", ins)

                @blk.scalar
                def _(s, w=w, ncd0=ncd0, rd0=rd0):
                    for mi in range(FQ_CH):
                        cx.wait(s, w + "ncd", ncd0 + mi + 1)
                        if mi >= 1:
                            cx.wait(s, w + "rd", rd0 + mi)
                        s.activation(rn[:, :], nnc[:, :], AF.Ln)
                        cx.inc(w + "rnd", s.activation(rn[:, :], rn[:, :], AF.Exp,
                                                       scale=-0.5))

                @blk.vector
                def _(v, w=w, qscol=qscol, dst=dst, nnm0=nnm0, rd0=rd0, mof=mof):
                    sm = cx.sem(w + "mm")
                    for mi in range(FQ_CH):
                        v.wait_ge(sm[0], sm[1] - (FQ_CH - 1 - mi))
                        cx.wait(v, w + "rld")
                        cx.inc(w + "ev", v.tensor_copy(qraw[:, :], ps_m[mi % 2][:, :]))
                        cx.inc(w + "sqd", v.tensor_mul(sq[:, :], qraw[:, :], qraw[:, :]))
                        cx.wait(v, w + "nnm", nnm0 + mi + 1)
                        cx.inc(w + "ncd", v.tensor_scalar_max(nnc[:, :], ps_n[mi % 2][:, :],
                                                              1e-24))
                        cx.wait(v, w + "rd", rd0 + mi + 1)
                        v.scalar_tensor_tensor(qn[:, :], qraw[:, :], qs_t[:, qscol:qscol + 1],
                                               ps_r[mi % 2][:, :], ALU.mult, ALU.mult)
                        for g in range(4):
                            ga, gb = g * 32, (g ^ 1) * 32
                            v.tensor_copy(t2[ga:ga + 32, :], qn[gb:gb + 32, :])
                        v.tensor_mul(t1[:, :], qn[:, :], rot[0][:, :])
                        v.tensor_mul(t2[:, :], t2[:, :], rot[1][:, :])
                        ins = v.tensor_add(dst[:, mi * T + t0: mi * T + t0 + TC],
                                           t1[:, :], t2[:, :])
                        cx.inc(w + "ap", ins)

            vv0 = cx.val(u + "vev")

            @blk.tensor
            def _(tn, vv0=vv0):
                cx.wait(tn, u + "kap")
                for tt in range(4):
                    if tt >= 2:
                        cx.wait(tn, u + "vev", vv0 + tt - 1)
                    for c in range(NCH):
                        off = 1 if c < 4 else 0
                        ins = tn.matmul(ps_v[tt % 2][:, :],
                                        ub[c][:, off + tt * 128: off + tt * 128 + 128],
                                        w_sb[:, c * 1536 + 1024: c * 1536 + 1536],
                                        start=(c == 0), stop=(c == NCH - 1))
                    cx.inc(u + "vmm", ins)

            @blk.vector
            def _(v):
                sm = cx.sem(u + "vmm")
                for tt in range(4):
                    v.wait_ge(sm[0], sm[1] - (4 - 1 - tt))
                    jt = tc * 4 + tt
                    ins = v.tensor_copy(
                        v_sb[:, jt * 520:(jt + 1) * 520]
                        .rearrange("p (h w) -> p h w", w=65)[:, :, 0:64],
                        ps_v[tt % 2][:, :].rearrange("p (h w) -> p h w", w=64))
                    cx.inc(u + "vev", ins)


def _l0ffn_attn(cx, l, q_rot, k_rot, v_sb, oT, qs, ks, mskc, mskw):
    nc = cx.nc
    u = "fb"
    nm = f"f{l}b"
    NET = 8
    et = [cx.sb(f"{nm}_e{i}", [128, TC], BF16) for i in range(NET)]
    mc = cx.sb(nm + "mc", [128, 128])
    mw = cx.sb(nm + "mw", [128, 128])
    qsr = cx.sb(nm + "qsr", [1, 64])
    ksr = cx.sb(nm + "ksr", [1, 64])
    mrow = cx.sb(nm + "mrow", [1, 2])
    mb = cx.sb(nm + "mb", [128, 1])
    drow = [cx.sb(f"{nm}_dr{i}", [65, TC]) for i in range(2)]
    one_tcf = cx.sb(nm + "_one", [1, TC])
    bc = cx.sb(nm + "_bc", [128, TC])
    ps_s = [cx.psum(f"{nm}_ps{i}", [128, TC]) for i in range(3)]
    po = [[cx.psum(f"{nm}_po{hl}{p}", [128, TC]) for p in range(2)] for hl in range(2)]
    ps_x = [cx.psum(nm + "_px0", [128, TC])]

    with nc.Block() as blk:
        @blk.sync
        def _(s):
            cx.inc(u + "cl", s.dma_start(mc[:, :], mskc.ap()[:, 0:128]), 16)
            cx.inc(u + "cl", s.dma_start(mw[:, :], mskw.ap()), 16)
            cx.inc(u + "cl", s.dma_start(qsr[:, :], qs.ap()[l].rearrange("(o n) -> o n", o=1)), 16)
            cx.inc(u + "cl", s.dma_start(ksr[:, :], ks.ap()[l].rearrange("(o n) -> o n", o=1)), 16)

        @blk.vector
        def _(v):
            v.memset(one_tcf[:, :], 1.0)
            v.memset(drow[0][:, :], 1.0)
            v.memset(drow[1][:, :], 1.0)
            cx.wait(v, u + "cl")
            v.tensor_reduce(mrow[:, 0:1], qsr[:, :], AX.X, ALU.max,
                            apply_absolute_value=True)
            v.tensor_reduce(mrow[:, 1:2], ksr[:, :], AX.X, ALU.max,
                            apply_absolute_value=True)
            v.tensor_mul(mrow[:, 0:1], mrow[:, 0:1], mrow[:, 1:2])
            ins = v.tensor_scalar_mul(mrow[:, 0:1], mrow[:, 0:1], -32.0 * 1.02)
            cx.inc(u + "m1", ins)

        @blk.tensor
        def _(tn):
            cx.wait(tn, u + "m1")
            ins = tn.matmul(ps_x[0][:, 0:1], cx.ones_row[0:1, :], mrow[0:1, 0:1],
                            start=True, stop=True)
            cx.inc(u + "m2", ins)

        @blk.vector
        def _(v):
            cx.wait(v, u + "m2")
            ins = v.tensor_copy(mb[:, 0:1], ps_x[0][:, 0:1])
            cx.inc(u + "m3", ins)

    NIG = T // TC
    items = []
    evseq = {}
    pvlast = {}
    nev = 0
    for ig in range(NIG):
        jts = list(range(max(0, 4 * ig - 4), 4 * ig + 4))
        for pp in range(4):
            for hl in range(2):
                for idx, jt in enumerate(jts):
                    items.append([ig, pp, hl, jt, idx == 0, idx == len(jts) - 1])
                evseq[(ig, pp, hl)] = nev + 1
                nev += 1
    items2 = []
    for ig in range(NIG):
        jts = list(range(max(0, 4 * ig - 4), 4 * ig + 4))
        for pp in range(4):
            for idx, jt in enumerate(jts):
                for hl in range(2):
                    items2.append((ig, pp, hl, jt, idx == 0, idx == len(jts) - 1))
    items = items2
    for k, (ig, pp, hl, jt, first, last) in enumerate(items):
        if last:
            pvlast[(ig, pp, hl)] = k
    n = len(items)

    def mask_ops(k):
        ig, pp, hl, jt, first, last = items[k]
        d0 = jt * 128 - ig * TC
        ops = []
        if d0 > 0:
            ops.append(("neg", 0, min(d0, TC)))
        if -127 <= d0 < TC:
            a = max(0, d0)
            b = min(TC, d0 + 128)
            ops.append(("mc", a, b, a - d0))
        d1 = d0 + 512
        if 0 <= d1 < TC:
            b = min(TC, d1 + 128)
            ops.append(("mw", d1, b, 0))
            if d1 + 128 < TC:
                ops.append(("neg", d1 + 128, TC))
        return ops

    nmask = [0]
    for k in range(n):
        nmask.append(nmask[-1] + len(mask_ops(k)))

    groups = [(ig, pp) for ig in range(NIG) for pp in range(4)]
    gfirst = {}
    for k, (ig, pp, hl, jt, first, last) in enumerate(items):
        if (ig, pp) not in gfirst:
            gfirst[(ig, pp)] = k
    BDELAY = 6
    bc_due = {}
    for gi, (ig, pp) in enumerate(groups):
        kl = pvlast[(ig, pp, 1)]
        cap = n - 1
        if gi + 2 < len(groups):
            cap = min(cap, gfirst[groups[gi + 2]] - 1)
        dk = max(min(kl + BDELAY, cap), kl)
        bc_due.setdefault(dk, []).append((ig, pp))

    s0 = cx.val(u + "s"); e0 = cx.val(u + "e"); v0 = cx.val(u + "mk")
    p0 = cx.val(u + "p"); rc0 = cx.val(u + "rc"); bc0 = cx.val(u + "bcd")
    ev0 = cx.val(u + "ev")

    with nc.Block() as blk:
        @blk.tensor
        def _(tn):
            cx.wait(tn, u + "m3")

            def emit_pv(k):
                ig, pp, hl, jt, first, last = items[k]
                cx.wait(tn, u + "e", e0 + k + 1)
                if first:
                    if pp >= 2:
                        cx.wait(tn, u + "ev", ev0 + evseq[(ig, pp - 2, hl)])
                    elif ig >= 1:
                        cx.wait(tn, u + "ev", ev0 + evseq[(ig - 1, pp + 2, hl)])
                ins = tn.matmul(po[hl][pp % 2][0:65, :],
                                v_sb[:, jt * 520 + (2 * pp + hl) * 65:
                                     jt * 520 + (2 * pp + hl + 1) * 65],
                                et[k % NET][:, :], start=first, stop=last,
                                skip_group_check=True)
                cx.inc(u + "p", ins)
                for (ig2, pp2) in bc_due.get(k, []):
                    g = 4 * ig2 + pp2 + 1
                    cx.wait(tn, u + "rc", rc0 + g)
                    if g >= 2:
                        cx.wait(tn, u + "ev", ev0 + 2 * (g - 1))
                    ins = tn.matmul(ps_x[0][:, :], cx.bsel[0:65, :],
                                    drow[g % 2][0:65, :], start=True, stop=True,
                                    skip_group_check=True)
                    cx.inc(u + "bcd", ins)

            for k, (ig, pp, hl, jt, first, last) in enumerate(items):
                if k >= 3:
                    cx.wait(tn, u + "e", e0 + k - 2)
                r0, r1 = (0, 64) if hl == 0 else (64, 128)
                ins = tn.matmul(ps_s[k % 3][:, :],
                                k_rot[r0:r1, pp * T + jt * 128: pp * T + (jt + 1) * 128],
                                q_rot[r0:r1, pp * T + ig * TC: pp * T + (ig + 1) * TC],
                                start=True, stop=True, skip_group_check=True)
                cx.inc(u + "s", ins)
                if k >= 1:
                    emit_pv(k - 1)
            emit_pv(n - 1)

        @blk.vector
        def _(v):
            for k, (ig, pp, hl, jt, first, last) in enumerate(items):
                ops = mask_ops(k)
                if ops:
                    v.wait_ge(cx.sem(u + "s")[0], s0 + k + 1)
                    for op in ops:
                        if op[0] == "neg":
                            ins = v.tensor_scalar_add(ps_s[k % 3][:, op[1]:op[2]],
                                                      ps_s[k % 3][:, op[1]:op[2]], NEG)
                        else:
                            m_t = mc if op[0] == "mc" else mw
                            a, b, off = op[1], op[2], op[3]
                            ins = v.tensor_add(ps_s[k % 3][:, a:b],
                                               ps_s[k % 3][:, a:b],
                                               m_t[:, off:off + (b - a)])
                        cx.inc(u + "mk", ins)
                if last and hl == 1:
                    g = 4 * ig + pp + 1
                    dt = drow[g % 2]
                    cx.wait(v, u + "p", p0 + pvlast[(ig, pp, 1)] + 1)
                    if g >= 3:
                        cx.wait(v, u + "bcd", bc0 + g - 2)
                    v.tensor_scalar_max(dt[0:1, :], po[0][pp % 2][64:65, :], 1e-35)
                    v.tensor_scalar_max(dt[64:65, :], po[1][pp % 2][64:65, :], 1e-35)
                    ins = v.reciprocal(dt[0:65, :], dt[0:65, :])
                    cx.inc(u + "rc", ins)
                for (ig2, pp2) in bc_due.get(k, []):
                    g = 4 * ig2 + pp2 + 1
                    cx.wait(v, u + "bcd", bc0 + g)
                    v.tensor_copy(bc[:, :], ps_x[0][:, :])
                    for hl2 in range(2):
                        ins = v.tensor_mul(
                            oT[hl2 * 64:(hl2 + 1) * 64, pp2 * T + ig2 * TC: pp2 * T + (ig2 + 1) * TC],
                            po[hl2][pp2 % 2][0:64, :], bc[hl2 * 64:(hl2 + 1) * 64, :])
                        cx.inc(u + "ev", ins)

        @blk.scalar
        def _(s):
            for k, (ig, pp, hl, jt, first, last) in enumerate(items):
                s.wait_ge(cx.sem(u + "s")[0], s0 + k + 1)
                if nmask[k + 1] > 0:
                    s.wait_ge(cx.sem(u + "mk")[0], v0 + nmask[k + 1])
                if k >= NET:
                    cx.wait(s, u + "p", p0 + k - NET + 1)
                ins = s.activation(et[k % NET][:, :], ps_s[k % 3][:, :],
                                   AF.Exp, scale=8.0, bias=mb[:, 0:1])
                cx.inc(u + "e", ins)

def _l0ffn_wo(cx, l, oT, usc, wopre):
    nc = cx.nc
    u = "fc"
    nm = f"f{l}c"
    w_o, wom = wopre
    dma_drain(cx, "wldg", wom)
    uin = [[cx.sb(f"{nm}_u{p}{c}", [128, TC]) for c in range(NCH)] for p in range(2)]
    yv = [cx.sb(f"{nm}_y{m}", [128, TC]) for m in range(NCH)]
    ps = [cx.psum(f"{nm}_p{i}", [128, TC]) for i in range(2)]
    fc_marks = {}

    def fc_loads(s, tc):
        p = tc % 2
        t0 = tc * TC
        for c in range(NCH):
            cx.inc(u + "ld", s.dma_start(
                uin[p][c][:, :], usc.ap()[c * 128:(c + 1) * 128, t0:t0 + TC]), 16)
        fc_marks[tc] = cx.val(u + "ld")

    for tc in range(NTC):
        p = tc % 2
        t0 = tc * TC
        with nc.Block() as blk:
            @blk.sync
            def _(s):
                if tc == 0:
                    fc_loads(s, 0)
                    fc_loads(s, 1)
                elif tc + 1 < NTC:
                    fc_loads(s, tc + 1)
                cx.wait(s, u + "st")

            ev0 = cx.val(u + "ev")

            @blk.tensor
            def _(tn, ev0=ev0):
                tn.wait_ge(cx.sem(u + "ld")[0], fc_marks[tc])
                for m in range(NCH):
                    if m >= 2:
                        cx.wait(tn, u + "ev", ev0 + m - 1)
                    for c in range(4):
                        ins = tn.matmul(ps[m % 2][:, :],
                                        w_o[:, c * D + m * 128: c * D + m * 128 + 128],
                                        oT[:, c * T + t0: c * T + t0 + TC],
                                        start=(c == 0), stop=(c == 3))
                    cx.inc(u + "mm", ins)

            @blk.vector
            def _(v):
                sm = cx.sem(u + "mm")
                for m in range(NCH):
                    v.wait_ge(sm[0], sm[1] - (NCH - 1 - m))
                    ins = v.tensor_add(yv[m][:, :], ps[m % 2][:, :], uin[p][m][:, :])
                    cx.inc(u + "ev", ins)

            @blk.sync
            def _(s):
                sm = cx.sem(u + "ev")
                for m in range(NCH):
                    s.wait_ge(sm[0], sm[1] - (NCH - 1 - m))
                    cx.inc(u + "st", s.dma_start(
                        usc.ap()[m * 128:(m + 1) * 128, t0:t0 + TC], yv[m][:, :]), 16)
    with nc.Block() as blk:
        @blk.sync
        def _(s):
            cx.wait(s, u + "st")


def _l0ffn_ffn(cx, l, lhsc, w1pre, b1, w2, b2, usc):
    nc = cx.nc
    u = "fd"
    nm = f"f{l}d"
    w_1, w1m = w1pre
    dma_drain(cx, "wldg", w1m)
    w_2 = load_weight_bf2(cx, nm + "w2", w2.ap()[l], HPAD, D)
    b1t = load_pp(cx, nm + "b1", b1, n=2 * HPAD, row=l)
    b2t = load_pp(cx, nm + "b2", b2, row=l)
    hx = [[cx.sb(f"{nm}_x{p}{c}", [128, TC], BF16) for c in range(NCH)] for p in range(2)]
    uin = [cx.sb(f"{nm}_u{c}", [128, TC]) for c in range(NCH)]
    fd_marks = {}
    fd_umarks = {}

    def fd_loads(s, tc):
        p = tc % 2
        t0 = tc * TC
        for c in range(NCH):
            cx.inc(u + "ld", s.dma_start(
                hx[p][c][:, :], lhsc.ap()[c * 128:(c + 1) * 128, t0:t0 + TC]), 16)
        fd_marks[tc] = cx.val(u + "ld")
    hsb = [cx.sb(f"{nm}_h{k}", [128, TC], BF16) for k in range(W1C)]
    gact = cx.sb(nm + "g", [128, TC])
    yv = [cx.sb(f"{nm}_y{m}", [128, TC]) for m in range(NCH)]
    ps_a = [cx.psum(f"{nm}_pa{i}", [128, TC]) for i in range(2)]
    ps_g = [cx.psum(f"{nm}_pg{i}", [128, TC]) for i in range(2)]
    ps_w2 = [cx.psum(f"{nm}_pw{i}", [128, TC]) for i in range(2)]

    for tc in range(NTC):
        p = tc % 2
        t0 = tc * TC
        with nc.Block() as blk:
            @blk.sync
            def _(s):
                if tc == 0:
                    fd_loads(s, 0)
                    fd_loads(s, 1)
                elif tc + 1 < NTC:
                    fd_loads(s, tc + 1)
                cx.wait(s, u + "st")
                for c in range(NCH):
                    cx.inc(u + "ld", s.dma_start(
                        uin[c][:, :], usc.ap()[c * 128:(c + 1) * 128, t0:t0 + TC]), 16)
                fd_umarks[tc] = cx.val(u + "ld")

            hd0 = cx.val(u + "hd")
            fe0 = cx.val(u + "fe")

            @blk.tensor
            def _(tn):
                tn.wait_ge(cx.sem(u + "ld")[0], fd_marks[tc])
                for kk in range(W1C):
                    if kk >= 2:
                        cx.wait(tn, u + "hd", hd0 + kk - 1)
                    for c in range(NCH):
                        tn.matmul(ps_a[kk % 2][:, :],
                                  w_1[:, c * 2 * HPAD + kk * 128: c * 2 * HPAD + kk * 128 + 128],
                                  hx[p][c][:, :], start=(c == 0), stop=(c == NCH - 1))
                    for c in range(NCH):
                        ins = tn.matmul(ps_g[kk % 2][:, :],
                                        w_1[:, c * 2 * HPAD + HPAD + kk * 128:
                                            c * 2 * HPAD + HPAD + kk * 128 + 128],
                                        hx[p][c][:, :], start=(c == 0), stop=(c == NCH - 1))
                    cx.inc(u + "mm", ins)
                cx.wait(tn, u + "hd", hd0 + W1C)
                for m in range(NCH):
                    if m >= 2:
                        cx.wait(tn, u + "fe", fe0 + m - 1)
                    for kk in range(W1C):
                        ins = tn.matmul(ps_w2[m % 2][:, :],
                                        w_2[:, kk * D + m * 128: kk * D + m * 128 + 128],
                                        hsb[kk][:, :], start=(kk == 0),
                                        stop=(kk == W1C - 1))
                    cx.inc(u + "w2m", ins)

            @blk.scalar
            def _(s):
                sm = cx.sem(u + "mm")
                for kk in range(W1C):
                    s.wait_ge(sm[0], sm[1] - (W1C - 1 - kk))
                    if kk >= 1:
                        cx.wait(s, u + "hd", hd0 + kk)
                    ins = s.activation(gact[:, :], ps_g[kk % 2][:, :],
                                       AF.Gelu, bias=b1t[:, W1C + kk:W1C + kk + 1])
                    cx.inc(u + "ga", ins)

            @blk.vector
            def _(v):
                sm = cx.sem(u + "ga")
                for kk in range(W1C):
                    v.wait_ge(sm[0], sm[1] - (W1C - 1 - kk))
                    ins = v.scalar_tensor_tensor(hsb[kk][:, :], ps_a[kk % 2][:, :],
                                                 b1t[:, kk:kk + 1], gact[:, :],
                                                 ALU.add, ALU.mult)
                    cx.inc(u + "hd", ins)

            @blk.vector
            def _(v):
                sm = cx.sem(u + "w2m")
                v.wait_ge(cx.sem(u + "ld")[0], fd_umarks[tc])
                for m in range(NCH):
                    v.wait_ge(sm[0], sm[1] - (NCH - 1 - m))
                    v.tensor_scalar_add(yv[m][:, :], ps_w2[m % 2][:, :], b2t[:, m:m + 1])
                    ins = v.tensor_add(yv[m][:, :], yv[m][:, :], uin[m][:, :])
                    cx.inc(u + "fe", ins)

            @blk.sync
            def _(s):
                sm = cx.sem(u + "fe")
                for m in range(NCH):
                    s.wait_ge(sm[0], sm[1] - (NCH - 1 - m))
                    cx.inc(u + "st", s.dma_start(
                        usc.ap()[m * 128:(m + 1) * 128, t0:t0 + TC], yv[m][:, :]), 16)
    with nc.Block() as blk:
        @blk.sync
        def _(s):
            cx.wait(s, u + "st")

def build_l0ffn(upto=None, fuse=False):
    nc = bass.Bass("TRN2", target_bir_lowering=False)
    xin = nc.dram_tensor("xin", [D, T], F32, kind="ExternalInput")
    wqkv = nc.dram_tensor("wqkv", [2, D, 1536], BF16, kind="ExternalInput")
    w1 = nc.dram_tensor("w1", [2, D, 2 * HPAD], BF16, kind="ExternalInput")
    b1 = nc.dram_tensor("b1", [2, 2 * HPAD], F32, kind="ExternalInput")
    w2 = nc.dram_tensor("w2", [2, HPAD, D], BF16, kind="ExternalInput")
    b2 = nc.dram_tensor("b2", [2, D], F32, kind="ExternalInput")
    wo = nc.dram_tensor("wo", [2, 512, D], BF16, kind="ExternalInput")
    lng = nc.dram_tensor("lng", [2, D], F32, kind="ExternalInput")
    lnb = nc.dram_tensor("lnb", [2, D], F32, kind="ExternalInput")
    qs = nc.dram_tensor("qs", [2, 64], F32, kind="ExternalInput")
    ks = nc.dram_tensor("ks", [2, 64], F32, kind="ExternalInput")
    rcsq = nc.dram_tensor("rcsq", [128, T], F32, kind="ExternalInput")
    rssq = nc.dram_tensor("rssq", [128, T], F32, kind="ExternalInput")
    rcsk = nc.dram_tensor("rcsk", [128, T], F32, kind="ExternalInput")
    rssk = nc.dram_tensor("rssk", [128, T], F32, kind="ExternalInput")
    mskc = nc.dram_tensor("mskc", [128, 640], F32, kind="ExternalInput")
    mskw = nc.dram_tensor("mskw", [128, 128], F32, kind="ExternalInput")
    xout = nc.dram_tensor("xout", [D, T], F32, kind="ExternalOutput")
    usc = nc.dram_tensor("u_scr", [D, T], F32, kind="Internal")
    lhsc = nc.dram_tensor("lh_scr", [D, T], BF16, kind="Internal")
    with ExitStack() as es:
        cx = Ctx(nc, es, fuse=fuse)
        cx.fwo_ap = [wo.ap()[i] for i in range(2)]
        emit_consts(cx)
        ln_phase(cx, "ln", xin, out=usc, names="q1")
        stages = {"q1": 0, "l0attn": 1, "l0ffn": 2, "l1attn": 3, None: 99}
        lim = stages[upto]
        for l in range(2):
            if lim <= 2 * l:
                break
            emit_l0ffn_layer(cx, l, usc, lhsc, wqkv, w1, b1, w2, b2, lng, lnb,
                             qs, ks, rcsq, rssq, rcsk, rssk, mskc, mskw,
                             half=(lim == 2 * l + 1))
        if upto is None:
            ln_phase(cx, "ln", usc, out=xout, res=xin, names="q7")
        else:
            with nc.Block() as blk:
                @blk.sync
                def _(sy):
                    for c in range(NCH):
                        cx.inc("bst", sy.dma_start(xout.ap()[c * 128:(c + 1) * 128, :],
                                                   usc.ap()[c * 128:(c + 1) * 128, :]), 16)
                    cx.wait(sy, "bst")
    return nc



def build_full(upto=None):
    nc = bass.Bass("TRN2", target_bir_lowering=False)
    xin = nc.dram_tensor("xin", [D, T], F32, kind="ExternalInput")
    xout = nc.dram_tensor("xout", [D, T], F32, kind="ExternalOutput")
    awq = nc.dram_tensor("awq", [D, D], BF16, kind="ExternalInput")
    awk = nc.dram_tensor("awk", [D, D], BF16, kind="ExternalInput")
    awv = nc.dram_tensor("awv", [D, D], BF16, kind="ExternalInput")
    awo = nc.dram_tensor("awo", [D, D], BF16, kind="ExternalInput")
    abq = nc.dram_tensor("abq", [D], F32, kind="ExternalInput")
    abo = nc.dram_tensor("abo", [D], F32, kind="ExternalInput")
    albq = nc.dram_tensor("albq", [48, T], F32, kind="ExternalInput")
    alck = nc.dram_tensor("alck", [48, T], F32, kind="ExternalInput")
    mskc = nc.dram_tensor("mskc", [128, 640], F32, kind="ExternalInput")
    mskw = nc.dram_tensor("mskw", [128, 128], F32, kind="ExternalInput")
    fwqkv = nc.dram_tensor("fwqkv", [2, D, 1536], BF16, kind="ExternalInput")
    fw1 = nc.dram_tensor("fw1", [2, D, 2 * HPAD], BF16, kind="ExternalInput")
    fb1 = nc.dram_tensor("fb1", [2, 2 * HPAD], F32, kind="ExternalInput")
    fw2 = nc.dram_tensor("fw2", [2, HPAD, D], BF16, kind="ExternalInput")
    fb2 = nc.dram_tensor("fb2", [2, D], F32, kind="ExternalInput")
    fwo = nc.dram_tensor("fwo", [2, 512, D], BF16, kind="ExternalInput")
    flng = nc.dram_tensor("flng", [2, D], F32, kind="ExternalInput")
    flnb = nc.dram_tensor("flnb", [2, D], F32, kind="ExternalInput")
    fqs = nc.dram_tensor("fqs", [2, 64], F32, kind="ExternalInput")
    fks = nc.dram_tensor("fks", [2, 64], F32, kind="ExternalInput")
    rcsq = nc.dram_tensor("rcsq", [128, T], F32, kind="ExternalInput")
    rssq = nc.dram_tensor("rssq", [128, T], F32, kind="ExternalInput")
    rcsk = nc.dram_tensor("rcsk", [128, T], F32, kind="ExternalInput")
    rssk = nc.dram_tensor("rssk", [128, T], F32, kind="ExternalInput")
    cw1t = nc.dram_tensor("cw1t", [D, D], BF16, kind="ExternalInput")
    cb1 = nc.dram_tensor("cb1", [D], F32, kind="ExternalInput")
    cw2c = nc.dram_tensor("cw2c", [D, 1], F32, kind="ExternalInput")
    cb2 = nc.dram_tensor("cb2", [1, 1], F32, kind="ExternalInput")
    rwr = nc.dram_tensor("rwr", [D, D], BF16, kind="ExternalInput")
    rwk = nc.dram_tensor("rwk", [D, D], BF16, kind="ExternalInput")
    rwv = nc.dram_tensor("rwv", [D, D], BF16, kind="ExternalInput")
    rmur = nc.dram_tensor("rmur", [D], F32, kind="ExternalInput")
    rmuk = nc.dram_tensor("rmuk", [D], F32, kind="ExternalInput")
    xres = nc.dram_tensor("xres", [D, T], F32,
                          kind="ExternalOutput" if upto else "Internal")
    lnxb = nc.dram_tensor("lnxb_scr", [D, T], BF16, kind="Internal")
    qsc = nc.dram_tensor("q_scr", [D, T], BF16, kind="Internal")
    ksc = nc.dram_tensor("k_scr", [D, T], BF16, kind="Internal")
    vsc = nc.dram_tensor("v_scr", [32, 128, 1040], BF16, kind="Internal")
    osc = nc.dram_tensor("o_scr", [D, T], BF16, kind="Internal")
    ysc = nc.dram_tensor("y_scr", [D, T], F32, kind="Internal")
    usc = nc.dram_tensor("u_scr", [D, T], F32, kind="Internal")
    lhsc = nc.dram_tensor("lh_scr", [D, T], BF16, kind="Internal")
    csc = nc.dram_tensor("c_scr", [D, T], F32, kind="Internal")

    with ExitStack() as es:
        cx = Ctx(nc, es, fuse=True)
        cx.fwo_ap = [fwo.ap()[i] for i in range(2)]
        emit_consts(cx)
        qn = cx.sb("a_qn", [128, NCH])
        kn = cx.sb("a_kn", [128, NCH])
        with cx.scope():
            lnx_sb = [cx.sb(f"a1sb{c}", [128, T], BF16) for c in range(NCH)]
            a2w = a2_loads(cx, awq, awk, abq)
            ln_phase(cx, "ln", xin, out_sb=lnx_sb, names="a1")
            emit_l0attn_proj(cx, lnx_sb, a2w, awv, qsc, ksc, vsc, qn, kn)
        with cx.scope():
            emit_l0attn_flash(cx, qsc, ksc, vsc, osc, qn, kn, albq, alck, mskc)
        emit_l0attn_out(cx, osc, awo, abo, xin, xres, ysc)
        if upto == "b1":
            return nc
        ln_phase(cx, "ln", xres, out=usc, names="q1")
        for l in range(2):
            emit_l0ffn_layer(cx, l, usc, lhsc, fwqkv, fw1, fb1, fw2, fb2,
                             flng, flnb, fqs, fks, rcsq, rssq, rcsk, rssk,
                             mskc, mskw)
        ln_phase(cx, "ln", usc, out=xres, res=xres, names="q7")
        if upto == "b2":
            return nc
        with cx.scope():
            lnx_sb = [cx.sb(f"r1sb{c}", [128, T], BF16) for c in range(NCH)]
            w_1 = load_weight_bf(cx, "w_1", cw1t, D, D, defer=True)[0]
            w_2, wm = load_weight_bf(cx, "w_2", cw2c, D, 1, defer=True)
            b1t = load_pp(cx, "b1t", cb1)
            ln_phase(cx, "ln", xres, out_sb=lnx_sb, names="r1")
            _l1attn_core(cx, (w_1, w_2, wm, b1t), cb2, lnx_sb, csc)
        ln_phase(cx, "ln", csc, out=xres, res=xres, names="r4")
        if upto == "b3":
            return nc
        with cx.scope():
            wpre = _l1ffn_loads(cx, rwr, rwk, rwv, rmur, rmuk)
            ln_phase(cx, "ln", xres, out_bf=lhsc, names="s1")
            with cx.scope():
                _l1ffn_core(cx, wpre, lhsc, ysc)
        ln_phase(cx, "ln", ysc, out=xout, res=xres, names="s3")
    return nc



_CACHE = {}


def _programs():
    if not _CACHE:
        _CACHE["full"] = build_full()
    return _CACHE


def kernel(**inputs):
    h = _host_shared(inputs)
    x = np.asarray(inputs["L0_attn_x"], dtype=np.float32)
    B = x.shape[0]
    cores = list(range(B))
    progs = _programs()
    im = []
    for b in range(B):
        im.append({
            "xin": np.ascontiguousarray(x[b].T),
            "awq": h["aWq"], "awk": h["aWk"], "awv": h["aWv"], "awo": h["aWo"],
            "abq": h["abq"], "abo": h["abo"],
            "albq": h["alibi_bq"], "alck": h["alibi_ck"],
            "mskc": h["mask_causal"], "mskw": h["mask_window"],
            "fwqkv": h["fwqkv"], "fw1": h["fw1"], "fb1": h["fb1"],
            "fw2": h["fw2"], "fb2": h["fb2"], "fwo": h["fwo"],
            "flng": h["flng"], "flnb": h["flnb"], "fqs": h["fqs"], "fks": h["fks"],
            "rcsq": h["rot_csq"], "rssq": h["rot_ssq"],
            "rcsk": h["rot_csk"], "rssk": h["rot_ssk"],
            "cw1t": h["cW1T"], "cb1": h["cb1"], "cw2c": h["cW2c"], "cb2": h["cb2"],
            "rwr": h["rWrT"], "rwk": h["rWkT"], "rwv": h["rWvT"],
            "rmur": h["rmur"], "rmuk": h["rmuk"],
        })
    r = run_bass_kernel_spmd(progs["full"], im, core_ids=cores)
    out = np.stack([np.asarray(r.results[b]["xout"]).T for b in range(B)])
    return np.ascontiguousarray(out, dtype=np.float32)


# version 22
# speedup vs baseline: 1.0059x; 1.0059x over previous
import numpy as np
from contextlib import ExitStack

import concourse.bass as bass
import concourse.mybir as mybir
from concourse.bass_utils import run_bass_kernel_spmd

F32 = mybir.dt.float32
BF16 = mybir.dt.bfloat16
AF = mybir.ActivationFunctionType
ALU = mybir.AluOpType
AX = mybir.AxisListType

D = 1024
T = 4096
NCH = 8
TC = 512
NTC = T // TC
EPS_LN = 1e-5
NEG = -1.0e9
HID = 2730
HPAD = 2816



def _f(x):
    return np.ascontiguousarray(np.asarray(x), dtype=np.float32)


def _bf(x):
    import ml_dtypes
    return np.ascontiguousarray(
        np.asarray(x, dtype=np.float32).astype(ml_dtypes.bfloat16))


def _host_shared(inputs):
    d = {}
    d["aWq"] = _bf(inputs["L0_attn_Wq"]); d["aWk"] = _bf(inputs["L0_attn_Wk"])
    d["aWv"] = _bf(inputs["L0_attn_Wv"]); d["aWo"] = _bf(inputs["L0_attn_Wo"])
    d["abq"] = _f(inputs["L0_attn_bq"]); d["abo"] = _f(inputs["L0_attn_bo"])
    w1 = _f(inputs["L0_ffn_w1"]); b1 = _f(inputs["L0_ffn_b1"])
    w1p = np.zeros((2, D, 2 * HPAD), np.float32)
    b1p = np.zeros((2, 2 * HPAD), np.float32)
    w1p[:, :, :HID] = w1[:, :, :HID]; w1p[:, :, HPAD:HPAD + HID] = w1[:, :, HID:]
    b1p[:, :HID] = b1[:, :HID]; b1p[:, HPAD:HPAD + HID] = b1[:, HID:]
    w2p = np.zeros((2, HPAD, D), np.float32)
    w2p[:, :HID, :] = _f(inputs["L0_ffn_w2"])
    d["fwqkv"] = _bf(inputs["L0_ffn_wqkv"])
    d["fw1"] = _bf(w1p); d["fb1"] = b1p; d["fw2"] = _bf(w2p)
    d["fwo"] = _bf(inputs["L0_ffn_wo"]); d["fb2"] = _f(inputs["L0_ffn_b2"])
    d["flng"] = _f(inputs["L0_ffn_ln_g"]); d["flnb"] = _f(inputs["L0_ffn_ln_b"])
    d["fqs"] = _f(inputs["L0_ffn_q_scale"]); d["fks"] = _f(inputs["L0_ffn_k_scale"])
    d["cW1T"] = _bf(np.asarray(inputs["L1_attn_W1"]).T)
    d["cb1"] = _f(inputs["L1_attn_b1_"])
    d["cW2c"] = _f(np.asarray(inputs["L1_attn_W2"]).reshape(1, D).T)
    d["cb2"] = _f(np.asarray(inputs["L1_attn_b2_"]).reshape(1, 1))
    d["rWrT"] = _bf(np.asarray(inputs["L1_ffn_W_r"]).T)
    d["rWkT"] = _bf(np.asarray(inputs["L1_ffn_W_k"]).T)
    d["rWvT"] = _bf(np.asarray(inputs["L1_ffn_W_v"]).T)
    d["rmur"] = _f(inputs["L1_ffn_mu_r"]); d["rmuk"] = _f(inputs["L1_ffn_mu_k"])
    d.update(_host_consts())
    return d


def _alibi_plan():
    slopes = (2.0 ** (-8.0 / 16.0)) ** np.arange(1, 17)
    heads = []
    for h in range(16):
        sl = float(slopes[h])
        skip = 104.0 / sl
        jlos = []
        dists = set()
        for ig in range(T // 512):
            igbase = 512 * ig
            jlo = max(0, int(np.ceil((igbase - skip - 127) / 128.0)))
            jlos.append(jlo)
            for jt in range(jlo, (igbase + 511) // 128 + 1):
                dists.add(igbase - 128 * jt)
        dists = sorted(dists)
        heads.append({"slope": sl, "dists": dists, "jlo": jlos,
                      "dmap": {dd: k for k, dd in enumerate(dists)}})
    return heads


ALIBI = _alibi_plan()
ALIBI_ND = max(len(hh["dists"]) for hh in ALIBI)


def _host_consts():
    d = {}
    jj, ii = np.meshgrid(np.arange(128), np.arange(128), indexing="ij")
    mc = np.zeros((128, 640), np.float32)
    mc[:, :128] = np.where(jj > ii, NEG, 0.0)
    d["mask_causal"] = mc
    d["mask_window"] = np.where(jj < ii, NEG, 0.0).astype(np.float32)
    hb = np.zeros((16, 128, ALIBI_ND), np.float32)
    for h in range(16):
        hh = ALIBI[h]
        for k, dd in enumerate(hh["dists"]):
            hb[h, :, k] = hh["slope"] * (np.arange(128) - dd)
    d["alibi_hb"] = hb.reshape(16 * 128, ALIBI_ND)
    import ml_dtypes as _mld

    def _split3(v):
        out = np.zeros((3, T), np.float32)
        r = v.astype(np.float64)
        for t in range(3):
            hi = r.astype(np.float32).astype(_mld.bfloat16).astype(np.float64)
            out[t] = hi.astype(np.float32)
            r = r - hi
        return out

    albq = np.zeros((16, 3, T), np.float32)
    alck = np.zeros((16, 3, T), np.float32)
    for h in range(16):
        albq[h] = _split3(-ALIBI[h]["slope"] * np.arange(T, dtype=np.float64))
        alck[h] = _split3(+ALIBI[h]["slope"] * np.arange(T, dtype=np.float64))
    d["alibi_bq"] = albq.reshape(48, T)
    d["alibi_ck"] = alck.reshape(48, T)
    DIM_HEAD, WINDOW = 64, 512
    half_idx = np.arange(0, DIM_HEAD, 2, dtype=np.float64)
    inv_freq = 1.0 / 10000.0 ** (half_idx / DIM_HEAD)
    t = np.arange(T, dtype=np.float64)
    freqs = np.concatenate([t[:, None] * inv_freq[None, :]] * 2, axis=-1)
    base = (half_idx + 0.4 * DIM_HEAD) / (1.4 * DIM_HEAD)
    power = (t - T // 2) / float(WINDOW // 2)
    scale = np.concatenate([base[None, :] ** power[:, None]] * 2, axis=-1)
    cos, sin = np.cos(freqs), np.sin(freqs)
    sgn = np.where(np.arange(DIM_HEAD) < DIM_HEAD // 2, -1.0, 1.0)
    for nm, v in (("csq", (cos * scale).T), ("ssq", (sin * scale).T * sgn[:, None]),
                  ("csk", (cos / scale).T), ("ssk", (sin / scale).T * sgn[:, None])):
        d["rot_" + nm] = np.ascontiguousarray(
            np.concatenate([v, v], axis=0), dtype=np.float32)
    return d



GP_DMA_SEMS = ("wldg", "faldg")


class Ctx:
    def __init__(self, nc, es, fuse=False):
        self.nc = nc
        self.es = es
        self.es_root = es
        self.fuse = fuse
        self._sems = {}

    def _canon(self, name):
        if not self.fuse:
            return name
        if name in GP_DMA_SEMS:
            return "g_dmg"
        if name.endswith(("ld", "st", "cl")) or name == "c0":
            return "g_dma"
        return name

    def sb(self, name, shape, dtype=F32):
        return self.es.enter_context(self.nc.sbuf_tensor(name, list(shape), dtype))

    def psum(self, name, shape, dtype=F32):
        return self.es.enter_context(self.nc.psum_tensor(name, list(shape), dtype))

    def scope(self):
        outer = self

        class _S:
            def __enter__(self_s):
                self_s.es2 = ExitStack()
                self_s.saved = outer.es
                outer.es = self_s.es2
                return outer

            def __exit__(self_s, *a):
                outer.es = self_s.saved
                return self_s.es2.__exit__(*a)

        return _S()

    def sem(self, name):
        name = self._canon(name)
        if name not in self._sems:
            self._sems[name] = [self.es_root.enter_context(self.nc.semaphore(name)), 0]
        return self._sems[name]

    def inc(self, name, instr, k=1):
        s = self.sem(name)
        instr.then_inc(s[0], k)
        s[1] += k

    def val(self, name):
        return self.sem(name)[1]

    def wait(self, eng, name, v=None):
        s = self.sem(name)
        v = s[1] if v is None else v
        if v > 0:
            eng.wait_ge(s[0], v)


def emit_consts(cx):
    cx.ones_col = cx.sb("cm_ones_col", [128, 1], F32)
    cx.ones_col_bf = cx.sb("cm_ones_col_bf", [128, 1], BF16)
    cx.ones_row = cx.sb("cm_ones_row", [65, 128], F32)
    cx.eps_t = cx.sb("cm_eps", [1, 1], F32)
    cx.tiny_t = cx.sb("cm_tiny", [128, 1], F32)
    cx.bsel = cx.sb("cm_bsel", [65, 128], F32)
    with cx.nc.Block() as blk:
        @blk.gpsimd
        def _(g):
            g.memset(cx.ones_col[:, :], 1.0)
            g.memset(cx.ones_col_bf[:, :], 1.0)
            g.memset(cx.ones_row[:, :], 1.0)
            g.memset(cx.eps_t[:, :], EPS_LN)
            g.memset(cx.tiny_t[:, :], 1e-38)
            g.memset(cx.bsel[:, :], 0.0)
            g.memset(cx.bsel[0:1, 0:64], 1.0)
            g.memset(cx.bsel[64:65, 64:128], 1.0)


def load_weight_bf(cx, tag, hbm, d_in, d_out, defer=False):
    nch = d_in // 128
    w = cx.sb(tag, [128, nch * d_out], BF16)
    with cx.nc.Block() as blk:
        @blk.gpsimd
        def _(g):
            for c in range(nch):
                cx.inc("wldg", g.dma_start(w[:, c * d_out:(c + 1) * d_out],
                                           hbm.ap()[c * 128:(c + 1) * 128, :]), 16)
            if not defer:
                cx.wait(g, "wldg")
    if defer:
        return w, cx.val("wldg")
    return w


def dma_drain(cx, sem, val):
    with cx.nc.Block() as blk:
        @blk.sync
        def _(s):
            s.wait_ge(cx.sem(sem)[0], val)


def load_pp(cx, tag, hbm_vec, n=D, row=None):
    nch = n // 128
    tile = cx.sb(tag, [128, nch], F32)
    src = hbm_vec.ap() if row is None else hbm_vec.ap()[row]
    with cx.nc.Block() as blk:
        @blk.sync
        def _(s):
            for c in range(nch):
                cx.inc("ppld", s.dma_start(
                    tile[:, c:c + 1],
                    src[c * 128:(c + 1) * 128].rearrange("(p o) -> p o", o=1)), 16)
            cx.wait(s, "ppld")
    return tile


def ln_phase(cx, u, src, out=None, out_bf=None, res=None, shift=False,
             gamma=None, beta=None, final_wait=True, names=None, out_sb=None,
             src_sb_full=None):
    with cx.scope():
        _ln_phase_body(cx, u, src, out, out_bf, res, shift, gamma, beta,
                       names or u, out_sb, src_sb_full)


def _ln_phase_body(cx, u, src, out, out_bf, res, shift, gamma, beta, nm,
                   out_sb=None, src_sb_full=None):
    nc = cx.nc
    inplace = src_sb_full is not None
    direct = (out_sb is not None and out is None and out_bf is None
              and res is None and gamma is None)
    direct_bf = (out_bf is not None and out is None and res is None
                 and not inplace)
    W = TC + 1 if shift else TC
    xin = None if inplace else [
        [cx.sb(f"{nm}_x{p}{c}", [128, W]) for c in range(NCH)] for p in range(2)]
    rin = [[cx.sb(f"{nm}_r{p}{c}", [128, TC]) for c in range(NCH)]
           for p in range(2)] if res is not None else None
    sq = [cx.sb(f"{nm}_sq{p}", [128, NCH * TC], BF16) for p in range(2)]
    xb = None if inplace else [
        [cx.sb(f"{nm}_xb{p}{c}", [128, TC], BF16) for c in range(NCH)]
        for p in range(2)]
    rows = [cx.sb(f"{nm}_rows{p}", [1, 5 * TC]) for p in range(2)]
    t1 = [cx.sb(f"{nm}_t1", [128, TC])] * 2
    yv = None if (inplace or direct or direct_bf) else [
        [cx.sb(f"{nm}_y{p}{c}", [128, TC]) for c in range(NCH)] for p in range(2)]
    yb = [[cx.sb(f"{nm}_yb{p}{c}", [128, TC], BF16) for c in range(NCH)]
          for p in range(2)] if out_bf is not None else None
    ps_s1 = [cx.psum(f"{nm}_ps1{p}", [128, TC]) for p in range(2)]
    ps_s2 = [cx.psum(f"{nm}_ps2{p}", [128, TC]) for p in range(2)]
    ps_a = [cx.psum(f"{nm}_psa{p}", [128, TC]) for p in range(2)]
    ps_b = [cx.psum(f"{nm}_psb{p}", [128, TC]) for p in range(2)]

    nl = NCH * (2 if res is not None else 1)
    ns = (1 if out is not None else 0) + (1 if out_bf is not None else 0)
    marks_ld = {}
    marks_st = {}

    def view(tc, c):
        if inplace:
            return src_sb_full[c][:, tc * TC:(tc + 1) * TC]
        p = tc % 2
        if not shift:
            return xin[p][c][:, :]
        return xin[p][c][:, 1:TC + 1] if c < 4 else xin[p][c][:, 0:TC]

    def ytile(tc, c):
        if inplace:
            return src_sb_full[c][:, tc * TC:(tc + 1) * TC]
        if direct:
            return out_sb[c][:, tc * TC:(tc + 1) * TC]
        return yv[tc % 2][c][:, :]

    if shift:
        with nc.Block() as blk:
            @blk.vector
            def _(v):
                for p in range(2):
                    for c in range(NCH):
                        v.memset(xin[p][c][:, 0:1], 0.0)

    sq0 = cx.val(u + "sqd"); mm0 = cx.val(u + "mmd"); va0 = cx.val(u + "vad")
    rs0 = cx.val(u + "rsd"); rw0 = cx.val(u + "rwd"); bc0 = cx.val(u + "bcd")
    ap0 = cx.val(u + "apd"); cb0 = cx.val(u + "cbd"); yb0 = cx.val(u + "ybd")

    with nc.Block() as blk:
        @blk.sync
        def _(s):
            apm = cx.sem(u + "apd")
            cbm = cx.sem(u + "cbd")
            ybm = cx.sem(u + "ybd")
            for tc in range(NTC):
                p = tc % 2
                t0 = tc * TC
                if tc >= 2:
                    if not inplace or res is not None:
                        s.wait_ge(apm[0], ap0 + NCH * (tc - 1))
                    if not inplace:
                        s.wait_ge(cbm[0], cb0 + tc - 1)
                for c in ([] if inplace else range(NCH)):
                    if shift and t0 > 0:
                        sl = src.ap()[c * 128:(c + 1) * 128, t0 - 1:t0 + TC]
                        dst = xin[p][c][:, 0:TC + 1]
                    elif shift:
                        sl = src.ap()[c * 128:(c + 1) * 128, 0:TC]
                        dst = xin[p][c][:, 1:TC + 1]
                    else:
                        sl = src.ap()[c * 128:(c + 1) * 128, t0:t0 + TC]
                        dst = xin[p][c][:, :]
                    cx.inc(u + "ld", s.dma_start(dst, sl), 16)
                if res is not None:
                    for c in range(NCH):
                        cx.inc(u + "ld", s.dma_start(
                            rin[p][c][:, :], res.ap()[c * 128:(c + 1) * 128, t0:t0 + TC]), 16)
                marks_ld[tc] = cx.val(u + "ld")
                if tc >= 1 and ns:
                    tp = tc - 1
                    for c in range(NCH):
                        if out is not None:
                            s.wait_ge(apm[0], ap0 + NCH * tp + c + 1)
                            cx.inc(u + "st", s.dma_start(
                                out.ap()[c * 128:(c + 1) * 128, tp * TC:(tp + 1) * TC],
                                ytile(tp, c)), 16)
                        if out_bf is not None:
                            if direct_bf:
                                s.wait_ge(apm[0], ap0 + NCH * tp + c + 1)
                            else:
                                s.wait_ge(ybm[0], yb0 + NCH * tp + c + 1)
                            cx.inc(u + "st", s.dma_start(
                                out_bf.ap()[c * 128:(c + 1) * 128, tp * TC:(tp + 1) * TC],
                                yb[tp % 2][c][:, :]), 16)
                marks_st[tc] = cx.val(u + "st")
            tp = NTC - 1
            if ns:
                for c in range(NCH):
                    if out is not None:
                        s.wait_ge(apm[0], ap0 + NCH * tp + c + 1)
                        cx.inc(u + "st", s.dma_start(
                            out.ap()[c * 128:(c + 1) * 128, tp * TC:(tp + 1) * TC],
                            ytile(tp, c)), 16)
                    if out_bf is not None:
                        if direct_bf:
                            s.wait_ge(apm[0], ap0 + NCH * tp + c + 1)
                        else:
                            s.wait_ge(ybm[0], yb0 + NCH * tp + c + 1)
                        cx.inc(u + "st", s.dma_start(
                            out_bf.ap()[c * 128:(c + 1) * 128, tp * TC:(tp + 1) * TC],
                            yb[tp % 2][c][:, :]), 16)
            cx.wait(s, u + "st")

        @blk.vector
        def _(v):
            ldm = cx.sem(u + "ld")
            for tc in range(NTC):
                p = tc % 2
                v.wait_ge(ldm[0], marks_ld[tc])
                if tc >= 2:
                    cx.wait(v, u + "mmd", mm0 + tc - 1)
                for c in range(NCH):
                    ins = v.tensor_mul(sq[p][:, c * TC:(c + 1) * TC], view(tc, c),
                                       view(tc, c))
                cx.inc(u + "sqd", ins)
                s1 = rows[p][:, 0:TC]; s2 = rows[p][:, TC:2 * TC]
                mu = rows[p][:, 2 * TC:3 * TC]; va = rows[p][:, 3 * TC:4 * TC]
                cx.wait(v, u + "mmd", mm0 + tc + 1)
                v.tensor_copy(s1[:, :], ps_s1[p][0:1, :])
                v.tensor_copy(s2[:, :], ps_s2[p][0:1, :])
                v.tensor_scalar_mul(mu[:, :], s1[:, :], 1.0 / D)
                v.tensor_mul(t1[p][0:1, 0:TC], mu[:, :], mu[:, :])
                ins = v.scalar_tensor_tensor(va[:, :], s2[:, :], 1.0 / D,
                                             t1[p][0:1, 0:TC], ALU.mult, ALU.subtract)
                cx.inc(u + "vad", ins)
                ar = rows[p][:, 4 * TC:5 * TC]; br = rows[p][:, TC:2 * TC]
                cx.wait(v, u + "rsd", rs0 + tc + 1)
                ins = v.scalar_tensor_tensor(br[:, :], mu[:, :], -1.0, ar[:, :],
                                             ALU.mult, ALU.mult)
                cx.inc(u + "rwd", ins)
                cx.wait(v, u + "bcd", bc0 + tc + 1)
                stm = cx.sem(u + "st")
                if tc >= 2 and not inplace and not direct and not direct_bf \
                        and (out_bf is not None or out_sb is not None):
                    cx.wait(v, u + "ybd", yb0 + NCH * (tc - 1))
                for c in range(NCH):
                    if tc >= 2 and not inplace:
                        v.wait_ge(stm[0], marks_st[tc - 1])
                    if direct_bf:
                        yt = yb[p][c][:, :]
                        v.tensor_mul(t1[p][:, :], view(tc, c), ps_a[p][:, :])
                        if gamma is not None:
                            v.tensor_add(t1[p][:, :], t1[p][:, :], ps_b[p][:, :])
                            ins = v.tensor_scalar(yt, t1[p][:, :],
                                                  gamma[:, c:c + 1],
                                                  beta[:, c:c + 1],
                                                  ALU.mult, ALU.add)
                        else:
                            ins = v.tensor_add(yt, t1[p][:, :], ps_b[p][:, :])
                        cx.inc(u + "apd", ins)
                        continue
                    yt = ytile(tc, c)
                    v.tensor_mul(t1[p][:, :], view(tc, c), ps_a[p][:, :])
                    ins = v.tensor_add(yt, t1[p][:, :], ps_b[p][:, :])
                    if gamma is not None:
                        ins = v.tensor_scalar(yt, yt,
                                              gamma[:, c:c + 1], beta[:, c:c + 1],
                                              ALU.mult, ALU.add)
                    if res is not None:
                        ins = v.tensor_add(yt, yt, rin[p][c][:, :])
                    cx.inc(u + "apd", ins)

        @blk.gpsimd
        def _(g):
            ldm = cx.sem(u + "ld")
            mmm = cx.sem(u + "mmd")
            for tc in range(0 if inplace else NTC):
                p = tc % 2
                g.wait_ge(ldm[0], marks_ld[tc])
                if tc >= 2:
                    g.wait_ge(mmm[0], mm0 + tc - 1)
                for c in range(NCH):
                    ins = g.tensor_copy(xb[p][c][:, :], view(tc, c))
                cx.inc(u + "cbd", ins)

        @blk.tensor
        def _(tn):
            for tc in range(NTC):
                p = tc % 2
                cx.wait(tn, u + "sqd", sq0 + tc + 1)
                if not inplace:
                    cx.wait(tn, u + "cbd", cb0 + tc + 1)
                if tc >= 2:
                    cx.wait(tn, u + "vad", va0 + tc - 1)
                for c in range(NCH):
                    if inplace:
                        tn.matmul(ps_s1[p][0:1, :], cx.ones_col[:, 0:1],
                                  view(tc, c),
                                  start=(c == 0), stop=(c == NCH - 1))
                    else:
                        tn.matmul(ps_s1[p][0:1, :], cx.ones_col_bf[:, 0:1],
                                  xb[p][c][:, :],
                                  start=(c == 0), stop=(c == NCH - 1))
                for c in range(NCH):
                    ins = tn.matmul(ps_s2[p][0:1, :], cx.ones_col_bf[:, 0:1],
                                    sq[p][:, c * TC:(c + 1) * TC],
                                    start=(c == 0), stop=(c == NCH - 1))
                cx.inc(u + "mmd", ins)
                cx.wait(tn, u + "rwd", rw0 + tc + 1)
                if tc >= 2:
                    cx.wait(tn, u + "apd", ap0 + NCH * (tc - 1))
                ar = rows[p][:, 4 * TC:5 * TC]; br = rows[p][:, TC:2 * TC]
                tn.matmul(ps_a[p][:, :], cx.ones_row[0:1, :], ar[:, :],
                          start=True, stop=True, skip_group_check=True)
                ins = tn.matmul(ps_b[p][:, :], cx.ones_row[0:1, :], br[:, :],
                                start=True, stop=True, skip_group_check=True)
                cx.inc(u + "bcd", ins)

        @blk.scalar
        def _(s):
            apm = cx.sem(u + "apd")
            stm = cx.sem(u + "st")
            for tc in range(NTC):
                p = tc % 2
                va = rows[p][:, 3 * TC:4 * TC]; ar = rows[p][:, 4 * TC:5 * TC]
                cx.wait(s, u + "vad", va0 + tc + 1)
                s.activation(ar[:, :], va[:, :], AF.Ln, bias=cx.eps_t[0:1, 0:1])
                ins = s.activation(ar[:, :], ar[:, :], AF.Exp, scale=-0.5)
                cx.inc(u + "rsd", ins)
                if (out_bf is not None or out_sb is not None) and not direct \
                        and not direct_bf:
                    if out_bf is not None and tc >= 2:
                        s.wait_ge(stm[0], marks_st[tc - 1])
                    for c in range(NCH):
                        s.wait_ge(apm[0], ap0 + NCH * tc + c + 1)
                        dst = (yb[p][c][:, :] if out_bf is not None else
                               out_sb[c][:, tc * TC:(tc + 1) * TC])
                        ins = s.activation(dst, ytile(tc, c), AF.Copy)
                        cx.inc(u + "ybd", ins)



def emit_l1ffn(cx, xin, wr, wk, wv, mur, muk, xout, lnx, ysc):
    nc = cx.nc
    with cx.scope():
        wpre = _l1ffn_loads(cx, wr, wk, wv, mur, muk)
        ln_phase(cx, "ln", xin, out_bf=lnx, names="s1")
        with cx.scope():
            _l1ffn_core(cx, wpre, lnx, ysc)
    ln_phase(cx, "ln", ysc, out=xout, res=xin, names="s3")


def _l1ffn_loads(cx, wr, wk, wv, mur, muk):
    w_r = load_weight_bf(cx, "w_r", wr, D, D, defer=True)[0]
    w_k = load_weight_bf(cx, "w_k", wk, D, D, defer=True)[0]
    w_v, wm = load_weight_bf(cx, "w_v", wv, D, D, defer=True)
    mur_t = load_pp(cx, "mur_t", mur)
    muk_t = load_pp(cx, "muk_t", muk)
    return w_r, w_k, w_v, wm, mur_t, muk_t


def _l1ffn_core(cx, wpre, lnx, ysc):
    nc = cx.nc
    w_r, w_k, w_v, wm, mur_t, muk_t = wpre
    dma_drain(cx, "wldg", wm)

    xt = [[cx.sb(f"s2_x{p}{c}", [128, TC + 1], BF16) for c in range(NCH)]
          for p in range(2)]
    xr = [cx.sb(f"s2_xr{c}", [128, TC], BF16) for c in range(NCH)]
    xk = [cx.sb(f"s2_xk{c}", [128, TC], BF16) for c in range(NCH)]
    rt = [cx.sb(f"s2_r{c}", [128, TC]) for c in range(NCH)]
    ka = [cx.sb(f"s2_ka{c}", [128, TC], BF16) for c in range(NCH)]
    tmp = cx.sb("s2_tmp", [128, TC])
    tmp2 = cx.sb("s2_tmp2", [128, TC])
    yo = [cx.sb(f"s2_y{c}", [128, TC]) for c in range(NCH)]
    pr = [cx.psum(f"s2_p{i}", [128, TC]) for i in range(4)]
    marks_ld = {}

    def emit_loads(s, tc):
        p = tc % 2
        t0 = tc * TC
        for c in range(NCH):
            if t0 == 0:
                cx.inc("s2ld", s.dma_start(
                    xt[p][c][:, 1:TC + 1], lnx.ap()[c * 128:(c + 1) * 128, 0:TC]), 16)
            else:
                cx.inc("s2ld", s.dma_start(
                    xt[p][c][:, 0:TC + 1], lnx.ap()[c * 128:(c + 1) * 128, t0 - 1:t0 + TC]), 16)
        marks_ld[tc] = cx.val("s2ld")

    with nc.Block() as blk:
        @blk.vector
        def _(v):
            for p in range(2):
                for c in range(NCH):
                    v.memset(xt[p][c][:, 0:1], 0.0)

    for tc in range(NTC):
        p = tc % 2
        t0 = tc * TC
        with nc.Block() as blk:
            @blk.sync
            def _(s):
                if tc == 0:
                    emit_loads(s, 0)
                    emit_loads(s, 1)
                elif tc + 1 < NTC:
                    emit_loads(s, tc + 1)
                cx.wait(s, "s2st")

            @blk.vector
            def _(v):
                v.wait_ge(cx.sem("s2ld")[0], marks_ld[tc])
                for c in range(NCH):
                    cur = xt[p][c][:, 1:TC + 1]
                    prev = xt[p][c][:, 0:TC]
                    v.tensor_sub(tmp[:, :], cur, prev)
                    v.scalar_tensor_tensor(xr[c][:, :], tmp[:, :], mur_t[:, c:c + 1],
                                           prev, ALU.mult, ALU.add)
                    v.tensor_sub(tmp2[:, :], cur, prev)
                    ins = v.scalar_tensor_tensor(xk[c][:, :], tmp2[:, :], muk_t[:, c:c + 1],
                                                 prev, ALU.mult, ALU.add)
                cx.inc("s2mix", ins)

            sig0 = cx.val("s2sig")
            ka0 = cx.val("s2ka")

            @blk.tensor
            def _(tn):
                cx.wait(tn, "s2mix")
                for m in range(NCH):
                    if m >= 2:
                        cx.wait(tn, "s2sig", sig0 + m - 1)
                        cx.wait(tn, "s2ka", ka0 + m - 1)
                    for c in range(NCH):
                        tn.matmul(pr[m % 2][:, :], w_r[:, c * D + m * 128: c * D + m * 128 + 128],
                                  xr[c][:, :], start=(c == 0), stop=(c == NCH - 1))
                    for c in range(NCH):
                        ins = tn.matmul(pr[2 + m % 2][:, :], w_k[:, c * D + m * 128: c * D + m * 128 + 128],
                                        xk[c][:, :], start=(c == 0), stop=(c == NCH - 1))
                    cx.inc("s2mm", ins)

            @blk.scalar
            def _(s):
                sm = cx.sem("s2mm")
                for m in range(NCH):
                    s.wait_ge(sm[0], sm[1] - (NCH - 1 - m))
                    ins = s.activation(rt[m][:, :], pr[m % 2][:, :], AF.Sigmoid)
                    cx.inc("s2sig", ins)

            @blk.vector
            def _(v):
                sm = cx.sem("s2mm")
                for m in range(NCH):
                    v.wait_ge(sm[0], sm[1] - (NCH - 1 - m))
                    v.tensor_scalar_max(tmp[:, :], pr[2 + m % 2][:, :], 0.0)
                    ins = v.tensor_mul(ka[m][:, :], tmp[:, :], tmp[:, :])
                    cx.inc("s2ka", ins)

            out0 = cx.val("s2out")

            @blk.tensor
            def _(tn):
                cx.wait(tn, "s2ka")
                cx.wait(tn, "s2sig")
                for m in range(NCH):
                    if m >= 2:
                        cx.wait(tn, "s2out", out0 + m - 1)
                    for c in range(NCH):
                        ins = tn.matmul(pr[m % 2][:, :], w_v[:, c * D + m * 128: c * D + m * 128 + 128],
                                        ka[c][:, :], start=(c == 0), stop=(c == NCH - 1))
                    cx.inc("s2mv", ins)

            @blk.vector
            def _(v):
                sm = cx.sem("s2mv")
                for m in range(NCH):
                    v.wait_ge(sm[0], sm[1] - (NCH - 1 - m))
                    ins = v.tensor_mul(yo[m][:, :], pr[m % 2][:, :], rt[m][:, :])
                    cx.inc("s2out", ins)

            @blk.sync
            def _(s):
                sm = cx.sem("s2out")
                for m in range(NCH):
                    s.wait_ge(sm[0], sm[1] - (NCH - 1 - m))
                    cx.inc("s2st", s.dma_start(
                        ysc.ap()[m * 128:(m + 1) * 128, t0:t0 + TC], yo[m][:, :]), 16)

    with nc.Block() as blk:
        @blk.sync
        def _(s):
            cx.wait(s, "s2st")

def build_l1ffn(fuse=False):
    nc = bass.Bass("TRN2", target_bir_lowering=False)
    xin = nc.dram_tensor("xin", [D, T], F32, kind="ExternalInput")
    wr = nc.dram_tensor("wr", [D, D], BF16, kind="ExternalInput")
    wk = nc.dram_tensor("wk", [D, D], BF16, kind="ExternalInput")
    wv = nc.dram_tensor("wv", [D, D], BF16, kind="ExternalInput")
    mur = nc.dram_tensor("mur", [D], F32, kind="ExternalInput")
    muk = nc.dram_tensor("muk", [D], F32, kind="ExternalInput")
    xout = nc.dram_tensor("xout", [D, T], F32, kind="ExternalOutput")
    lnx = nc.dram_tensor("lnx_scr", [D, T], BF16, kind="Internal")
    ysc = nc.dram_tensor("y_scr", [D, T], F32, kind="Internal")
    with ExitStack() as es:
        cx = Ctx(nc, es, fuse=fuse)
        emit_consts(cx)
        emit_l1ffn(cx, xin, wr, wk, wv, mur, muk, xout, lnx, ysc)
    return nc



def emit_l1attn(cx, xin, w1t, b1, w2c, b2, xout, lnx, csc):
    nc = cx.nc
    with cx.scope():
        lnx_sb = [cx.sb(f"r1sb{c}", [128, T], BF16) for c in range(NCH)]
        w_1 = load_weight_bf(cx, "w_1", w1t, D, D, defer=True)[0]
        w_2, wm = load_weight_bf(cx, "w_2", w2c, D, 1, defer=True)
        b1t = load_pp(cx, "b1t", b1)
        ln_phase(cx, "ln", xin, out_sb=lnx_sb, names="r1")
        _l1attn_core(cx, (w_1, w_2, wm, b1t), b2, lnx_sb, csc)
    ln_phase(cx, "ln", csc, out=xout, res=xin, names="r4")


def _l1attn_core(cx, wpre, b2, lnx_sb, csc):
    nc = cx.nc
    w_1, w_2, wm, b1t = wpre
    dma_drain(cx, "wldg", wm)
    b2t = cx.sb("b2t", [1, 1], F32)
    erow = cx.sb("erow", [1, T])
    mrow = cx.sb("mrow", [1, 1])
    drow = cx.sb("drow", [1, T])
    zrow = cx.sb("zrow", [1, T])
    with nc.Block() as blk:
        @blk.sync
        def _(s):
            cx.inc("c0", s.dma_start(b2t[:, :], b2.ap()), 16)
            cx.wait(s, "c0")
        @blk.vector
        def _(v):
            v.memset(zrow[:, :], 0.0)

    hb = [cx.sb(f"r2_h{c}", [128, TC], BF16) for c in range(NCH)]
    ph = [cx.psum(f"r2_p{i}", [128, TC]) for i in range(2)]
    psr = cx.psum("r2_ps", [1, TC])
    for tc in range(NTC):
        t0 = tc * TC
        with nc.Block() as blk:
            h0 = cx.val("r2h")
            e0 = cx.val("r2e")

            @blk.tensor
            def _(tn, t0=t0):
                cx.wait(tn, "r2e", e0)
                for m in range(NCH):
                    if m >= 2:
                        cx.wait(tn, "r2h", h0 + m - 1)
                    for c in range(NCH):
                        ins = tn.matmul(ph[m % 2][:, :], w_1[:, c * D + m * 128: c * D + m * 128 + 128],
                                        lnx_sb[c][:, t0:t0 + TC],
                                        start=(c == 0), stop=(c == NCH - 1))
                    cx.inc("r2mm", ins)

            @blk.scalar
            def _(s):
                sm = cx.sem("r2mm")
                for m in range(NCH):
                    s.wait_ge(sm[0], sm[1] - (NCH - 1 - m))
                    ins = s.activation(hb[m][:, :], ph[m % 2][:, :], AF.Tanh,
                                       bias=b1t[:, m:m + 1])
                    cx.inc("r2h", ins)

            @blk.tensor
            def _(tn):
                sm = cx.sem("r2h")
                for m in range(NCH):
                    tn.wait_ge(sm[0], sm[1] - (NCH - 1 - m))
                    ins = tn.matmul(psr[0:1, :], w_2[:, m:m + 1], hb[m][:, :],
                                    start=(m == 0), stop=(m == NCH - 1))
                cx.inc("r2sd", ins)

            @blk.vector
            def _(v):
                cx.wait(v, "r2sd")
                ins = v.tensor_scalar_add(erow[:, t0:t0 + TC], psr[0:1, :], b2t[0:1, 0:1])
                cx.inc("r2e", ins)

    with nc.Block() as blk:
        @blk.vector
        def _(v):
            cx.wait(v, "r2e")
            v.reduce_max(mrow[:, 0:1], erow[:, :], axis=AX.X)
            ins = v.tensor_scalar_mul(mrow[:, 0:1], mrow[:, 0:1], -1.0)
            cx.inc("r2m", ins)

    with nc.Block() as blk:
        @blk.scalar
        def _(s):
            cx.wait(s, "r2m")
            ins = s.activation(erow[:, :], erow[:, :], AF.Exp, bias=mrow[0:1, 0:1])
            cx.inc("r2ed", ins)

    with nc.Block() as blk:
        @blk.vector
        def _(v):
            cx.wait(v, "r2ed")
            ins = v.tensor_tensor_scan(drow[:, :], erow[:, :], zrow[:, :], 0.0,
                                       ALU.add, ALU.add)
            cx.inc("r2sc", ins)

        @blk.vector
        def _(v):
            cx.wait(v, "r2sc")
            v.tensor_scalar_max(drow[:, :], drow[:, :], 1e-35)
            v.reciprocal(drow[:, :], drow[:, :])

    wrow = cx.sb("r3_w", [128, T])
    nrow = cx.sb("r3_n", [128, T])
    zbig = cx.sb("r3_z", [128, T])
    crow = cx.sb("r3_c", [128, T])
    pse = [cx.psum(f"r3_pe{i}", [128, TC]) for i in range(2)]
    psd = [cx.psum(f"r3_pd{i}", [128, TC]) for i in range(2)]
    with nc.Block() as blk:
        @blk.vector
        def _(v):
            v.memset(zbig[:, :], 0.0)
    for c in range(NCH):
        with nc.Block() as blk:
            w0 = cx.val("r3w")
            c0 = cx.val("r3c")

            @blk.tensor
            def _(tn):
                for tc in range(NTC):
                    if tc >= 2:
                        cx.wait(tn, "r3w", w0 + tc - 1)
                    ins = tn.matmul(pse[tc % 2][:, :], cx.ones_row[0:1, :],
                                    erow[0:1, tc * TC:(tc + 1) * TC], start=True, stop=True)
                    cx.inc("r3bc", ins)

            @blk.vector
            def _(v, c=c):
                sm = cx.sem("r3bc")
                for tc in range(NTC):
                    v.wait_ge(sm[0], sm[1] - (NTC - 1 - tc))
                    ins = v.tensor_mul(wrow[:, tc * TC:(tc + 1) * TC],
                                       lnx_sb[c][:, tc * TC:(tc + 1) * TC], pse[tc % 2][:, :])
                    cx.inc("r3w", ins)
                v.tensor_tensor_scan(nrow[:, :], wrow[:, :], zbig[:, :], 0.0, ALU.add, ALU.add)

            @blk.tensor
            def _(tn):
                for tc in range(NTC):
                    if tc >= 2:
                        cx.wait(tn, "r3c", c0 + tc - 1)
                    ins = tn.matmul(psd[tc % 2][:, :], cx.ones_row[0:1, :],
                                    drow[0:1, tc * TC:(tc + 1) * TC], start=True, stop=True)
                    cx.inc("r3bd", ins)

            @blk.vector
            def _(v):
                sm = cx.sem("r3bd")
                for tc in range(NTC):
                    v.wait_ge(sm[0], sm[1] - (NTC - 1 - tc))
                    ins = v.tensor_mul(crow[:, tc * TC:(tc + 1) * TC],
                                       nrow[:, tc * TC:(tc + 1) * TC], psd[tc % 2][:, :])
                    cx.inc("r3c", ins)

            @blk.sync
            def _(s):
                cx.wait(s, "r3c")
                cx.inc("r3st", s.dma_start(csc.ap()[c * 128:(c + 1) * 128, :], crow[:, :]), 16)
                cx.wait(s, "r3st")


def build_l1attn(fuse=False):
    nc = bass.Bass("TRN2", target_bir_lowering=False)
    xin = nc.dram_tensor("xin", [D, T], F32, kind="ExternalInput")
    w1t = nc.dram_tensor("w1t", [D, D], BF16, kind="ExternalInput")
    b1 = nc.dram_tensor("b1", [D], F32, kind="ExternalInput")
    w2c = nc.dram_tensor("w2c", [D, 1], F32, kind="ExternalInput")
    b2 = nc.dram_tensor("b2", [1, 1], F32, kind="ExternalInput")
    xout = nc.dram_tensor("xout", [D, T], F32, kind="ExternalOutput")
    lnx = nc.dram_tensor("lnx_scr", [D, T], BF16, kind="Internal")
    csc = nc.dram_tensor("c_scr", [D, T], F32, kind="Internal")
    with ExitStack() as es:
        cx = Ctx(nc, es, fuse=fuse)
        emit_consts(cx)
        emit_l1attn(cx, xin, w1t, b1, w2c, b2, xout, lnx, csc)
    return nc



def _proj_block(cx, tn_eng, w_sb, rhs_tiles, ps, m, d_out=D, sem=None):
    for c in range(NCH):
        ins = tn_eng.matmul(ps, w_sb[:, c * d_out + m * 128: c * d_out + m * 128 + 128],
                            rhs_tiles[c], start=(c == 0), stop=(c == NCH - 1))
    if sem is not None:
        cx.inc(sem, ins)


def a2_loads(cx, wq, wk, bq):
    w_q = load_weight_bf(cx, "a2wq", wq, D, D, defer=True)[0]
    w_k, wm = load_weight_bf(cx, "a2wk", wk, D, D, defer=True)
    bq_t = load_pp(cx, "a2bq", bq)
    return w_q, w_k, wm, bq_t


def emit_l0attn_proj(cx, lnx_sb, a2w, wv, qsc, ksc, vsc, qn, kn):
    nc = cx.nc
    w_q, w_k, wm, bq_t = a2w
    dma_drain(cx, "wldg", wm)
    w_v, wvm = load_weight_bf(cx, "a2wv", wv, D, D, defer=True)
    hmask = cx.sb("a2hm", [128, 128], BF16)
    qb = [cx.sb(f"a2_q{m}", [128, TC], BF16) for m in range(NCH)]
    kb = [cx.sb(f"a2_k{m}", [128, TC], BF16) for m in range(NCH)]
    sqt = cx.sb("a2_sq", [128, TC], BF16)
    nmx = cx.sb("a2_nm", [128, 1])
    v_sb = [cx.sb(f"a2_v{i}", [128, 1040], BF16) for i in range(4)]
    ps_q = [cx.psum(f"a2_pq{i}", [128, TC]) for i in range(2)]
    ps_n = [cx.psum(f"a2_pn{i}", [128, TC]) for i in range(2)]
    ps_v = [cx.psum(f"a2_pv{i}", [128, 512]) for i in range(4)]
    vst_marks = []
    qk_marks = []

    with nc.Block() as blk:
        @blk.vector
        def _(v):
            v.memset(hmask[:, :], 0.0)
            v.memset(hmask[0:64, 0:1], 1.0)
            v.memset(hmask[64:128, 64:65], 1.0)
            for i in range(4):
                v.memset(v_sb[i][:, :].rearrange("p (h w) -> p h w", w=65)[:, :, 64:65], 1.0)
            v.memset(qn[:, :], 0.0)
            v.memset(kn[:, :], 0.0)

    for tc in range(NTC):
        t0 = tc * TC
        with nc.Block() as blk:
            pre = [lnx_sb[c][:, t0:t0 + TC] for c in range(NCH)]
            for which in ("q", "k"):
                w_sb = w_q if which == "q" else w_k
                dst = qb if which == "q" else kb
                sc_bias = bq_t if which == "q" else None
                nrun = qn if which == "q" else kn
                ev = f"a2{which}ev"
                mm = f"a2{which}mm"
                sq = f"a2{which}sq"
                nmm = f"a2{which}nmm"
                nd = f"a2{which}nd"
                ev0 = cx.val(ev)
                nmm0 = cx.val(nmm)

                @blk.tensor
                def _(tn, w_sb=w_sb, ev=ev, ev0=ev0, which=which, mm=mm):
                    if which == "k":
                        cx.wait(tn, "a2qev")
                    else:
                        cx.wait(tn, "a2kev")
                    for m in range(NCH):
                        if m >= 2:
                            cx.wait(tn, ev, ev0 + m - 1)
                        _proj_block(cx, tn, w_sb, pre, ps_q[m % 2][:, :], m, sem=mm)

                @blk.vector
                def _(v, dst=dst, sc_bias=sc_bias, ev=ev, mm=mm, tc=tc):
                    sm = cx.sem(mm)
                    if tc >= 1:
                        v.wait_ge(cx.sem("a2st")[0], qk_marks[tc - 1])
                    for m in range(NCH):
                        v.wait_ge(sm[0], sm[1] - (NCH - 1 - m))
                        if sc_bias is not None:
                            ins = v.tensor_scalar(dst[m][:, :], ps_q[m % 2][:, :],
                                                  sc_bias[:, m:m + 1], 0.125,
                                                  ALU.add, ALU.mult)
                        else:
                            ins = v.tensor_copy(dst[m][:, :], ps_q[m % 2][:, :])
                        cx.inc(ev, ins)

                @blk.scalar
                def _(s, dst=dst, ev=ev, sq=sq, nmm=nmm, nmm0=nmm0, which=which):
                    sm = cx.sem(ev)
                    base = sm[1] - NCH
                    if which == "k":
                        cx.wait(s, "a2qnmm")
                    for m in range(NCH):
                        s.wait_ge(sm[0], base + m + 1)
                        if m >= 1:
                            cx.wait(s, nmm, nmm0 + m)
                        ins = s.activation(sqt[:, :], dst[m][:, :], AF.Square)
                        cx.inc(sq, ins)

                nd0 = cx.val(nd)

                @blk.tensor
                def _(tn, sq=sq, nmm=nmm, nd=nd, nd0=nd0):
                    sm = cx.sem(sq)
                    base = sm[1] - NCH
                    for m in range(NCH):
                        tn.wait_ge(sm[0], base + m + 1)
                        if m >= 2:
                            cx.wait(tn, nd, nd0 + m - 1)
                        ins = tn.matmul(ps_n[m % 2][:, :], hmask[:, :], sqt[:, :],
                                        start=True, stop=True)
                        cx.inc(nmm, ins)

                @blk.vector
                def _(v, nrun=nrun, nmm=nmm, nd=nd):
                    sm = cx.sem(nmm)
                    base = sm[1] - NCH
                    for m in range(NCH):
                        v.wait_ge(sm[0], base + m + 1)
                        v.reduce_max(nmx[:, 0:1], ps_n[m % 2][:, :], axis=AX.X)
                        ins = v.tensor_max(nrun[:, m:m + 1], nrun[:, m:m + 1], nmx[:, 0:1])
                        cx.inc(nd, ins)


            vev0 = cx.val("a2vev")

            @blk.tensor
            def _(tn, vev0=vev0, t0=t0, tc=tc):
                if tc == 0:
                    cx.wait(tn, "wldg", wvm)
                for tt in range(4):
                    if tt >= 2:
                        cx.wait(tn, "a2vev", vev0 + tt - 1)
                    for half in range(2):
                        for c in range(NCH):
                            ins = tn.matmul(ps_v[2 * (tt % 2) + half][:, :],
                                            lnx_sb[c][:, t0 + tt * 128: t0 + tt * 128 + 128],
                                            w_v[:, c * D + half * 512: c * D + half * 512 + 512],
                                            start=(c == 0), stop=(c == NCH - 1))
                    cx.inc("a2vmm", ins)

            @blk.vector
            def _(v, tc=tc):
                sm = cx.sem("a2vmm")
                for tt in range(4):
                    v.wait_ge(sm[0], sm[1] - (4 - 1 - tt))
                    if tc >= 1:
                        v.wait_ge(cx.sem("a2vst")[0], vst_marks[(tc - 1) * 4 + tt])
                    for half in range(2):
                        ins = v.tensor_copy(
                            v_sb[tt][:, half * 520:(half + 1) * 520]
                            .rearrange("p (h w) -> p h w", w=65)[:, :, 0:64],
                            ps_v[2 * (tt % 2) + half][:, :].rearrange("p (h w) -> p h w", w=64))
                    cx.inc("a2vev", ins)

            @blk.sync
            def _(s):
                sm = cx.sem("a2qev")
                for m in range(NCH):
                    s.wait_ge(sm[0], sm[1] - NCH + m + 1)
                    cx.inc("a2st", s.dma_start(
                        qsc.ap()[m * 128:(m + 1) * 128, t0:t0 + TC], qb[m][:, :]), 16)
                sm2 = cx.sem("a2kev")
                for m in range(NCH):
                    s.wait_ge(sm2[0], sm2[1] - NCH + m + 1)
                    cx.inc("a2st", s.dma_start(
                        ksc.ap()[m * 128:(m + 1) * 128, t0:t0 + TC], kb[m][:, :]), 16)
                qk_marks.append(cx.val("a2st"))
                sm3 = cx.sem("a2vev")
                for tt in range(4):
                    s.wait_ge(sm3[0], sm3[1] - (4 - 1 - tt))
                    cx.inc("a2vst", s.dma_start(vsc.ap()[tc * 4 + tt], v_sb[tt][:, :]), 16)
                    vst_marks.append(cx.val("a2vst"))

    with nc.Block() as blk:
        @blk.sync
        def _(s):
            cx.wait(s, "a2st")
            cx.wait(s, "a2vst")




def emit_l0attn_flash(cx, qsc, ksc, vsc, osc, qn, kn, albq, alck, mskc):
    nc = cx.nc
    NET = 8
    qh = [[cx.sb(f"a3_qh{b}{i}", [70, T], BF16) for i in range(2)]
          for b in range(2)]
    kh = [[cx.sb(f"a3_kh{b}{i}", [70, T], BF16) for i in range(2)]
          for b in range(2)]
    vp = [cx.sb(f"a3_v{b}", [128, 32 * 130], BF16) for b in range(2)]
    oT = cx.sb("a3_o", [128, T], BF16)
    et = [cx.sb(f"a3_e{i}", [128, TC], BF16) for i in range(NET)]
    mrow = cx.sb("a3_mr", [128, 2])
    mneg = cx.sb("a3_mn", [128, 2])
    drow = [cx.sb(f"a3_dr{i}", [65, TC]) for i in range(2)]
    one_tc = cx.sb("a3_one", [1, TC])
    bc_sb = cx.sb("a3_bc", [128, TC])
    msk = cx.sb("a3_msk", [128, 640])
    ps_s = [cx.psum(f"a3_ps{i}", [128, TC]) for i in range(3)]
    po = [[cx.psum(f"a3_po{hl}{p}", [128, TC]) for p in range(2)] for hl in range(2)]
    ps_x = [cx.psum("a3_px0", [128, TC])]

    with nc.Block() as blk:
        @blk.sync
        def _(s):
            cx.inc("a3mld", s.dma_start(msk[:, :], mskc.ap()), 16)
            cx.wait(s, "a3mld")

        @blk.vector
        def _(v):
            v.memset(one_tc[:, :], 1.0)
            v.memset(drow[0][:, :], 1.0)
            v.memset(drow[1][:, :], 1.0)
            for b in range(2):
                for i in range(2):
                    v.memset(kh[b][i][64:70, :], 1.0)
                    v.memset(qh[b][i][64:70, :], 1.0)

    ld_marks = {}
    g_marks = {}

    def a3_loads(s, pn):
        b = pn % 2
        for hl in range(2):
            r0 = pn * 128 + hl * 64
            for m in range(NCH):
                cx.inc("a3ld", s.dma_start(
                    qh[b][hl][0:64, m * TC:(m + 1) * TC],
                    qsc.ap()[r0:r0 + 64, m * TC:(m + 1) * TC]), 16)
                cx.inc("a3ld", s.dma_start(
                    kh[b][hl][0:64, m * TC:(m + 1) * TC],
                    ksc.ap()[r0:r0 + 64, m * TC:(m + 1) * TC]), 16)
        for jt in range(32):
            cx.inc("a3ld", s.dma_start(
                vp[b][:, jt * 130:(jt + 1) * 130],
                vsc.ap()[jt, :, pn * 130:(pn + 1) * 130]), 16)
        ld_marks[pn] = cx.val("a3ld")

    def a3_gloads(g, pn):
        b = pn % 2
        for hl in range(2):
            h = 2 * pn + hl
            cx.inc("wldg", g.dma_start(qh[b][hl][67:70, :],
                                       albq.ap()[3 * h:3 * h + 3, :]), 16)
            cx.inc("wldg", g.dma_start(kh[b][hl][64:67, :],
                                       alck.ap()[3 * h:3 * h + 3, :]), 16)
        g_marks[pn] = cx.val("wldg")

    for p in range(8):
        with nc.Block() as blk:
            @blk.sync
            def _(s):
                cx.wait(s, "a3ost")
                if p == 0:
                    a3_loads(s, 0)
                s.wait_ge(cx.sem("a3ld")[0], ld_marks[p])

            @blk.gpsimd
            def _(g):
                if p == 0:
                    a3_gloads(g, 0)
                g.wait_ge(cx.sem("wldg")[0], g_marks[p])

            @blk.vector
            def _(v):
                cx.wait(v, "a3ld")
                ins = v.tensor_mul(mrow[:, 0:1], qn[:, p:p + 1], kn[:, p:p + 1])
                cx.inc("a3m1", ins)

            @blk.scalar
            def _(s):
                cx.wait(s, "a3m1")
                s.activation(mrow[:, 1:2], mrow[:, 0:1], AF.Ln, bias=cx.tiny_t[:, 0:1])
                ins = s.activation(mrow[:, 0:1], mrow[:, 1:2], AF.Exp, scale=0.5)
                cx.inc("a3m2", ins)

            @blk.vector
            def _(v):
                cx.wait(v, "a3m2")
                ins = v.tensor_scalar_mul(mrow[:, 0:1], mrow[:, 0:1], -1.03)
                cx.inc("a3m3", ins)

            @blk.tensor
            def _(tn):
                cx.wait(tn, "a3m3")
                cx.wait(tn, "a3ev")
                tn.matmul(ps_x[0][:, 0:1], cx.ones_row[0:1, :], mrow[0:1, 0:1],
                          start=True, stop=True, skip_group_check=True)
                ins = tn.matmul(ps_x[0][:, 1:2], cx.ones_row[64:65, :], mrow[64:65, 0:1],
                                start=True, stop=True, skip_group_check=True)
                cx.inc("a3m4", ins)

            @blk.vector
            def _(v):
                cx.wait(v, "a3m4")
                ins = v.tensor_copy(mneg[:, 0:2], ps_x[0][:, 0:2])
                cx.inc("a3m5", ins)

        NIG = T // TC
        items = []
        evseq = {}
        pvlast = {}
        nev = 0
        for ig in range(NIG):
            jhi = (ig * TC + TC - 1) // 128
            jlos = [ALIBI[2 * p + hl]["jlo"][ig] for hl in range(2)]
            for jt in range(min(jlos), jhi + 1):
                for hl in range(2):
                    if jt >= jlos[hl]:
                        items.append((ig, hl, jt))
            for hl in range(2):
                evseq[(ig, hl)] = nev + 1
                nev += 1
        firsts = set()
        seen = set()
        for k, (ig, hl, jt) in enumerate(items):
            if (ig, hl) not in seen:
                firsts.add(k)
                seen.add((ig, hl))
            pvlast[(ig, hl)] = k
        n = len(items)

        def mask_ops(k):
            ig, hl, jt = items[k]
            d0 = jt * 128 - ig * TC
            ops = []
            if d0 > 0:
                ops.append(("neg", 0, min(d0, TC)))
            if -127 <= d0 < TC:
                a = max(0, d0)
                b = min(TC, d0 + 128)
                ops.append(("mc", a, b, a - d0))
            return ops

        nmask = [0]
        for k in range(n):
            nmask.append(nmask[-1] + len(mask_ops(k)))

        gfirst = {}
        glast = {}
        for k, (ig, hl, jt) in enumerate(items):
            if ig not in gfirst:
                gfirst[ig] = k
            glast[ig] = k
        BDELAY = 6
        bc_due = {}
        for ig in range(NIG):
            cap = n - 1
            if ig + 2 < NIG:
                cap = min(cap, gfirst[ig + 2] - 1)
            dk = max(min(glast[ig] + BDELAY, cap), glast[ig])
            bc_due.setdefault(dk, []).append(ig)

        s0 = cx.val("a3s"); e0 = cx.val("a3e"); v0 = cx.val("a3mk")
        p0 = cx.val("a3p"); rc0 = cx.val("a3rc"); bc0 = cx.val("a3bcd")
        ev0 = cx.val("a3ev")

        with nc.Block() as blk:
            @blk.sync
            def _(s):
                if p + 1 < 8:
                    a3_loads(s, p + 1)

            @blk.gpsimd
            def _(g):
                if p + 1 < 8:
                    a3_gloads(g, p + 1)

            @blk.tensor
            def _(tn):
                cx.wait(tn, "a3m5")

                def emit_pv(k):
                    ig, hl, jt = items[k]
                    cx.wait(tn, "a3e", e0 + k + 1)
                    if k in firsts and ig >= 2:
                        cx.wait(tn, "a3ev", ev0 + evseq[(ig - 2, hl)])
                    ins = tn.matmul(po[hl][ig % 2][0:65, :],
                                    vp[p % 2][:, jt * 130 + hl * 65: jt * 130 + (hl + 1) * 65],
                                    et[k % NET][:, :], start=(k in firsts),
                                    stop=(pvlast[(ig, hl)] == k), skip_group_check=True)
                    cx.inc("a3p", ins)
                    for ig2 in bc_due.get(k, []):
                        cx.wait(tn, "a3rc", rc0 + ig2 + 1)
                        if ig2 >= 1:
                            cx.wait(tn, "a3ev", ev0 + 2 * ig2)
                        ins = tn.matmul(ps_x[0][:, :], cx.bsel[0:65, :],
                                        drow[ig2 % 2][0:65, :], start=True,
                                        stop=True, skip_group_check=True)
                        cx.inc("a3bcd", ins)

                for k, (ig, hl, jt) in enumerate(items):
                    if k >= 3:
                        cx.wait(tn, "a3e", e0 + k - 2)
                    ins = tn.matmul(ps_s[k % 3][:, :],
                                    kh[p % 2][hl][0:70, jt * 128:(jt + 1) * 128],
                                    qh[p % 2][hl][0:70, ig * TC:(ig + 1) * TC],
                                    start=True, stop=True, skip_group_check=True)
                    cx.inc("a3s", ins)
                    if k >= 1:
                        emit_pv(k - 1)
                emit_pv(n - 1)

            @blk.vector
            def _(v):
                for k, (ig, hl, jt) in enumerate(items):
                    ops = mask_ops(k)
                    if ops:
                        v.wait_ge(cx.sem("a3s")[0], s0 + k + 1)
                        for op in ops:
                            if op[0] == "neg":
                                ins = v.tensor_scalar_add(ps_s[k % 3][:, op[1]:op[2]],
                                                          ps_s[k % 3][:, op[1]:op[2]], NEG)
                            else:
                                a, b, off = op[1], op[2], op[3]
                                ins = v.tensor_add(ps_s[k % 3][:, a:b],
                                                   ps_s[k % 3][:, a:b],
                                                   msk[:, off:off + (b - a)])
                            cx.inc("a3mk", ins)
                    if pvlast[(ig, hl)] == k and all(
                            pvlast[(ig, h2)] <= k for h2 in range(2)):
                        dt = drow[ig % 2]
                        v.wait_ge(cx.sem("a3p")[0],
                                  p0 + max(pvlast[(ig, 0)], pvlast[(ig, 1)]) + 1)
                        if ig >= 2:
                            cx.wait(v, "a3bcd", bc0 + ig - 1)
                        v.tensor_scalar_max(dt[0:1, :],
                                            po[0][ig % 2][64:65, :], 1e-35)
                        v.tensor_scalar_max(dt[64:65, :],
                                            po[1][ig % 2][64:65, :], 1e-35)
                        ins = v.reciprocal(dt[0:65, :], dt[0:65, :])
                        cx.inc("a3rc", ins)
                    for ig2 in bc_due.get(k, []):
                        cx.wait(v, "a3bcd", bc0 + ig2 + 1)
                        v.tensor_copy(bc_sb[:, :], ps_x[0][:, :])
                        for hl2 in range(2):
                            ins = v.tensor_mul(
                                oT[hl2 * 64:(hl2 + 1) * 64, ig2 * TC:(ig2 + 1) * TC],
                                po[hl2][ig2 % 2][0:64, :], bc_sb[hl2 * 64:(hl2 + 1) * 64, :])
                            cx.inc("a3ev", ins)

            @blk.scalar
            def _(s):
                for k, (ig, hl, jt) in enumerate(items):
                    hh = ALIBI[2 * p + hl]
                    s.wait_ge(cx.sem("a3s")[0], s0 + k + 1)
                    if nmask[k + 1] > 0:
                        s.wait_ge(cx.sem("a3mk")[0], v0 + nmask[k + 1])
                    if k >= NET:
                        cx.wait(s, "a3p", p0 + k - NET + 1)
                    ins = s.activation(et[k % NET][:, :], ps_s[k % 3][:, :],
                                       AF.Exp, bias=mneg[:, hl:hl + 1])
                    cx.inc("a3e", ins)

        with nc.Block() as blk:
            @blk.sync
            def _(s):
                cx.wait(s, "a3ev")
                for m in range(NCH):
                    cx.inc("a3ost", s.dma_start(
                        osc.ap()[p * 128:(p + 1) * 128, m * TC:(m + 1) * TC],
                        oT[:, m * TC:(m + 1) * TC]), 16)

    with nc.Block() as blk:
        @blk.sync
        def _(s):
            cx.wait(s, "a3ost")

def emit_l0attn_out(cx, osc, wo, bo, xin, xout, ysc):
    nc = cx.nc
    with cx.scope():
        ysb = [cx.sb(f"a4sb{m}", [128, T]) for m in range(NCH)]
        with cx.scope():
            w_o = load_weight_bf(cx, "a4wo", wo, D, D)
            bo_t = load_pp(cx, "a4bo", bo)
            ob = [cx.sb(f"a4_o{c}", [128, TC], BF16) for c in range(NCH)]
            ps = [cx.psum(f"a4_p{i}", [128, TC]) for i in range(2)]
            for tc in range(NTC):
                t0 = tc * TC
                with nc.Block() as blk:
                    @blk.sync
                    def _(s):
                        for c in range(NCH):
                            cx.inc("a4ld", s.dma_start(
                                ob[c][:, :], osc.ap()[c * 128:(c + 1) * 128, t0:t0 + TC]), 16)

                    ev0 = cx.val("a4ev")

                    @blk.tensor
                    def _(tn):
                        cx.wait(tn, "a4ld")
                        for m in range(NCH):
                            if m >= 2:
                                cx.wait(tn, "a4ev", ev0 + m - 1)
                            _proj_block(cx, tn, w_o, [t[:, :] for t in ob],
                                        ps[m % 2][:, :], m, sem="a4mm")

                    @blk.vector
                    def _(v, t0=t0):
                        sm = cx.sem("a4mm")
                        for m in range(NCH):
                            v.wait_ge(sm[0], sm[1] - (NCH - 1 - m))
                            ins = v.tensor_scalar_add(ysb[m][:, t0:t0 + TC],
                                                      ps[m % 2][:, :],
                                                      bo_t[:, m:m + 1])
                            cx.inc("a4ev", ins)
        ln_phase(cx, "ln", None, out=xout, res=xin, names="a5", src_sb_full=ysb)


def build_l0attn(debug=False, fuse=False):
    nc = bass.Bass("TRN2", target_bir_lowering=False)
    xin = nc.dram_tensor("xin", [D, T], F32, kind="ExternalInput")
    wq = nc.dram_tensor("wq", [D, D], BF16, kind="ExternalInput")
    wk = nc.dram_tensor("wk", [D, D], BF16, kind="ExternalInput")
    wv = nc.dram_tensor("wv", [D, D], BF16, kind="ExternalInput")
    wo = nc.dram_tensor("wo", [D, D], BF16, kind="ExternalInput")
    bq = nc.dram_tensor("bq", [D], F32, kind="ExternalInput")
    bo = nc.dram_tensor("bo", [D], F32, kind="ExternalInput")
    albq = nc.dram_tensor("albq", [48, T], F32, kind="ExternalInput")
    alck = nc.dram_tensor("alck", [48, T], F32, kind="ExternalInput")
    mskc = nc.dram_tensor("mskc", [128, 640], F32, kind="ExternalInput")
    xout = nc.dram_tensor("xout", [D, T], F32, kind="ExternalOutput")
    lnx = nc.dram_tensor("lnx_scr", [D, T], BF16, kind="Internal")
    qsc = nc.dram_tensor("q_scr", [D, T], BF16, kind="Internal")
    ksc = nc.dram_tensor("k_scr", [D, T], BF16, kind="Internal")
    vsc = nc.dram_tensor("v_scr", [32, 128, 1040], BF16, kind="Internal")
    osc = nc.dram_tensor("o_scr", [D, T], BF16,
                         kind="ExternalOutput" if debug else "Internal")
    ysc = nc.dram_tensor("y_scr", [D, T], F32, kind="Internal")
    with ExitStack() as es:
        cx = Ctx(nc, es, fuse=fuse)
        emit_consts(cx)
        qn = cx.sb("a_qn", [128, NCH])
        kn = cx.sb("a_kn", [128, NCH])
        with cx.scope():
            lnx_sb = [cx.sb(f"a1sb{c}", [128, T], BF16) for c in range(NCH)]
            a2w = a2_loads(cx, wq, wk, bq)
            ln_phase(cx, "ln", xin, out_sb=lnx_sb, names="a1")
            emit_l0attn_proj(cx, lnx_sb, a2w, wv, qsc, ksc, vsc, qn, kn)
        with cx.scope():
            emit_l0attn_flash(cx, qsc, ksc, vsc, osc, qn, kn, albq, alck, mskc)
        emit_l0attn_out(cx, osc, wo, bo, xin, xout, ysc)
    return nc



FQ_CH = 4
W1C = 22
TC2 = 256


def load_weight_bf2(cx, tag, src_ap, d_in, d_out, defer=False):
    nch = d_in // 128
    w = cx.sb(tag, [128, nch * d_out], BF16)
    with cx.nc.Block() as blk:
        @blk.gpsimd
        def _(g):
            for c in range(nch):
                cx.inc("wldg", g.dma_start(w[:, c * d_out:(c + 1) * d_out],
                                           src_ap[c * 128:(c + 1) * 128, :]), 16)
            if not defer:
                cx.wait(g, "wldg")
    if defer:
        return w, cx.val("wldg")
    return w


def emit_l0ffn_layer(cx, l, usc, lhsc, wqkv, w1, b1, w2, b2, lng, lnb, qs, ks,
                     rcsq, rssq, rcsk, rssk, mskc, mskw, half=False):
    nc = cx.nc
    with cx.scope():
        q_rot = cx.sb(f"f{l}qr", [128, FQ_CH * T], BF16)
        k_rot = cx.sb(f"f{l}kr", [128, FQ_CH * T], BF16)
        v_sb = cx.sb(f"f{l}vv", [128, 32 * 520], BF16)
        oT = cx.sb(f"f{l}oT", [128, FQ_CH * T], BF16)
        with cx.scope():
            _l0ffn_qkv(cx, l, usc, wqkv, qs, ks, rcsq, rssq, rcsk, rssk,
                       q_rot, k_rot, v_sb)
        wopre = load_weight_bf2(cx, f"f{l}cw", cx.fwo_ap[l], 512, D,
                                defer=True)
        with cx.scope():
            _l0ffn_attn(cx, l, q_rot, k_rot, v_sb, oT, qs, ks, mskc, mskw)
        with cx.scope():
            _l0ffn_wo(cx, l, oT, usc, wopre)
    if half:
        return
    with cx.scope():
        gam = load_pp(cx, f"f{l}lg", lng, row=l)
        bet = load_pp(cx, f"f{l}lb", lnb, row=l)
        w1pre = load_weight_bf2(cx, f"f{l}dw1", w1.ap()[l], D, 2 * HPAD,
                                defer=True)
        ln_phase(cx, "ln", usc, out_bf=lhsc, shift=True, gamma=gam, beta=bet,
                 names=f"f{l}n")
        with cx.scope():
            _l0ffn_ffn(cx, l, lhsc, w1pre, b1, w2, b2, usc)


def _l0ffn_qkv(cx, l, usc, wqkv, qs, ks, rcsq, rssq, rcsk, rssk, q_rot, k_rot, v_sb):
    nc = cx.nc
    u = "fa"
    nm = f"f{l}a"
    w_sb = load_weight_bf2(cx, nm + "w", wqkv.ap()[l], D, 1536)
    qs_t = cx.sb(nm + "qs", [128, 2])
    hmask = cx.sb(nm + "hm", [128, 128])
    sel2 = cx.sb(nm + "s2", [128, 128])
    ub = [cx.sb(f"{nm}_u{c}", [128, TC + 1], BF16) for c in range(NCH)]
    qraw = cx.sb(nm + "qraw", [128, TC])
    sq = cx.sb(nm + "sq", [128, TC])
    nnc = cx.sb(nm + "nnc", [128, TC])
    rn = cx.sb(nm + "rn", [128, TC])
    qn = cx.sb(nm + "qn", [128, TC])
    t1 = cx.sb(nm + "t1", [128, TC])
    t2 = cx.sb(nm + "t2", [128, TC])
    rot = [cx.sb(f"{nm}_rc{i}", [128, TC]) for i in range(2)]
    ps_m = [cx.psum(f"{nm}_pm{i}", [128, TC]) for i in range(2)]
    ps_n = [cx.psum(f"{nm}_pn{i}", [128, TC]) for i in range(2)]
    ps_r = [cx.psum(f"{nm}_pr{i}", [128, TC]) for i in range(2)]
    ps_v = [cx.psum(f"{nm}_pv{i}", [128, TC]) for i in range(2)]

    with nc.Block() as blk:
        @blk.vector
        def _(v):
            v.memset(hmask[:, :], 0.0)
            v.memset(hmask[0:64, 0:1], 1.0)
            v.memset(hmask[64:128, 64:65], 1.0)
            v.memset(sel2[:, :], 0.0)
            v.memset(sel2[0:1, 0:64], 1.0)
            v.memset(sel2[64:65, 64:128], 1.0)
            v.memset(v_sb[:, :].rearrange("p (h w) -> p h w", w=65)[:, :, 64:65], 1.0)
            for c in range(NCH):
                v.memset(ub[c][:, 0:1], 0.0)

        @blk.sync
        def _(s):
            for half in range(2):
                cx.inc(u + "qsld", s.dma_start(
                    qs_t[half * 64:(half + 1) * 64, 0:1],
                    qs.ap()[l].rearrange("(p o) -> p o", o=1)), 16)
                cx.inc(u + "qsld", s.dma_start(
                    qs_t[half * 64:(half + 1) * 64, 1:2],
                    ks.ap()[l].rearrange("(p o) -> p o", o=1)), 16)
            cx.wait(s, u + "qsld")

    for tc in range(NTC):
        t0 = tc * TC
        with nc.Block() as blk:
            @blk.gpsimd
            def _(g):
                for c in range(NCH):
                    if t0 == 0:
                        cx.inc(u + "ldg", g.dma_start(
                            ub[c][:, 1:TC + 1], usc.ap()[c * 128:(c + 1) * 128, 0:TC]), 16)
                    else:
                        cx.inc(u + "ldg", g.dma_start(
                            ub[c][:, 0:TC + 1],
                            usc.ap()[c * 128:(c + 1) * 128, t0 - 1:t0 + TC]), 16)

            def hv(c):
                return ub[c][:, 1:TC + 1] if c < 4 else ub[c][:, 0:TC]

            for which in ("q", "k"):
                mof = 0 if which == "q" else 4
                dst = q_rot if which == "q" else k_rot
                csrc = rcsq if which == "q" else rcsk
                ssrc = rssq if which == "q" else rssk
                qscol = 0 if which == "q" else 1
                w = u + which
                ev0 = cx.val(w + "ev"); sqd0 = cx.val(w + "sqd")
                nnm0 = cx.val(w + "nnm"); ncd0 = cx.val(w + "ncd")
                rnd0 = cx.val(w + "rnd"); rd0 = cx.val(w + "rd")
                ap0 = cx.val(w + "ap")

                @blk.sync
                def _(s, w=w, csrc=csrc, ssrc=ssrc):
                    other = (u + "k") if w.endswith("q") else (u + "q")
                    cx.wait(s, other + "ap")
                    cx.inc(w + "rld", s.dma_start(rot[0][:, :], csrc.ap()[:, t0:t0 + TC]), 16)
                    cx.inc(w + "rld", s.dma_start(rot[1][:, :], ssrc.ap()[:, t0:t0 + TC]), 16)

                @blk.tensor
                def _(tn, w=w, mof=mof, ev0=ev0, sqd0=sqd0, rnd0=rnd0,
                      ncd0=ncd0, ap0=ap0):
                    cx.wait(tn, u + "ldg")
                    order = [("qkv", 0), ("qkv", 1), ("nn", 0), ("r", 0),
                             ("qkv", 2), ("nn", 1), ("r", 1),
                             ("qkv", 3), ("nn", 2), ("r", 2), ("nn", 3), ("r", 3)]
                    for kind, mi in order:
                        if kind == "qkv":
                            if mi >= 2:
                                cx.wait(tn, w + "ev", ev0 + mi - 1)
                            for c in range(NCH):
                                ins = tn.matmul(
                                    ps_m[mi % 2][:, :],
                                    w_sb[:, c * 1536 + (mof + mi) * 128:
                                         c * 1536 + (mof + mi) * 128 + 128],
                                    hv(c), start=(c == 0), stop=(c == NCH - 1))
                            cx.inc(w + "mm", ins)
                        elif kind == "nn":
                            cx.wait(tn, w + "sqd", sqd0 + mi + 1)
                            if mi >= 2:
                                cx.wait(tn, w + "ncd", ncd0 + mi - 1)
                            ins = tn.matmul(ps_n[mi % 2][:, :], hmask[:, :], sq[:, :],
                                            start=True, stop=True, skip_group_check=True)
                            cx.inc(w + "nnm", ins)
                        else:
                            cx.wait(tn, w + "rnd", rnd0 + mi + 1)
                            if mi >= 2:
                                cx.wait(tn, w + "ap", ap0 + mi - 1)
                            ins = tn.matmul(ps_r[mi % 2][:, :], sel2[:, :], rn[:, :],
                                            start=True, stop=True, skip_group_check=True)
                            cx.inc(w + "rd", ins)

                @blk.scalar
                def _(s, w=w, ncd0=ncd0, rd0=rd0):
                    for mi in range(FQ_CH):
                        cx.wait(s, w + "ncd", ncd0 + mi + 1)
                        if mi >= 1:
                            cx.wait(s, w + "rd", rd0 + mi)
                        s.activation(rn[:, :], nnc[:, :], AF.Ln)
                        cx.inc(w + "rnd", s.activation(rn[:, :], rn[:, :], AF.Exp,
                                                       scale=-0.5))

                @blk.vector
                def _(v, w=w, qscol=qscol, dst=dst, nnm0=nnm0, rd0=rd0, mof=mof):
                    sm = cx.sem(w + "mm")
                    for mi in range(FQ_CH):
                        v.wait_ge(sm[0], sm[1] - (FQ_CH - 1 - mi))
                        cx.wait(v, w + "rld")
                        cx.inc(w + "ev", v.tensor_copy(qraw[:, :], ps_m[mi % 2][:, :]))
                        cx.inc(w + "sqd", v.tensor_mul(sq[:, :], qraw[:, :], qraw[:, :]))
                        cx.wait(v, w + "nnm", nnm0 + mi + 1)
                        cx.inc(w + "ncd", v.tensor_scalar_max(nnc[:, :], ps_n[mi % 2][:, :],
                                                              1e-24))
                        cx.wait(v, w + "rd", rd0 + mi + 1)
                        v.scalar_tensor_tensor(qn[:, :], qraw[:, :], qs_t[:, qscol:qscol + 1],
                                               ps_r[mi % 2][:, :], ALU.mult, ALU.mult)
                        for g in range(4):
                            ga, gb = g * 32, (g ^ 1) * 32
                            v.tensor_copy(t2[ga:ga + 32, :], qn[gb:gb + 32, :])
                        v.tensor_mul(t1[:, :], qn[:, :], rot[0][:, :])
                        v.tensor_mul(t2[:, :], t2[:, :], rot[1][:, :])
                        ins = v.tensor_add(dst[:, mi * T + t0: mi * T + t0 + TC],
                                           t1[:, :], t2[:, :])
                        cx.inc(w + "ap", ins)

            vv0 = cx.val(u + "vev")

            @blk.tensor
            def _(tn, vv0=vv0):
                cx.wait(tn, u + "kap")
                for tt in range(4):
                    if tt >= 2:
                        cx.wait(tn, u + "vev", vv0 + tt - 1)
                    for c in range(NCH):
                        off = 1 if c < 4 else 0
                        ins = tn.matmul(ps_v[tt % 2][:, :],
                                        ub[c][:, off + tt * 128: off + tt * 128 + 128],
                                        w_sb[:, c * 1536 + 1024: c * 1536 + 1536],
                                        start=(c == 0), stop=(c == NCH - 1))
                    cx.inc(u + "vmm", ins)

            @blk.vector
            def _(v):
                sm = cx.sem(u + "vmm")
                for tt in range(4):
                    v.wait_ge(sm[0], sm[1] - (4 - 1 - tt))
                    jt = tc * 4 + tt
                    ins = v.tensor_copy(
                        v_sb[:, jt * 520:(jt + 1) * 520]
                        .rearrange("p (h w) -> p h w", w=65)[:, :, 0:64],
                        ps_v[tt % 2][:, :].rearrange("p (h w) -> p h w", w=64))
                    cx.inc(u + "vev", ins)


def _l0ffn_attn(cx, l, q_rot, k_rot, v_sb, oT, qs, ks, mskc, mskw):
    nc = cx.nc
    u = "fb"
    nm = f"f{l}b"
    NET = 8
    et = [cx.sb(f"{nm}_e{i}", [128, TC], BF16) for i in range(NET)]
    mc = cx.sb(nm + "mc", [128, 128])
    mw = cx.sb(nm + "mw", [128, 128])
    qsr = cx.sb(nm + "qsr", [1, 64])
    ksr = cx.sb(nm + "ksr", [1, 64])
    mrow = cx.sb(nm + "mrow", [1, 2])
    mb = cx.sb(nm + "mb", [128, 1])
    drow = [cx.sb(f"{nm}_dr{i}", [65, TC]) for i in range(2)]
    one_tcf = cx.sb(nm + "_one", [1, TC])
    bc = cx.sb(nm + "_bc", [128, TC])
    ps_s = [cx.psum(f"{nm}_ps{i}", [128, TC]) for i in range(3)]
    po = [[cx.psum(f"{nm}_po{hl}{p}", [128, TC]) for p in range(2)] for hl in range(2)]
    ps_x = [cx.psum(nm + "_px0", [128, TC])]

    with nc.Block() as blk:
        @blk.sync
        def _(s):
            cx.inc(u + "cl", s.dma_start(mc[:, :], mskc.ap()[:, 0:128]), 16)
            cx.inc(u + "cl", s.dma_start(mw[:, :], mskw.ap()), 16)
            cx.inc(u + "cl", s.dma_start(qsr[:, :], qs.ap()[l].rearrange("(o n) -> o n", o=1)), 16)
            cx.inc(u + "cl", s.dma_start(ksr[:, :], ks.ap()[l].rearrange("(o n) -> o n", o=1)), 16)

        @blk.vector
        def _(v):
            v.memset(one_tcf[:, :], 1.0)
            v.memset(drow[0][:, :], 1.0)
            v.memset(drow[1][:, :], 1.0)
            cx.wait(v, u + "cl")
            v.tensor_reduce(mrow[:, 0:1], qsr[:, :], AX.X, ALU.max,
                            apply_absolute_value=True)
            v.tensor_reduce(mrow[:, 1:2], ksr[:, :], AX.X, ALU.max,
                            apply_absolute_value=True)
            v.tensor_mul(mrow[:, 0:1], mrow[:, 0:1], mrow[:, 1:2])
            ins = v.tensor_scalar_mul(mrow[:, 0:1], mrow[:, 0:1], -32.0 * 1.02)
            cx.inc(u + "m1", ins)

        @blk.tensor
        def _(tn):
            cx.wait(tn, u + "m1")
            ins = tn.matmul(ps_x[0][:, 0:1], cx.ones_row[0:1, :], mrow[0:1, 0:1],
                            start=True, stop=True)
            cx.inc(u + "m2", ins)

        @blk.vector
        def _(v):
            cx.wait(v, u + "m2")
            ins = v.tensor_copy(mb[:, 0:1], ps_x[0][:, 0:1])
            cx.inc(u + "m3", ins)

    NIG = T // TC
    items = []
    evseq = {}
    pvlast = {}
    nev = 0
    for ig in range(NIG):
        jts = list(range(max(0, 4 * ig - 4), 4 * ig + 4))
        for pp in range(4):
            for hl in range(2):
                for idx, jt in enumerate(jts):
                    items.append([ig, pp, hl, jt, idx == 0, idx == len(jts) - 1])
                evseq[(ig, pp, hl)] = nev + 1
                nev += 1
    items2 = []
    for ig in range(NIG):
        jts = list(range(max(0, 4 * ig - 4), 4 * ig + 4))
        for pp in range(4):
            for idx, jt in enumerate(jts):
                for hl in range(2):
                    items2.append((ig, pp, hl, jt, idx == 0, idx == len(jts) - 1))
    items = items2
    for k, (ig, pp, hl, jt, first, last) in enumerate(items):
        if last:
            pvlast[(ig, pp, hl)] = k
    n = len(items)

    def mask_ops(k):
        ig, pp, hl, jt, first, last = items[k]
        d0 = jt * 128 - ig * TC
        ops = []
        if d0 > 0:
            ops.append(("neg", 0, min(d0, TC)))
        if -127 <= d0 < TC:
            a = max(0, d0)
            b = min(TC, d0 + 128)
            ops.append(("mc", a, b, a - d0))
        d1 = d0 + 512
        if 0 <= d1 < TC:
            b = min(TC, d1 + 128)
            ops.append(("mw", d1, b, 0))
            if d1 + 128 < TC:
                ops.append(("neg", d1 + 128, TC))
        return ops

    nmask = [0]
    for k in range(n):
        nmask.append(nmask[-1] + len(mask_ops(k)))

    groups = [(ig, pp) for ig in range(NIG) for pp in range(4)]
    gfirst = {}
    for k, (ig, pp, hl, jt, first, last) in enumerate(items):
        if (ig, pp) not in gfirst:
            gfirst[(ig, pp)] = k
    BDELAY = 6
    bc_due = {}
    for gi, (ig, pp) in enumerate(groups):
        kl = pvlast[(ig, pp, 1)]
        cap = n - 1
        if gi + 2 < len(groups):
            cap = min(cap, gfirst[groups[gi + 2]] - 1)
        dk = max(min(kl + BDELAY, cap), kl)
        bc_due.setdefault(dk, []).append((ig, pp))

    s0 = cx.val(u + "s"); e0 = cx.val(u + "e"); v0 = cx.val(u + "mk")
    p0 = cx.val(u + "p"); rc0 = cx.val(u + "rc"); bc0 = cx.val(u + "bcd")
    ev0 = cx.val(u + "ev")

    with nc.Block() as blk:
        @blk.tensor
        def _(tn):
            cx.wait(tn, u + "m3")

            def emit_pv(k):
                ig, pp, hl, jt, first, last = items[k]
                cx.wait(tn, u + "e", e0 + k + 1)
                if first:
                    if pp >= 2:
                        cx.wait(tn, u + "ev", ev0 + evseq[(ig, pp - 2, hl)])
                    elif ig >= 1:
                        cx.wait(tn, u + "ev", ev0 + evseq[(ig - 1, pp + 2, hl)])
                ins = tn.matmul(po[hl][pp % 2][0:65, :],
                                v_sb[:, jt * 520 + (2 * pp + hl) * 65:
                                     jt * 520 + (2 * pp + hl + 1) * 65],
                                et[k % NET][:, :], start=first, stop=last,
                                skip_group_check=True)
                cx.inc(u + "p", ins)
                for (ig2, pp2) in bc_due.get(k, []):
                    g = 4 * ig2 + pp2 + 1
                    cx.wait(tn, u + "rc", rc0 + g)
                    if g >= 2:
                        cx.wait(tn, u + "ev", ev0 + 2 * (g - 1))
                    ins = tn.matmul(ps_x[0][:, :], cx.bsel[0:65, :],
                                    drow[g % 2][0:65, :], start=True, stop=True,
                                    skip_group_check=True)
                    cx.inc(u + "bcd", ins)

            for k, (ig, pp, hl, jt, first, last) in enumerate(items):
                if k >= 3:
                    cx.wait(tn, u + "e", e0 + k - 2)
                r0, r1 = (0, 64) if hl == 0 else (64, 128)
                ins = tn.matmul(ps_s[k % 3][:, :],
                                k_rot[r0:r1, pp * T + jt * 128: pp * T + (jt + 1) * 128],
                                q_rot[r0:r1, pp * T + ig * TC: pp * T + (ig + 1) * TC],
                                start=True, stop=True, skip_group_check=True)
                cx.inc(u + "s", ins)
                if k >= 1:
                    emit_pv(k - 1)
            emit_pv(n - 1)

        @blk.vector
        def _(v):
            for k, (ig, pp, hl, jt, first, last) in enumerate(items):
                ops = mask_ops(k)
                if ops:
                    v.wait_ge(cx.sem(u + "s")[0], s0 + k + 1)
                    for op in ops:
                        if op[0] == "neg":
                            ins = v.tensor_scalar_add(ps_s[k % 3][:, op[1]:op[2]],
                                                      ps_s[k % 3][:, op[1]:op[2]], NEG)
                        else:
                            m_t = mc if op[0] == "mc" else mw
                            a, b, off = op[1], op[2], op[3]
                            ins = v.tensor_add(ps_s[k % 3][:, a:b],
                                               ps_s[k % 3][:, a:b],
                                               m_t[:, off:off + (b - a)])
                        cx.inc(u + "mk", ins)
                if last and hl == 1:
                    g = 4 * ig + pp + 1
                    dt = drow[g % 2]
                    cx.wait(v, u + "p", p0 + pvlast[(ig, pp, 1)] + 1)
                    if g >= 3:
                        cx.wait(v, u + "bcd", bc0 + g - 2)
                    v.tensor_scalar_max(dt[0:1, :], po[0][pp % 2][64:65, :], 1e-35)
                    v.tensor_scalar_max(dt[64:65, :], po[1][pp % 2][64:65, :], 1e-35)
                    ins = v.reciprocal(dt[0:65, :], dt[0:65, :])
                    cx.inc(u + "rc", ins)
                for (ig2, pp2) in bc_due.get(k, []):
                    g = 4 * ig2 + pp2 + 1
                    cx.wait(v, u + "bcd", bc0 + g)
                    v.tensor_copy(bc[:, :], ps_x[0][:, :])
                    for hl2 in range(2):
                        ins = v.tensor_mul(
                            oT[hl2 * 64:(hl2 + 1) * 64, pp2 * T + ig2 * TC: pp2 * T + (ig2 + 1) * TC],
                            po[hl2][pp2 % 2][0:64, :], bc[hl2 * 64:(hl2 + 1) * 64, :])
                        cx.inc(u + "ev", ins)

        @blk.scalar
        def _(s):
            for k, (ig, pp, hl, jt, first, last) in enumerate(items):
                s.wait_ge(cx.sem(u + "s")[0], s0 + k + 1)
                if nmask[k + 1] > 0:
                    s.wait_ge(cx.sem(u + "mk")[0], v0 + nmask[k + 1])
                if k >= NET:
                    cx.wait(s, u + "p", p0 + k - NET + 1)
                ins = s.activation(et[k % NET][:, :], ps_s[k % 3][:, :],
                                   AF.Exp, scale=8.0, bias=mb[:, 0:1])
                cx.inc(u + "e", ins)

def _l0ffn_wo(cx, l, oT, usc, wopre):
    nc = cx.nc
    u = "fc"
    nm = f"f{l}c"
    w_o, wom = wopre
    dma_drain(cx, "wldg", wom)
    uin = [[cx.sb(f"{nm}_u{p}{c}", [128, TC]) for c in range(NCH)] for p in range(2)]
    yv = [cx.sb(f"{nm}_y{m}", [128, TC]) for m in range(NCH)]
    ps = [cx.psum(f"{nm}_p{i}", [128, TC]) for i in range(2)]
    fc_marks = {}

    def fc_loads(s, tc):
        p = tc % 2
        t0 = tc * TC
        for c in range(NCH):
            cx.inc(u + "ld", s.dma_start(
                uin[p][c][:, :], usc.ap()[c * 128:(c + 1) * 128, t0:t0 + TC]), 16)
        fc_marks[tc] = cx.val(u + "ld")

    for tc in range(NTC):
        p = tc % 2
        t0 = tc * TC
        with nc.Block() as blk:
            @blk.sync
            def _(s):
                if tc == 0:
                    fc_loads(s, 0)
                    fc_loads(s, 1)
                elif tc + 1 < NTC:
                    fc_loads(s, tc + 1)
                cx.wait(s, u + "st")

            ev0 = cx.val(u + "ev")

            @blk.tensor
            def _(tn, ev0=ev0):
                tn.wait_ge(cx.sem(u + "ld")[0], fc_marks[tc])
                for m in range(NCH):
                    if m >= 2:
                        cx.wait(tn, u + "ev", ev0 + m - 1)
                    for c in range(4):
                        ins = tn.matmul(ps[m % 2][:, :],
                                        w_o[:, c * D + m * 128: c * D + m * 128 + 128],
                                        oT[:, c * T + t0: c * T + t0 + TC],
                                        start=(c == 0), stop=(c == 3))
                    cx.inc(u + "mm", ins)

            @blk.vector
            def _(v):
                sm = cx.sem(u + "mm")
                for m in range(NCH):
                    v.wait_ge(sm[0], sm[1] - (NCH - 1 - m))
                    ins = v.tensor_add(yv[m][:, :], ps[m % 2][:, :], uin[p][m][:, :])
                    cx.inc(u + "ev", ins)

            @blk.sync
            def _(s):
                sm = cx.sem(u + "ev")
                for m in range(NCH):
                    s.wait_ge(sm[0], sm[1] - (NCH - 1 - m))
                    cx.inc(u + "st", s.dma_start(
                        usc.ap()[m * 128:(m + 1) * 128, t0:t0 + TC], yv[m][:, :]), 16)
    with nc.Block() as blk:
        @blk.sync
        def _(s):
            cx.wait(s, u + "st")


def _l0ffn_ffn(cx, l, lhsc, w1pre, b1, w2, b2, usc):
    nc = cx.nc
    u = "fd"
    nm = f"f{l}d"
    w_1, w1m = w1pre
    dma_drain(cx, "wldg", w1m)
    w_2 = load_weight_bf2(cx, nm + "w2", w2.ap()[l], HPAD, D)
    b1t = load_pp(cx, nm + "b1", b1, n=2 * HPAD, row=l)
    b2t = load_pp(cx, nm + "b2", b2, row=l)
    hx = [[cx.sb(f"{nm}_x{p}{c}", [128, TC], BF16) for c in range(NCH)] for p in range(2)]
    uin = [cx.sb(f"{nm}_u{c}", [128, TC]) for c in range(NCH)]
    fd_marks = {}
    fd_umarks = {}

    def fd_loads(s, tc):
        p = tc % 2
        t0 = tc * TC
        for c in range(NCH):
            cx.inc(u + "ld", s.dma_start(
                hx[p][c][:, :], lhsc.ap()[c * 128:(c + 1) * 128, t0:t0 + TC]), 16)
        fd_marks[tc] = cx.val(u + "ld")
    hsb = [cx.sb(f"{nm}_h{k}", [128, TC], BF16) for k in range(W1C)]
    gact = cx.sb(nm + "g", [128, TC])
    yv = [cx.sb(f"{nm}_y{m}", [128, TC]) for m in range(NCH)]
    ps_a = [cx.psum(f"{nm}_pa{i}", [128, TC]) for i in range(2)]
    ps_g = [cx.psum(f"{nm}_pg{i}", [128, TC]) for i in range(2)]
    ps_w2 = [cx.psum(f"{nm}_pw{i}", [128, TC]) for i in range(2)]

    for tc in range(NTC):
        p = tc % 2
        t0 = tc * TC
        with nc.Block() as blk:
            @blk.sync
            def _(s):
                if tc == 0:
                    fd_loads(s, 0)
                    fd_loads(s, 1)
                elif tc + 1 < NTC:
                    fd_loads(s, tc + 1)
                cx.wait(s, u + "st")
                for c in range(NCH):
                    cx.inc(u + "ld", s.dma_start(
                        uin[c][:, :], usc.ap()[c * 128:(c + 1) * 128, t0:t0 + TC]), 16)
                fd_umarks[tc] = cx.val(u + "ld")

            hd0 = cx.val(u + "hd")
            fe0 = cx.val(u + "fe")

            @blk.tensor
            def _(tn):
                tn.wait_ge(cx.sem(u + "ld")[0], fd_marks[tc])
                for kk in range(W1C):
                    if kk >= 2:
                        cx.wait(tn, u + "hd", hd0 + kk - 1)
                    for c in range(NCH):
                        tn.matmul(ps_a[kk % 2][:, :],
                                  w_1[:, c * 2 * HPAD + kk * 128: c * 2 * HPAD + kk * 128 + 128],
                                  hx[p][c][:, :], start=(c == 0), stop=(c == NCH - 1))
                    for c in range(NCH):
                        ins = tn.matmul(ps_g[kk % 2][:, :],
                                        w_1[:, c * 2 * HPAD + HPAD + kk * 128:
                                            c * 2 * HPAD + HPAD + kk * 128 + 128],
                                        hx[p][c][:, :], start=(c == 0), stop=(c == NCH - 1))
                    cx.inc(u + "mm", ins)
                cx.wait(tn, u + "hd", hd0 + W1C)
                for m in range(NCH):
                    if m >= 2:
                        cx.wait(tn, u + "fe", fe0 + m - 1)
                    for kk in range(W1C):
                        ins = tn.matmul(ps_w2[m % 2][:, :],
                                        w_2[:, kk * D + m * 128: kk * D + m * 128 + 128],
                                        hsb[kk][:, :], start=(kk == 0),
                                        stop=(kk == W1C - 1))
                    cx.inc(u + "w2m", ins)

            @blk.scalar
            def _(s):
                sm = cx.sem(u + "mm")
                for kk in range(W1C):
                    s.wait_ge(sm[0], sm[1] - (W1C - 1 - kk))
                    if kk >= 1:
                        cx.wait(s, u + "hd", hd0 + kk)
                    ins = s.activation(gact[:, :], ps_g[kk % 2][:, :],
                                       AF.Gelu, bias=b1t[:, W1C + kk:W1C + kk + 1])
                    cx.inc(u + "ga", ins)

            @blk.vector
            def _(v):
                sm = cx.sem(u + "ga")
                for kk in range(W1C):
                    v.wait_ge(sm[0], sm[1] - (W1C - 1 - kk))
                    ins = v.scalar_tensor_tensor(hsb[kk][:, :], ps_a[kk % 2][:, :],
                                                 b1t[:, kk:kk + 1], gact[:, :],
                                                 ALU.add, ALU.mult)
                    cx.inc(u + "hd", ins)

            @blk.vector
            def _(v):
                sm = cx.sem(u + "w2m")
                v.wait_ge(cx.sem(u + "ld")[0], fd_umarks[tc])
                for m in range(NCH):
                    v.wait_ge(sm[0], sm[1] - (NCH - 1 - m))
                    v.tensor_scalar_add(yv[m][:, :], ps_w2[m % 2][:, :], b2t[:, m:m + 1])
                    ins = v.tensor_add(yv[m][:, :], yv[m][:, :], uin[m][:, :])
                    cx.inc(u + "fe", ins)

            @blk.sync
            def _(s):
                sm = cx.sem(u + "fe")
                for m in range(NCH):
                    s.wait_ge(sm[0], sm[1] - (NCH - 1 - m))
                    cx.inc(u + "st", s.dma_start(
                        usc.ap()[m * 128:(m + 1) * 128, t0:t0 + TC], yv[m][:, :]), 16)
    with nc.Block() as blk:
        @blk.sync
        def _(s):
            cx.wait(s, u + "st")

def build_l0ffn(upto=None, fuse=False):
    nc = bass.Bass("TRN2", target_bir_lowering=False)
    xin = nc.dram_tensor("xin", [D, T], F32, kind="ExternalInput")
    wqkv = nc.dram_tensor("wqkv", [2, D, 1536], BF16, kind="ExternalInput")
    w1 = nc.dram_tensor("w1", [2, D, 2 * HPAD], BF16, kind="ExternalInput")
    b1 = nc.dram_tensor("b1", [2, 2 * HPAD], F32, kind="ExternalInput")
    w2 = nc.dram_tensor("w2", [2, HPAD, D], BF16, kind="ExternalInput")
    b2 = nc.dram_tensor("b2", [2, D], F32, kind="ExternalInput")
    wo = nc.dram_tensor("wo", [2, 512, D], BF16, kind="ExternalInput")
    lng = nc.dram_tensor("lng", [2, D], F32, kind="ExternalInput")
    lnb = nc.dram_tensor("lnb", [2, D], F32, kind="ExternalInput")
    qs = nc.dram_tensor("qs", [2, 64], F32, kind="ExternalInput")
    ks = nc.dram_tensor("ks", [2, 64], F32, kind="ExternalInput")
    rcsq = nc.dram_tensor("rcsq", [128, T], F32, kind="ExternalInput")
    rssq = nc.dram_tensor("rssq", [128, T], F32, kind="ExternalInput")
    rcsk = nc.dram_tensor("rcsk", [128, T], F32, kind="ExternalInput")
    rssk = nc.dram_tensor("rssk", [128, T], F32, kind="ExternalInput")
    mskc = nc.dram_tensor("mskc", [128, 640], F32, kind="ExternalInput")
    mskw = nc.dram_tensor("mskw", [128, 128], F32, kind="ExternalInput")
    xout = nc.dram_tensor("xout", [D, T], F32, kind="ExternalOutput")
    usc = nc.dram_tensor("u_scr", [D, T], F32, kind="Internal")
    lhsc = nc.dram_tensor("lh_scr", [D, T], BF16, kind="Internal")
    with ExitStack() as es:
        cx = Ctx(nc, es, fuse=fuse)
        cx.fwo_ap = [wo.ap()[i] for i in range(2)]
        emit_consts(cx)
        ln_phase(cx, "ln", xin, out=usc, names="q1")
        stages = {"q1": 0, "l0attn": 1, "l0ffn": 2, "l1attn": 3, None: 99}
        lim = stages[upto]
        for l in range(2):
            if lim <= 2 * l:
                break
            emit_l0ffn_layer(cx, l, usc, lhsc, wqkv, w1, b1, w2, b2, lng, lnb,
                             qs, ks, rcsq, rssq, rcsk, rssk, mskc, mskw,
                             half=(lim == 2 * l + 1))
        if upto is None:
            ln_phase(cx, "ln", usc, out=xout, res=xin, names="q7")
        else:
            with nc.Block() as blk:
                @blk.sync
                def _(sy):
                    for c in range(NCH):
                        cx.inc("bst", sy.dma_start(xout.ap()[c * 128:(c + 1) * 128, :],
                                                   usc.ap()[c * 128:(c + 1) * 128, :]), 16)
                    cx.wait(sy, "bst")
    return nc



def build_full(upto=None):
    nc = bass.Bass("TRN2", target_bir_lowering=False)
    xin = nc.dram_tensor("xin", [D, T], F32, kind="ExternalInput")
    xout = nc.dram_tensor("xout", [D, T], F32, kind="ExternalOutput")
    awq = nc.dram_tensor("awq", [D, D], BF16, kind="ExternalInput")
    awk = nc.dram_tensor("awk", [D, D], BF16, kind="ExternalInput")
    awv = nc.dram_tensor("awv", [D, D], BF16, kind="ExternalInput")
    awo = nc.dram_tensor("awo", [D, D], BF16, kind="ExternalInput")
    abq = nc.dram_tensor("abq", [D], F32, kind="ExternalInput")
    abo = nc.dram_tensor("abo", [D], F32, kind="ExternalInput")
    albq = nc.dram_tensor("albq", [48, T], F32, kind="ExternalInput")
    alck = nc.dram_tensor("alck", [48, T], F32, kind="ExternalInput")
    mskc = nc.dram_tensor("mskc", [128, 640], F32, kind="ExternalInput")
    mskw = nc.dram_tensor("mskw", [128, 128], F32, kind="ExternalInput")
    fwqkv = nc.dram_tensor("fwqkv", [2, D, 1536], BF16, kind="ExternalInput")
    fw1 = nc.dram_tensor("fw1", [2, D, 2 * HPAD], BF16, kind="ExternalInput")
    fb1 = nc.dram_tensor("fb1", [2, 2 * HPAD], F32, kind="ExternalInput")
    fw2 = nc.dram_tensor("fw2", [2, HPAD, D], BF16, kind="ExternalInput")
    fb2 = nc.dram_tensor("fb2", [2, D], F32, kind="ExternalInput")
    fwo = nc.dram_tensor("fwo", [2, 512, D], BF16, kind="ExternalInput")
    flng = nc.dram_tensor("flng", [2, D], F32, kind="ExternalInput")
    flnb = nc.dram_tensor("flnb", [2, D], F32, kind="ExternalInput")
    fqs = nc.dram_tensor("fqs", [2, 64], F32, kind="ExternalInput")
    fks = nc.dram_tensor("fks", [2, 64], F32, kind="ExternalInput")
    rcsq = nc.dram_tensor("rcsq", [128, T], F32, kind="ExternalInput")
    rssq = nc.dram_tensor("rssq", [128, T], F32, kind="ExternalInput")
    rcsk = nc.dram_tensor("rcsk", [128, T], F32, kind="ExternalInput")
    rssk = nc.dram_tensor("rssk", [128, T], F32, kind="ExternalInput")
    cw1t = nc.dram_tensor("cw1t", [D, D], BF16, kind="ExternalInput")
    cb1 = nc.dram_tensor("cb1", [D], F32, kind="ExternalInput")
    cw2c = nc.dram_tensor("cw2c", [D, 1], F32, kind="ExternalInput")
    cb2 = nc.dram_tensor("cb2", [1, 1], F32, kind="ExternalInput")
    rwr = nc.dram_tensor("rwr", [D, D], BF16, kind="ExternalInput")
    rwk = nc.dram_tensor("rwk", [D, D], BF16, kind="ExternalInput")
    rwv = nc.dram_tensor("rwv", [D, D], BF16, kind="ExternalInput")
    rmur = nc.dram_tensor("rmur", [D], F32, kind="ExternalInput")
    rmuk = nc.dram_tensor("rmuk", [D], F32, kind="ExternalInput")
    xres = nc.dram_tensor("xres", [D, T], F32,
                          kind="ExternalOutput" if upto else "Internal")
    lnxb = nc.dram_tensor("lnxb_scr", [D, T], BF16, kind="Internal")
    qsc = nc.dram_tensor("q_scr", [D, T], BF16, kind="Internal")
    ksc = nc.dram_tensor("k_scr", [D, T], BF16, kind="Internal")
    vsc = nc.dram_tensor("v_scr", [32, 128, 1040], BF16, kind="Internal")
    osc = nc.dram_tensor("o_scr", [D, T], BF16, kind="Internal")
    ysc = nc.dram_tensor("y_scr", [D, T], F32, kind="Internal")
    usc = nc.dram_tensor("u_scr", [D, T], F32, kind="Internal")
    lhsc = nc.dram_tensor("lh_scr", [D, T], BF16, kind="Internal")
    csc = nc.dram_tensor("c_scr", [D, T], F32, kind="Internal")

    with ExitStack() as es:
        cx = Ctx(nc, es, fuse=True)
        cx.fwo_ap = [fwo.ap()[i] for i in range(2)]
        emit_consts(cx)
        qn = cx.sb("a_qn", [128, NCH])
        kn = cx.sb("a_kn", [128, NCH])
        with cx.scope():
            lnx_sb = [cx.sb(f"a1sb{c}", [128, T], BF16) for c in range(NCH)]
            a2w = a2_loads(cx, awq, awk, abq)
            ln_phase(cx, "ln", xin, out_sb=lnx_sb, names="a1")
            emit_l0attn_proj(cx, lnx_sb, a2w, awv, qsc, ksc, vsc, qn, kn)
        with cx.scope():
            emit_l0attn_flash(cx, qsc, ksc, vsc, osc, qn, kn, albq, alck, mskc)
        emit_l0attn_out(cx, osc, awo, abo, xin, xres, ysc)
        if upto == "b1":
            return nc
        ln_phase(cx, "ln", xres, out=usc, names="q1")
        for l in range(2):
            emit_l0ffn_layer(cx, l, usc, lhsc, fwqkv, fw1, fb1, fw2, fb2,
                             flng, flnb, fqs, fks, rcsq, rssq, rcsk, rssk,
                             mskc, mskw)
        ln_phase(cx, "ln", usc, out=xres, res=xres, names="q7")
        if upto == "b2":
            return nc
        with cx.scope():
            lnx_sb = [cx.sb(f"r1sb{c}", [128, T], BF16) for c in range(NCH)]
            w_1 = load_weight_bf(cx, "w_1", cw1t, D, D, defer=True)[0]
            w_2, wm = load_weight_bf(cx, "w_2", cw2c, D, 1, defer=True)
            b1t = load_pp(cx, "b1t", cb1)
            ln_phase(cx, "ln", xres, out_sb=lnx_sb, names="r1")
            _l1attn_core(cx, (w_1, w_2, wm, b1t), cb2, lnx_sb, csc)
        ln_phase(cx, "ln", csc, out=xres, res=xres, names="r4")
        if upto == "b3":
            return nc
        with cx.scope():
            wpre = _l1ffn_loads(cx, rwr, rwk, rwv, rmur, rmuk)
            ln_phase(cx, "ln", xres, out_bf=lhsc, names="s1")
            with cx.scope():
                _l1ffn_core(cx, wpre, lhsc, ysc)
        ln_phase(cx, "ln", ysc, out=xout, res=xres, names="s3")
    return nc



_CACHE = {}


def _programs():
    if not _CACHE:
        _CACHE["full"] = build_full()
    return _CACHE


def kernel(**inputs):
    h = _host_shared(inputs)
    x = np.asarray(inputs["L0_attn_x"], dtype=np.float32)
    B = x.shape[0]
    cores = list(range(B))
    progs = _programs()
    im = []
    for b in range(B):
        im.append({
            "xin": np.ascontiguousarray(x[b].T),
            "awq": h["aWq"], "awk": h["aWk"], "awv": h["aWv"], "awo": h["aWo"],
            "abq": h["abq"], "abo": h["abo"],
            "albq": h["alibi_bq"], "alck": h["alibi_ck"],
            "mskc": h["mask_causal"], "mskw": h["mask_window"],
            "fwqkv": h["fwqkv"], "fw1": h["fw1"], "fb1": h["fb1"],
            "fw2": h["fw2"], "fb2": h["fb2"], "fwo": h["fwo"],
            "flng": h["flng"], "flnb": h["flnb"], "fqs": h["fqs"], "fks": h["fks"],
            "rcsq": h["rot_csq"], "rssq": h["rot_ssq"],
            "rcsk": h["rot_csk"], "rssk": h["rot_ssk"],
            "cw1t": h["cW1T"], "cb1": h["cb1"], "cw2c": h["cW2c"], "cb2": h["cb2"],
            "rwr": h["rWrT"], "rwk": h["rWkT"], "rwv": h["rWvT"],
            "rmur": h["rmur"], "rmuk": h["rmuk"],
        })
    r = run_bass_kernel_spmd(progs["full"], im, core_ids=cores)
    out = np.stack([np.asarray(r.results[b]["xout"]).T for b in range(B)])
    return np.ascontiguousarray(out, dtype=np.float32)
